# version 04
# baseline (speedup 1.0000x reference)
import contextlib
import numpy as np
import concourse.bass as bass
import concourse.mybir as mybir
from concourse.bass_utils import run_bass_kernel_spmd

F32 = mybir.dt.float32
I32 = mybir.dt.int32
U8 = mybir.dt.uint8
AF = mybir.ActivationFunctionType
OP = mybir.AluOpType
AX = mybir.AxisListType

S = 2048
D = 1024
NT = S // 128
NCH = D // 128
H = 16
DH = 64
LN2_HI = float(np.float32(0.6931471805599453))
LN2_LO = float(np.float64(0.6931471805599453094172321214581765680755)
               - np.float64(np.float32(0.6931471805599453)))


def ts(i, n=128):
    return slice(i * n, (i + 1) * n)


class P:
    ENGS = ("tensor", "vector", "scalar", "gpsimd", "sync")

    NDS = 8

    def __init__(self, nc, es):
        self.nc = nc
        self.steps = []
        self.cnt = {e: 0 for e in self.ENGS}
        self.dma_cnt = {"sync": 0, "gpsimd": 0, "scalar": 0}
        self.sem = {e: es.enter_context(nc.semaphore("sp_" + e)) for e in self.ENGS}
        self.dsem = {q: [es.enter_context(nc.semaphore("sd_%s%d" % (q, i)))
                         for i in range(self.NDS)] for q in self.dma_cnt}
        self._last = {}
        self._dirty = {e: False for e in self.ENGS}

    def on(self, eng, fn):
        self._dirty[eng] = True

        def run(engine, eng=eng, fn=fn):
            r = fn(engine)
            if r is not None:
                self._last[eng] = r
        self.steps.append((eng, run))

    def ck(self, eng):
        assert self._dirty[eng], "checkpoint on engine with no new ops: " + eng
        self._dirty[eng] = False
        self.cnt[eng] += 1
        sem = self.sem[eng]

        def run(engine, eng=eng, sem=sem):
            self._last[eng].then_inc(sem, 1)
        self.steps.append((eng, run))
        return (eng, self.cnt[eng])

    def ckw(self, eng):
        t = self.ck(eng) if self._dirty[eng] else (eng, self.cnt[eng])
        self.wait(eng, t)
        return t

    def wait(self, eng, token):
        if token is None:
            return
        if isinstance(token, list):
            best = {}
            for t in token:
                if t is None:
                    continue
                src, val = t
                key = src if not src.startswith("d") else (src, (val - 1) % self.NDS)
                if key not in best or best[key][1] < val:
                    best[key] = t
            for t in best.values():
                self.wait(eng, t)
            return
        src, val = token
        if src.startswith("d") and src[1:] in self.dsem:
            idx = val - 1
            sem = self.dsem[src[1:]][idx % self.NDS]
            v = 16 * (idx // self.NDS + 1)
        else:
            sem, v = self.sem[src], val

        def run(engine, sem=sem, v=v):
            engine.wait_ge(sem, v)
        self.steps.append((eng, run))

    def dma(self, q, out, in_, **kw):
        idx = self.dma_cnt[q]
        self.dma_cnt[q] += 1
        sem = self.dsem[q][idx % self.NDS]
        prev_round = idx // self.NDS

        def run(engine, out=out, in_=in_, kw=kw, sem=sem, prev_round=prev_round):
            if prev_round > 0:
                engine.wait_ge(sem, 16 * prev_round)
            engine.dma_start(out=out, in_=in_, **kw).then_inc(sem, 16)
        self.steps.append((q, run))
        return ("d" + q, self.dma_cnt[q])

    def barrier(self):
        for e in self.ENGS:
            if self._dirty[e]:
                self.ck(e)
        toks = [(e, self.cnt[e]) for e in self.ENGS if self.cnt[e] > 0]
        dtoks = []
        for q, c in self.dma_cnt.items():
            for i in range(max(0, c - self.NDS), c):
                dtoks.append(("d" + q, i + 1))
        for e in self.ENGS:
            for t in toks:
                if t[0] != e:
                    self.wait(e, t)
            for t in dtoks:
                self.wait(e, t)

    def emit(self, engine_name, engine):
        for eng, fn in self.steps:
            if eng == engine_name:
                fn(engine)


def build_core_program(nc, dbg=None, layers=(0, 1, 2, 3), l0_stop=None):
    final_store_combine = (3 in layers) and (dbg is None)
    es = contextlib.ExitStack()
    din = {}
    spec = {
        "L0_attn_x": (S, D),
        "L0_attn_lt_w_in": (3 * D, D), "L0_attn_lt_b_in": (3 * D,),
        "L0_attn_lt_w_out": (D, D), "L0_attn_lt_b_out": (D,),
        "L0_attn_st_w_in": (3 * D, D), "L0_attn_st_b_in": (3 * D,),
        "L0_attn_st_w_out": (D, D), "L0_attn_st_b_out": (D,),
        "L0_attn_protos": (8, D), "L0_attn_fus_w": (D, 2 * D), "L0_attn_fus_b": (D,),
        "L0_ffn_router_w": (D, 16),
        "L0_ffn_w1": (16, D, 2048), "L0_ffn_b1": (16, 2048),
        "L0_ffn_w2": (16, 2048, D), "L0_ffn_b2": (16, D),
        "L1_attn_W_qkv": (D, 3 * D), "L1_attn_b_qkv": (3 * D,),
        "L1_attn_W_out": (D, D), "L1_attn_b_out": (D,),
        "L1_ffn_w1": (4096, D), "L1_ffn_conv_w": (4096, 1, 3), "L1_ffn_conv_b": (4096,),
        "L1_ffn_w2": (D, 4096),
        "L2_attn_W_qkv": (3 * D, D), "L2_attn_b_qkv": (3 * D,),
        "L2_attn_W_out": (D, D), "L2_attn_b_out": (D,),
        "L2_attn_W_gate": (D, D), "L2_attn_b_gate": (D,),
        "L2_attn_scale_gain": (11, 16), "L2_attn_W_qscale": (11, 64),
        "L2_attn_identity_bypass": (16,),
        "L2_ffn_weight": (D, D), "L2_ffn_gate_w": (8, D),
        "L2_ffn_lora_U": (8, 16, D), "L2_ffn_lora_V": (8, D, 16),
        "L3_attn_Wq": (D, D), "L3_attn_bq": (D,),
        "L3_attn_Wk": (D, D), "L3_attn_bk": (D,),
        "L3_attn_Wv": (D, D), "L3_attn_bv": (D,),
        "L3_attn_Wo": (D, D), "L3_attn_bo": (D,),
        "L3_ffn_w1": (4096, D), "L3_ffn_b1": (4096,),
        "L3_ffn_w2": (D, 4096), "L3_ffn_b2": (D,),
    }
    for name, shp in spec.items():
        din[name] = nc.dram_tensor(name, list(shp), F32, kind="ExternalInput").ap()
    dout = nc.dram_tensor("out", [S, D], F32, kind="ExternalOutput").ap()

    dbg_outs = set() if dbg != "dump" else {
        "d_s", "d_xn", "d_q2", "d_k2", "d_v2", "d_gains", "d_sd", "d_t1", "d_t2", "d_comb", "d_av"}
    dram = lambda n, shp: nc.dram_tensor(
        n, list(shp), F32, kind=("ExternalOutput" if n in dbg_outs else "Internal")).ap()
    d_s = dram("d_s", (S, D))
    d_x = dram("d_x", (S, D))
    d_xn = dram("d_xn", (S, D))
    d_av = dram("d_av", (D, S))
    d_t1 = dram("d_t1", (4096, 2048))
    d_t2 = dram("d_t2", (4096, 2048))
    d_wq1 = dram("d_wq1", (D, 4096))
    d_wq2 = dram("d_wq2", (4096, D))
    d_comb = dram("d_comb", (2 * D, S))
    d_q2 = dram("d_q2", (D, S))
    d_k2 = dram("d_k2", (D, S))
    d_q2_tm = dram("d_q2_tm", (S, D))
    d_k2_tm = dram("d_k2_tm", (S, D))
    d_v2 = dram("d_v2", (S, D))
    d_gains = dram("d_gains", (16, 11 * S))
    d_sd = dram("d_sd", (S, 358))

    tw_specs = {
        "lt_w_in": ("L0_attn_lt_w_in", 3 * D, D), "lt_w_out": ("L0_attn_lt_w_out", D, D),
        "st_w_in": ("L0_attn_st_w_in", 3 * D, D), "st_w_out": ("L0_attn_st_w_out", D, D),
        "fus_w": ("L0_attn_fus_w", D, 2 * D), "protos": ("L0_attn_protos", 8, D),
        "l1f_w1": ("L1_ffn_w1", 4096, D), "l1f_w2": ("L1_ffn_w2", D, 4096),
        "l2_qkv": ("L2_attn_W_qkv", 3 * D, D), "l2_out": ("L2_attn_W_out", D, D),
        "l2_gate": ("L2_attn_W_gate", D, D), "l2_qscale": ("L2_attn_W_qscale", 11, 64),
        "l2f_w": ("L2_ffn_weight", D, D), "l2f_gw": ("L2_ffn_gate_w", 8, D),
        "l3f_w1": ("L3_ffn_w1", 4096, D), "l3f_w2": ("L3_ffn_w2", D, 4096),
        "l2f_U": (None, 128, D), "l2f_V": (None, 128, D),
    }
    d_w = {}
    for k, (src, r, c) in tw_specs.items():
        if k == "l2f_U":
            d_w[k] = dram("d_w_" + k, (D, 128))
        elif k == "l2f_V":
            d_w[k] = dram("d_w_" + k, (128, D))
        else:
            d_w[k] = dram("d_w_" + k, (c, r))

    sbuf = {}

    def SB(name, shape, dt=F32):
        sbuf[name] = nc.alloc_sbuf_tensor(name, shape, dt)
        return sbuf[name]

    xnT = SB("xnT", [128, NCH * S])
    ident = SB("ident", [128, 128])
    ones_row = SB("ones_row", [1, 128])
    ln_nmean = SB("ln_nmean", [128, NT])
    ln_rstd = SB("ln_rstd", [128, NT])
    wa = SB("wa", [128, 4096])
    wb = SB("wb", [128, 4096])
    tA = SB("tA", [128, 2048])
    tB = SB("tB", [128, 2048])
    tC = SB("tC", [128, 2056])
    tD = SB("tD", [128, 2048])
    tE = SB("tE", [128, 2048])
    tF = SB("tF", [128, 2112])
    expb = SB("expb", [128, 4 * 512])
    sc1 = SB("sc1", [128, 16])
    sc2 = SB("sc2", [128, 16])
    sc3 = SB("sc3", [128, 16])
    rrow = SB("rrow", [1, 2048])
    brow = SB("brow", [1, 2048])
    xtile = SB("xtile", [128, 1024])
    big = SB("big", [128, 6144])
    segq_bc = big.ap()[:, 0:2048]
    sing_bc = big.ap()[:, 2048:4096]
    masks = big.ap()[:, 4096:6144]
    qg = big.ap()
    ti32 = SB("ti32", [128, 2048], I32)
    ones_col = SB("ones_col", [128, 1])
    seg_tm = SB("seg_tm", [128, 16])
    ohR_tm = SB("ohR_tm", [128, 256])
    wtok_tm = SB("wtok_tm", [128, 16])
    b1col = SB("b1col", [128, 16])
    accA = SB("accA", [128, 1])
    convw = SB("convw", [128, 128])
    convb = SB("convb", [128, 32])

    ps = [nc.alloc_psum_tensor("psb%d" % i, [128, 512], F32) for i in range(8)]

    p = P(nc, es)
    bank_guard = {}

    def _gp_init(g):
        g.memset(ones_row[:, :], 1.0)
        g.memset(ones_col[:, :], 1.0)
        return g.iota(tA[:, 0:512], [[1, 512]], base=0, channel_multiplier=-1,
                      allow_small_or_imprecise_dtypes=True)
    p.on("gpsimd", _gp_init)
    t = p.ck("gpsimd")
    p.wait("vector", t)
    p.on("vector", lambda v: v.tensor_scalar(ident[:, :], tA[:, 0:128], 0.0, None, OP.is_equal))
    for r in range(4):
        p.on("vector", lambda v, r=r: v.tensor_scalar(
            masks[:, ts(r, 512)], tA[:, 0:512], float(128 * r), None, OP.is_ge))
    t_const = p.ck("vector")
    p.wait("tensor", t_const)
    p.wait("gpsimd", t_const)
    p.wait("scalar", t_const)

    pret_guard = {}

    def pretranspose(src, rows, cols, dst):
        if cols > 2048:
            assert cols % 2048 == 0
            for cg in range(cols // 2048):
                pretranspose(src[:, cg * 2048:(cg + 1) * 2048], rows, 2048,
                             dst[cg * 2048:(cg + 1) * 2048, :])
            return
        ncol = (cols + 127) // 128
        nslab = (rows + 127) // 128
        for ri in range(nslab):
            r0, r1 = ri * 128, min(rows, ri * 128 + 128)
            rl = r1 - r0
            bname = "wa" if ri % 2 == 0 else "wb"
            ename = "tA" if ri % 2 == 0 else "tB"
            buf = wa if ri % 2 == 0 else wb
            ebuf = tA if ri % 2 == 0 else tB
            p.wait("sync", pret_guard.get(bname))
            tok_l = p.dma("sync", buf[0:rl, 0:cols], src[r0:r1, 0:cols])
            p.wait("tensor", tok_l)
            p.wait("vector", pret_guard.get(ename))
            prev_ev = {}
            for ci in range(ncol):
                c0, c1 = ci * 128, min(cols, ci * 128 + 128)
                bi = 6 + (ci % 2)
                bank = ps[bi]
                p.wait("tensor", bank_guard.get(bi))
                p.on("tensor", lambda te, rl=rl, c0=c0, c1=c1, bank=bank, buf=buf:
                     te.transpose(bank[0:c1 - c0, 0:rl], buf[0:rl, c0:c1], ident[0:rl, 0:rl]))
                tmm = p.ck("tensor")
                p.wait("vector", tmm)
                p.on("vector", lambda v, rl=rl, c0=c0, c1=c1, bank=bank, ci=ci, ebuf=ebuf:
                     v.tensor_copy(ebuf[0:c1 - c0, ci * 128:ci * 128 + rl], bank[0:c1 - c0, 0:rl]))
                prev_ev[ci] = p.ck("vector")
                bank_guard[bi] = prev_ev[ci]
                pret_guard[bname] = tmm
            p.wait("gpsimd", prev_ev[ncol - 1])
            if cols >= 128:
                out_ap = dst[0:cols, r0:r1].rearrange("(c q) r -> q c r", q=128)
                in_ap = ebuf.ap().rearrange("q (c r) -> q c r", r=128)[0:128, 0:ncol, 0:rl]
            else:
                out_ap = dst[0:cols, r0:r1]
                in_ap = ebuf.ap()[0:cols, 0:rl]
            pret_guard[ename] = p.dma("gpsimd", out_ap, in_ap)

    need_prep = []
    if 0 in layers:
        need_prep += ["lt_w_in", "lt_w_out", "st_w_in", "st_w_out", "fus_w", "protos"]
    if 1 in layers:
        need_prep += ["l1f_w1", "l1f_w2"]
    if 2 in layers:
        need_prep += ["l2_qkv", "l2_out", "l2_gate", "l2_qscale", "l2f_w", "l2f_gw"]
    if 3 in layers:
        need_prep += ["l3f_w1", "l3f_w2"]
    for k in need_prep:
        srcname, rows, cols = tw_specs[k]
        pretranspose(din[srcname], rows, cols, d_w[k])
    if 2 in layers:
        pretranspose(din["L2_ffn_lora_U"].rearrange("e r d -> (e r) d"), 128, D, d_w["l2f_U"])
        for e in range(8):
            pretranspose(din["L2_ffn_lora_V"][e], D, 16, d_w["l2f_V"][16 * e:16 * e + 16, :])
    if need_prep or 2 in layers:
        p.barrier()

    ln_eps = 1e-5

    def emit_ln_and_transpose(store_xn_dram=True, src=None):
        src = src if src is not None else d_x
        tok_dxn = None
        tok_xt_prev = None
        for tt in range(NT):
            xt = xtile[:, 0:1024]
            tok_xt = p.dma("sync", xtile[:, 0:1024], src[ts(tt), :])
            p.wait("vector", tok_xt)
            p.wait("scalar", tok_xt)
            p.on("vector", lambda v, xt=xt, tt=tt: v.reduce_sum(sc1[:, tt:tt + 1], xt, axis=AX.X))
            p.ckw("vector")
            p.on("vector", lambda v, tt=tt: v.tensor_scalar(
                ln_nmean[:, tt:tt + 1], sc1[:, tt:tt + 1], -1.0 / D, None, OP.mult))
            tv = p.ck("vector")
            p.wait("scalar", tv)
            if tok_dxn is not None:
                p.wait("scalar", tok_dxn)
            p.on("scalar", lambda s, xt=xt, tt=tt: s.activation(
                tB[:, 0:1024], xt, AF.Square, bias=ln_nmean[:, tt:tt + 1], scale=1.0,
                accum_out=accA[:, 0:1]))
            tsq = p.ck("scalar")
            p.wait("vector", tsq)
            p.on("vector", lambda v, tt=tt: v.tensor_scalar(
                sc2[:, tt:tt + 1], accA[:, 0:1], 1.0 / D, ln_eps, OP.mult, OP.add))
            tv2 = p.ck("vector")
            p.wait("scalar", tv2)
            p.on("scalar", lambda s, tt=tt: s.activation(sc3[:, tt:tt + 1], sc2[:, tt:tt + 1], AF.Sqrt))
            tsq2 = p.ck("scalar")
            p.wait("vector", tsq2)
            p.on("vector", lambda v, tt=tt: v.reciprocal(sc1[:, tt:tt + 1], sc3[:, tt:tt + 1]))
            p.ckw("vector")
            p.on("vector", lambda v, tt=tt: v.tensor_mul(sc3[:, tt:tt + 1], sc1[:, tt:tt + 1], sc1[:, tt:tt + 1]))
            p.ckw("vector")
            p.on("vector", lambda v, tt=tt: v.tensor_mul(sc3[:, tt:tt + 1], sc3[:, tt:tt + 1], sc2[:, tt:tt + 1]))
            p.ckw("vector")
            p.on("vector", lambda v, tt=tt: v.tensor_scalar(
                sc3[:, tt:tt + 1], sc3[:, tt:tt + 1], -0.5, 1.5, OP.mult, OP.add))
            p.ckw("vector")
            p.on("vector", lambda v, tt=tt: v.tensor_mul(ln_rstd[:, tt:tt + 1], sc1[:, tt:tt + 1], sc3[:, tt:tt + 1]))
            p.ckw("vector")
            p.on("vector", lambda v, tt=tt: v.tensor_mul(sc2[:, tt:tt + 1], ln_nmean[:, tt:tt + 1], ln_rstd[:, tt:tt + 1]))
            tvr = p.ck("vector")
            p.wait("scalar", tvr)
            p.on("scalar", lambda s, xt=xt, tt=tt: s.activation(
                tB[:, 0:1024], xt, AF.Identity, bias=sc2[:, tt:tt + 1], scale=ln_rstd[:, tt:tt + 1]))
            tap = p.ck("scalar")
            if store_xn_dram:
                p.wait("gpsimd", tap)
                tok_dxn = p.dma("gpsimd", d_xn[ts(tt), :], tB[:, 0:1024])
            p.wait("tensor", tap)
            evs = {}
            for c in range(NCH):
                bi = 6 + (c % 2)
                bank = ps[bi]
                p.wait("tensor", bank_guard.get(bi))
                p.on("tensor", lambda te, c=c, bank=bank: te.transpose(
                    bank[0:128, 0:128], tB[:, ts(c)], ident[:, :]))
                tmm = p.ck("tensor")
                p.wait("vector", tmm)
                p.on("vector", lambda v, c=c, tt=tt, bank=bank: v.tensor_copy(
                    xnT[:, c * S + tt * 128: c * S + tt * 128 + 128], bank[0:128, 0:128]))
                evs[c] = p.ck("vector")
                bank_guard[bi] = evs[c]
            p.wait("scalar", evs[NCH - 1])
            p.wait("sync", tap)
        p.barrier()

    def load_w_slices(dw, os0, width, buf, nch=NCH, bufoff=0):
        src = dw.rearrange("(c r) o -> r c o", r=128)[:, :, os0:os0 + width]
        dst = buf.ap()[:, bufoff:bufoff + nch * width].rearrange("p (c o) -> p c o", c=nch)
        return p.dma("sync", dst, src)

    def bias_row(dbias, n, off=0, buf=None):
        buf = buf or brow
        return p.dma("sync", buf[0:1, off:off + n], dbias.unsqueeze(0)[0:1, 0:n])

    def emit_mha(w_qkv_inmajor, qkv_perhead_cols, b_qkv, w_out_inmajor, b_out,
                 causal=False, stabilize=False, mask_fm=None, dst_s=None,
                 scale=0.125, proj_fm_dst=None, proj_post_evac=None):
        V2 = tF
        for hp in range(8):
            h0 = 2 * hp
            wtoks = []
            for j, kind in enumerate((0, 1, 2)):
                for hh in range(2):
                    qc, kc_, vc = qkv_perhead_cols(h0 + hh)
                    col0 = (qc, kc_, vc)[kind]
                    src = w_qkv_inmajor.rearrange("(c r) o -> r c o", r=128)[:, :, col0:col0 + 64]
                    dstap = wa.ap()[:, kind * 1024:kind * 1024 + 8 * 128].rearrange(
                        "p (c o) -> p c o", c=8)[:, :, hh * 64:hh * 64 + 64]
                    wtoks.append(p.dma("sync", dstap, src))
            btoks = []
            for hh in range(2):
                qc, kc_, vc = qkv_perhead_cols(h0 + hh)
                btoks.append(p.dma("sync", sc1[hh * 64:hh * 64 + 64, 8:9], b_qkv[qc:qc + 64].unsqueeze(1)))
                btoks.append(p.dma("sync", sc2[hh * 64:hh * 64 + 64, 8:9], b_qkv[kc_:kc_ + 64].unsqueeze(1)))
                btoks.append(p.dma("sync", rrow[0:1, hh * 64:hh * 64 + 64], b_qkv[vc:vc + 64].unsqueeze(0)))
            p.wait("tensor", wtoks)
            p.wait("tensor", btoks)
            p.wait("scalar", btoks)
            for dst, woff, bcol in ((tD, 0, sc1), (tC, 1024, sc2)):
                evtoks = {}
                for qs in range(4):
                    bank = ps[4 + (qs % 2)]
                    if qs >= 2:
                        p.wait("tensor", evtoks[qs - 2])
                    for c in range(NCH):
                        p.on("tensor", lambda te, c=c, qs=qs, woff=woff, bank=bank:
                             te.matmul(bank[:, 0:512], wa.ap()[:, woff + c * 128:woff + c * 128 + 128],
                                       xnT[:, c * S + qs * 512:c * S + qs * 512 + 512],
                                       start=(c == 0), stop=(c == NCH - 1)))
                    tmm = p.ck("tensor")
                    p.wait("scalar", tmm)
                    p.on("scalar", lambda s, dst=dst, qs=qs, bank=bank, bcol=bcol:
                         s.activation(dst[:, ts(qs, 512)], bank[:, 0:512], AF.Identity,
                                      bias=bcol[:, 8:9], scale=1.0))
                    evtoks[qs] = p.ck("scalar")
                p.wait("tensor", evtoks[2])
                p.wait("tensor", evtoks[3])
            evtoks = {}
            for tt in range(NT):
                bank = ps[6]
                if tt >= 1:
                    p.wait("tensor", evtoks[tt - 1])
                for c in range(NCH):
                    p.on("tensor", lambda te, c=c, tt=tt, bank=bank:
                         te.matmul(bank[:, 0:128], xnT[:, c * S + tt * 128:c * S + tt * 128 + 128],
                                   wa.ap()[:, 2048 + c * 128:2048 + c * 128 + 128],
                                   start=(c == 0), stop=False))
                p.on("tensor", lambda te, tt=tt, bank=bank:
                     te.matmul(bank[:, 0:128], ones_row[0:1, 0:128], rrow[0:1, 0:128],
                               start=False, stop=True))
                tmm = p.ck("tensor")
                p.wait("vector", tmm)
                p.on("vector", lambda v, tt=tt, bank=bank: v.tensor_copy(V2[:, tt * 132:tt * 132 + 64], bank[:, 0:64]))
                p.on("vector", lambda v, tt=tt, bank=bank: v.tensor_copy(V2[:, tt * 132 + 66:tt * 132 + 130], bank[:, 64:128]))
                evtoks[tt] = p.ck("vector")
            V2r = V2.ap().rearrange("p (t o) -> p t o", t=NT)
            p.on("vector", lambda v: v.memset(V2r[:, :, 64:65], 1.0))
            p.on("vector", lambda v: v.memset(V2r[:, :, 130:131], 1.0))
            tones = p.ck("vector")
            p.wait("tensor", tones)

            prev_exp_last = None
            prev_av_last = None
            for hh in range(2):
                qoff = 64 * hh
                voff = 66 * hh
                for qs in range(4):
                    nkc = (4 * qs + 4) if causal else NT
                    av_bank = ps[2 + hh]
                    exp_ready = {}
                    av_done = {}
                    for kc in range(nkc):
                        sb_bank = ps[kc % 2]
                        if kc >= 2:
                            p.wait("tensor", exp_ready[kc - 2])
                        elif kc == 0 and prev_exp_last is not None:
                            p.wait("tensor", prev_exp_last)
                        p.on("tensor", lambda te, kc=kc, qs=qs, qoff=qoff, sb_bank=sb_bank:
                             te.matmul(sb_bank[:, 0:512],
                                       tC[qoff:qoff + 64, kc * 128:kc * 128 + 128],
                                       tD[qoff:qoff + 64, qs * 512:qs * 512 + 512],
                                       start=True, stop=True))
                        tsc = p.ck("tensor")
                        p.wait("scalar", tsc)
                        if kc >= 4:
                            p.wait("scalar", av_done[kc - 4])
                        elif kc == 0 and prev_av_last is not None:
                            p.wait("scalar", prev_av_last)
                        ebuf = expb[:, ts(kc % 4, 512)]
                        p.on("scalar", lambda s, ebuf=ebuf, sb_bank=sb_bank:
                             s.activation(ebuf, sb_bank[:, 0:512], AF.Exp, scale=scale))
                        texp = p.ck("scalar")
                        if causal and kc >= 4 * qs:
                            r = kc - 4 * qs
                            p.wait("vector", texp)
                            p.on("vector", lambda v, ebuf=ebuf, r=r:
                                 v.tensor_mul(ebuf, ebuf, masks[:, ts(r, 512)]))
                            texp = p.ck("vector")
                        if mask_fm is not None:
                            p.wait("vector", texp)
                            mask_fm(kc, qs, ebuf)
                            texp = p.ck("vector")
                        exp_ready[kc] = texp
                        p.wait("tensor", texp)
                        p.on("tensor", lambda te, kc=kc, voff=voff, av_bank=av_bank, ebuf=ebuf, nkc=nkc:
                             te.matmul(av_bank[0:65, 0:512], V2[:, kc * 132 + voff:kc * 132 + voff + 65],
                                       ebuf, start=(kc == 0), stop=(kc == nkc - 1)))
                        av_done[kc] = p.ck("tensor")
                    p.wait("vector", av_done[nkc - 1])
                    p.on("vector", lambda v, av_bank=av_bank: v.reciprocal(
                        rrow[0:1, 1024:1536], av_bank[64:65, 0:512]))
                    trc = p.ck("vector")
                    p.wait("tensor", trc)
                    p.on("tensor", lambda te: te.matmul(ps[4][0:64, 0:512], ones_row[0:1, 0:64],
                                                        rrow[0:1, 1024:1536], start=True, stop=True))
                    tbc = p.ck("tensor")
                    p.wait("vector", tbc)
                    p.on("vector", lambda v: v.tensor_copy(tE[0:64, 512:1024], ps[4][0:64, 0:512]))
                    p.on("vector", lambda v, av_bank=av_bank: v.tensor_mul(
                        tE[0:64, 0:512], av_bank[0:64, 0:512], tE[0:64, 512:1024]))
                    tn = p.ck("vector")
                    p.wait("gpsimd", tn)
                    tok_park = p.dma("gpsimd", d_av[64 * (h0 + hh):64 * (h0 + hh) + 64, qs * 512:qs * 512 + 512],
                                     tE[0:64, 0:512])
                    p.wait("vector", tok_park)
                    p.wait("tensor", tok_park)
                    prev_exp_last = exp_ready[nkc - 1]
                    prev_av_last = av_done[nkc - 1]
            p.barrier()

        if proj_fm_dst is None:
            proj_from_fm_dram(d_av, w_out_inmajor, b_out, dst_s)
        else:
            dstd, row0 = proj_fm_dst
            proj_fm(d_av, w_out_inmajor, b_out, dstd, post_evac=proj_post_evac, dst_row0=row0)

    def proj_from_fm_dram(d_src_fm, w_inmajor, b_dram, dst_dram, nch_in=8, dout=1024):
        tb = bias_row(b_dram, dout, off=0) if b_dram is not None else None
        if tb is not None:
            p.wait("tensor", tb)
        nos = dout // 512
        nhalf = (nch_in + 7) // 8
        prev_te = None
        prev_ev = None
        for tg in range(4):
            for half in range(nhalf):
                nchh = min(8, nch_in - half * 8)
                if prev_te is not None:
                    p.wait("sync", prev_te)
                toks = []
                for os_, buf in ((0, wa), (1, wb))[:nos]:
                    src = w_inmajor.rearrange("(c r) o -> r c o", r=128)[
                        :, half * 8:half * 8 + nchh, os_ * 512:os_ * 512 + 512]
                    dstap = buf.ap()[:, 0:nchh * 512].rearrange("p (c o) -> p c o", c=nchh)
                    toks.append(p.dma("sync", dstap, src))
                for c in range(nchh):
                    dstb = tC if c < 4 else tD
                    toks.append(p.dma("sync", dstb[:, ts(c % 4, 512)],
                                      d_src_fm[ts(half * 8 + c), tg * 512:tg * 512 + 512]))
                p.wait("tensor", toks)
                if prev_ev is not None and half == 0:
                    p.wait("tensor", prev_ev)
                for tt4 in range(4):
                    for os_ in range(nos):
                        bank = ps[tt4 * 2 + os_]
                        wbuf = wa if os_ == 0 else wb
                        for c in range(nchh):
                            dstb = tC if c < 4 else tD
                            p.on("tensor", lambda te, c=c, tt4=tt4, bank=bank, wbuf=wbuf, dstb=dstb, half=half:
                                 te.matmul(bank[:, 0:512],
                                           dstb[:, (c % 4) * 512 + tt4 * 128:(c % 4) * 512 + tt4 * 128 + 128],
                                           wbuf.ap()[:, c * 512:c * 512 + 512],
                                           start=(half == 0 and c == 0), stop=False))
                prev_te = p.ck("tensor")
            evs = []
            for tt4 in range(4):
                tt = tg * 4 + tt4
                for os_ in range(nos):
                    bank = ps[tt4 * 2 + os_]
                    p.on("tensor", lambda te, bank=bank, os_=os_:
                         te.matmul(bank[:, 0:512], ones_row[0:1, 0:128],
                                   brow[0:1, os_ * 512:os_ * 512 + 512], start=False, stop=True)
                         if b_dram is not None else
                         te.matmul(bank[:, 0:512], ones_row[0:1, 0:128],
                                   brow[0:1, os_ * 512:os_ * 512 + 512], start=False, stop=True))
                    tmm = p.ck("tensor")
                    last_tmm = tmm
                    p.wait("vector", tmm)
                    dstt = tE if os_ == 0 else tF
                    p.on("vector", lambda v, bank=bank, tt4=tt4, dstt=dstt:
                         v.tensor_copy(dstt[:, tt4 * 512:tt4 * 512 + 512], bank[:, 0:512]))
                    evs.append(p.ck("vector"))
                p.wait("gpsimd", evs[-1])
                tok_s1 = p.dma("gpsimd", dst_dram[ts(tt), 0:512], tE[:, tt4 * 512:tt4 * 512 + 512])
                if nos > 1:
                    tok_s1 = p.dma("gpsimd", dst_dram[ts(tt), 512:1024], tF[:, tt4 * 512:tt4 * 512 + 512])
                p.wait("vector", tok_s1)
            prev_te = last_tmm
            prev_ev = evs[-1]
        p.barrier()

    def proj_fm(d_src_fm, w_inmajor, b_dram, dst_fm_dram, nch_in=8, dout=1024,
                post_evac=None, dst_row0=0):
        tb = p.dma("sync", sc1[:, 0:dout // 128],
                   b_dram.rearrange("(c p) -> p c", p=128), allow_slow_non_contiguous=True)
        p.wait("scalar", tb)
        prev_te = None
        for oc in range(dout // 128):
            bufw = wa if oc % 2 == 0 else wb
            if prev_te is not None:
                p.wait("sync", prev_te)
            tokw = p.dma("sync", bufw.ap()[:, 0:nch_in * 128].rearrange("p (c o) -> p c o", c=nch_in),
                         w_inmajor.rearrange("(c r) o -> r c o", r=128)[:, :, ts(oc)])
            p.wait("tensor", tokw)
            for qs in range(4):
                toks = []
                if prev_te is not None:
                    p.wait("sync", prev_te)
                for c in range(nch_in):
                    dstb = (tC, tD)[c // 4]
                    toks.append(p.dma("sync", dstb[:, ts(c % 4, 512)],
                                      d_src_fm[ts(c), qs * 512:qs * 512 + 512]))
                p.wait("tensor", toks)
                bi = qs % 2
                p.wait("tensor", bank_guard.get(bi))
                for c in range(nch_in):
                    dstb = (tC, tD)[c // 4]
                    p.on("tensor", lambda te, c=c, qs=qs, bi=bi, bufw=bufw, dstb=dstb:
                         te.matmul(ps[bi][:, 0:512], bufw.ap()[:, c * 128:c * 128 + 128],
                                   dstb[:, ts(c % 4, 512)], start=(c == 0), stop=(c == nch_in - 1)))
                tmm = p.ck("tensor")
                prev_te = tmm
                p.wait("scalar", tmm)
                p.on("scalar", lambda sx, bi=bi, oc=oc: sx.activation(
                    tB[:, 0:512], ps[bi][:, 0:512], AF.Identity, bias=sc1[:, oc:oc + 1], scale=1.0))
                tev = p.ck("scalar")
                bank_guard[bi] = tev
                if post_evac is not None:
                    tev = post_evac(oc, qs, tB[:, 0:512], tev)
                p.wait("gpsimd", tev)
                tok_st = p.dma("gpsimd", dst_fm_dram[dst_row0 + oc * 128:dst_row0 + oc * 128 + 128,
                                                     qs * 512:qs * 512 + 512], tB[:, 0:512])
                p.wait("scalar", tok_st)
                p.wait("vector", tok_st)
        p.barrier()

    def l0_seg_and_router():
        tokp = p.dma("sync", wa[0:128, 0:8 * 8].rearrange("p (c o) -> p c o", c=8),
                     d_w["protos"].rearrange("(c r) o -> r c o", r=128))
        tokr = p.dma("sync", wb[0:128, 0:8 * 16].rearrange("p (c o) -> p c o", c=8),
                     din["L0_ffn_router_w"].rearrange("(c r) o -> r c o", r=128))
        tok8 = None
        p.on("gpsimd", lambda g: g.iota(tB[:, 0:16], [[1, 16]], base=0, channel_multiplier=0,
                                        allow_small_or_imprecise_dtypes=True))
        tio = p.ck("gpsimd")
        p.wait("tensor", tokp)
        p.wait("tensor", tokr)
        p.wait("vector", tio)
        for tt in range(NT):
            bi = 4 + (tt % 2)
            p.wait("tensor", bank_guard.get(bi))
            for c in range(NCH):
                p.on("tensor", lambda te, c=c, tt=tt, bi=bi: te.matmul(
                    ps[bi][:, 0:8], wa.ap()[:, c * 8:c * 8 + 8],
                    xnT[:, c * S + tt * 128:c * S + tt * 128 + 128].rearrange("p t -> p t") if False else xnT[:, c * S + tt * 128:c * S + tt * 128 + 128],
                    start=(c == 0), stop=False) if False else te.matmul(
                    ps[bi][0:128, 0:8], xnT[:, c * S + tt * 128:c * S + tt * 128 + 128],
                    wa.ap()[:, c * 8:c * 8 + 8], start=(c == 0), stop=(c == NCH - 1)))
            tmm = p.ck("tensor")
            p.wait("vector", tmm)
            p.on("vector", lambda v, bi=bi: v.tensor_copy(tA[:, 0:8], ps[bi][0:128, 0:8]))
            p.ckw("vector")
            p.on("vector", lambda v: v.reduce_max(sc2[:, 0:1], tA[:, 0:8], axis=AX.X))
            p.ckw("vector")
            p.on("vector", lambda v, tt=tt: v.tensor_scalar(tA[:, 8:16], tA[:, 0:8], sc2[:, 0:1], None, OP.is_equal))
            p.ckw("vector")
            p.on("vector", lambda v, tt=tt: v.tensor_mul(tA[:, 16:24], tA[:, 8:16], tB[:, 0:8]))
            p.ckw("vector")
            p.on("vector", lambda v, tt=tt: v.reduce_sum(seg_tm[:, tt:tt + 1], tA[:, 16:24], axis=AX.X))
            p.ckw("vector")
            p.on("vector", lambda v, tt=tt: v.tensor_copy(tF[:, tt * 8:tt * 8 + 8], tA[:, 8:16]))
            tv = p.ck("vector")
            bank_guard[bi] = tv
            last_tv = tv
            bi2 = 6 + (tt % 2)
            p.wait("tensor", bank_guard.get(bi2))
            for c in range(NCH):
                p.on("tensor", lambda te, c=c, tt=tt, bi2=bi2: te.matmul(
                    ps[bi2][0:128, 0:16], xnT[:, c * S + tt * 128:c * S + tt * 128 + 128],
                    wb.ap()[:, c * 16:c * 16 + 16], start=(c == 0), stop=(c == NCH - 1)))
            tmm2 = p.ck("tensor")
            p.wait("vector", tmm2)
            p.on("vector", lambda v, bi2=bi2: v.tensor_copy(tA[:, 32:48], ps[bi2][0:128, 0:16]))
            p.ckw("vector")
            p.on("vector", lambda v: v.reduce_max(sc3[:, 0:1], tA[:, 32:48], axis=AX.X))
            p.ckw("vector")
            p.on("vector", lambda v, tt=tt: v.tensor_scalar(ohR_tm[:, tt * 16:tt * 16 + 16], tA[:, 32:48], sc3[:, 0:1], None, OP.is_equal))
            tv2 = p.ck("vector")
            bank_guard[bi2] = tv2
            p.wait("scalar", tv2)
            p.on("scalar", lambda sx, tt=tt: sx.activation(wtok_tm[:, tt:tt + 1], sc3[:, 0:1], AF.Sigmoid))
            tsg = p.ck("scalar")
            p.wait("vector", tsg)
        bi = 4
        p.wait("tensor", bank_guard.get(bi))
        p.wait("tensor", last_tv)
        for tt in range(NT):
            p.on("tensor", lambda te, tt=tt, bi=bi: te.matmul(
                ps[bi][0:1, 0:8], ones_col[:, 0:1], tF[:, tt * 8:tt * 8 + 8],
                start=(tt == 0), stop=(tt == NT - 1)))
        tmm = p.ck("tensor")
        p.wait("vector", tmm)
        p.on("vector", lambda v, bi=bi: v.tensor_copy(rrow[0:1, 1600:1608], ps[bi][0:1, 0:8]))
        tcv = p.ck("vector")
        bank_guard[bi] = tcv
        p.wait("tensor", tcv)
        bi = 5
        p.wait("tensor", bank_guard.get(bi))
        p.on("tensor", lambda te, bi=bi: te.matmul(ps[bi][0:128, 0:8], ones_row[0:1, 0:128],
                                                   rrow[0:1, 1600:1608], start=True, stop=True))
        tmm = p.ck("tensor")
        p.wait("vector", tmm)
        p.on("vector", lambda v, bi=bi: v.tensor_copy(tA[:, 0:8], ps[bi][0:128, 0:8]))
        tbc = p.ck("vector")
        bank_guard[bi] = tbc
        for tt in range(NT):
            p.on("vector", lambda v, tt=tt: v.tensor_mul(tB[:, 32:40], tF[:, tt * 8:tt * 8 + 8], tA[:, 0:8]))
            p.ckw("vector")
            p.on("vector", lambda v, tt=tt: v.reduce_sum(sc2[:, tt:tt + 1], tB[:, 32:40], axis=AX.X))
            p.ckw("vector")
            p.on("vector", lambda v, tt=tt: v.tensor_scalar(sc2[:, tt:tt + 1], sc2[:, tt:tt + 1], 1.0, None, OP.is_equal))
            tv = p.ckw("vector")
        p.wait("tensor", tv)
        for (srccol, dst, dstoff) in ((seg_tm, rrow, 0), (sc2, brow, 0)):
            for tt in range(NT):
                bi = 6 + (tt % 2)
                p.wait("tensor", bank_guard.get(bi))
                p.on("tensor", lambda te, srccol=srccol, tt=tt, bi=bi: te.transpose(
                    ps[bi][0:1, 0:128], srccol[:, tt:tt + 1], ident[:, :]))
                tmm = p.ck("tensor")
                p.wait("vector", tmm)
                p.on("vector", lambda v, tt=tt, bi=bi, dst=dst, dstoff=dstoff: v.tensor_copy(
                    dst[0:1, dstoff + tt * 128:dstoff + tt * 128 + 128], ps[bi][0:1, 0:128]))
                bank_guard[bi] = p.ck("vector")
        for (srcrow, off, dstt) in ((rrow, 0, segq_bc), (brow, 0, sing_bc)):
            for qs in range(4):
                bi = 4 + (qs % 2)
                p.wait("tensor", bank_guard.get(bi))
                p.on("tensor", lambda te, srcrow=srcrow, off=off, qs=qs, bi=bi: te.matmul(
                    ps[bi][0:128, 0:512], ones_row[0:1, 0:128],
                    srcrow[0:1, off + qs * 512:off + qs * 512 + 512], start=True, stop=True))
                tmm = p.ck("tensor")
                p.wait("vector", tmm)
                p.on("vector", lambda v, qs=qs, bi=bi, dstt=dstt: v.tensor_copy(
                    dstt[:, qs * 512:qs * 512 + 512], ps[bi][0:128, 0:512]))
                bank_guard[bi] = p.ck("vector")
        p.barrier()

    def layer0():
        emit_ln_and_transpose(store_xn_dram=True)
        l0_seg_and_router()
        emit_mha(d_w["lt_w_in"], lambda h: (64 * h, 1024 + 64 * h, 2048 + 64 * h),
                 din["L0_attn_lt_b_in"], d_w["lt_w_out"], din["L0_attn_lt_b_out"],
                 causal=False, scale=0.125, proj_fm_dst=(d_comb, 0))
        if l0_stop == "long":
            return

        def st_mask(kc, qs, ebuf):
            p.on("vector", lambda v, kc=kc, qs=qs, ebuf=ebuf: v.scalar_tensor_tensor(
                ebuf, segq_bc[:, qs * 512:qs * 512 + 512], seg_tm[:, kc:kc + 1], ebuf,
                OP.is_equal, OP.mult))

        def st_where(oc, qs, tile_ap, tok):
            p.wait("vector", tok)
            p.on("vector", lambda v, oc=oc, qs=qs, tile_ap=tile_ap: v.tensor_sub(
                tD[:, 0:512], xnT[:, oc * S + qs * 512:oc * S + qs * 512 + 512], tile_ap))
            p.on("vector", lambda v, qs=qs: v.tensor_mul(
                tD[:, 0:512], tD[:, 0:512], sing_bc[:, qs * 512:qs * 512 + 512]))
            p.on("vector", lambda v, tile_ap=tile_ap: v.tensor_add(tile_ap, tile_ap, tD[:, 0:512]))
            return p.ck("vector")

        emit_mha(d_w["st_w_in"], lambda h: (64 * h, 1024 + 64 * h, 2048 + 64 * h),
                 din["L0_attn_st_b_in"], d_w["st_w_out"], din["L0_attn_st_b_out"],
                 causal=False, scale=0.125, mask_fm=st_mask,
                 proj_fm_dst=(d_comb, 1024), proj_post_evac=st_where)
        proj_from_fm_dram(d_comb, d_w["fus_w"], din["L0_attn_fus_b"], d_s, nch_in=16)
        l0_ffn_dense_and_combine()

    def l0_ffn_dense_and_combine():
        for e in range(16):
            tokb1 = p.dma("sync", b1col[:, 0:16],
                          din["L0_ffn_b1"][e].rearrange("(c p) -> p c", p=128),
                          allow_slow_non_contiguous=True)
            tokb2 = p.dma("sync", brow[0:1, 0:1024], din["L0_ffn_b2"][e].unsqueeze(0))
            p.wait("scalar", tokb1)
            p.wait("tensor", tokb2)
            for hc in range(16):
                bufw = wa if hc % 2 == 0 else wb
                tokw = p.dma("sync", bufw.ap()[:, 0:1024].rearrange("p (c o) -> p c o", c=8),
                             din["L0_ffn_w1"][e].rearrange("(c r) o -> r c o", r=128)[:, :, ts(hc)])
                p.wait("tensor", tokw)
                for qs in range(4):
                    bi = qs % 4
                    p.wait("tensor", bank_guard.get(bi))
                    for c in range(NCH):
                        p.on("tensor", lambda te, c=c, qs=qs, bi=bi, bufw=bufw: te.matmul(
                            ps[bi][:, 0:512], bufw.ap()[:, c * 128:c * 128 + 128],
                            xnT[:, c * S + qs * 512:c * S + qs * 512 + 512],
                            start=(c == 0), stop=(c == NCH - 1)))
                    tmm = p.ck("tensor")
                    p.wait("scalar", tmm)
                    p.on("scalar", lambda sx, bi=bi, hc=hc: sx.activation(
                        tE[:, 0:512], ps[bi][:, 0:512], AF.Gelu, bias=b1col[:, hc:hc + 1], scale=1.0))
                    tev = p.ck("scalar")
                    bank_guard[bi] = tev
                    p.wait("gpsimd", tev)
                    tok_st = p.dma("gpsimd", d_t1[ts(hc), qs * 512:qs * 512 + 512], tE[:, 0:512])
                    p.wait("scalar", tok_st)
                tte = p.ck("tensor") if p._dirty["tensor"] else tmm
                p.wait("sync", tte)
            for tg in range(4):
                for hc in range(16):
                    bufw = wa if hc % 2 == 0 else wb
                    tokw = p.dma("sync", bufw[0:128, 0:1024], din["L0_ffn_w2"][e][ts(hc), :])
                    tokh = p.dma("sync", tC[:, 0:512], d_t1[ts(hc), tg * 512:tg * 512 + 512])
                    p.wait("tensor", tokw)
                    p.wait("tensor", tokh)
                    for tt4 in range(4):
                        for os_ in range(2):
                            bi = tt4 * 2 + os_
                            if hc == 0:
                                p.wait("tensor", bank_guard.get(bi))
                            p.on("tensor", lambda te, tt4=tt4, os_=os_, bi=bi, hc=hc, bufw=bufw:
                                 te.matmul(ps[bi][:, 0:512], tC[:, tt4 * 128:tt4 * 128 + 128],
                                           bufw.ap()[:, os_ * 512:os_ * 512 + 512],
                                           start=(hc == 0), stop=False))
                    tmm = p.ck("tensor")
                    p.wait("sync", tmm)
                for tt4 in range(4):
                    tt = tg * 4 + tt4
                    for os_ in range(2):
                        bi = tt4 * 2 + os_
                        p.on("tensor", lambda te, bi=bi, os_=os_: te.matmul(
                            ps[bi][:, 0:512], ones_row[0:1, 0:128],
                            brow[0:1, os_ * 512:os_ * 512 + 512], start=False, stop=True))
                        tmm = p.ck("tensor")
                        p.wait("vector", tmm)
                        p.on("vector", lambda v, bi=bi, os_=os_, tt=tt, e=e: v.tensor_scalar(
                            tF[:, bi * 256:bi * 256 + 512] if False else tF[:, (bi % 4) * 512:(bi % 4) * 512 + 512],
                            ps[bi][:, 0:512], ohR_tm[:, tt * 16 + e:tt * 16 + e + 1], None, OP.mult))
                        tev = p.ck("vector")
                        bank_guard[bi] = tev
                        p.wait("gpsimd", tev)
                        tok_st = p.dma("gpsimd", d_t2[ts(tt), os_ * 512:os_ * 512 + 512],
                                       tF[:, (bi % 4) * 512:(bi % 4) * 512 + 512],
                                       accum_op=(OP.add if e > 0 else OP.bypass))
                        p.wait("vector", tok_st)
        p.barrier()
        for tt in range(NT):
            tokg = p.dma("sync", tB[:, 0:1024], d_t2[ts(tt), 0:1024])
            tok_xx = p.dma("sync", tC[:, 0:1024], d_x[ts(tt), :])
            tok_ss = p.dma("sync", tC[:, 1024:2048], d_s[ts(tt), :])
            p.wait("vector", tokg)
            p.on("vector", lambda v, tt=tt: v.tensor_scalar(tB[:, 0:1024], tB[:, 0:1024],
                                                            wtok_tm[:, tt:tt + 1], None, OP.mult))
            tv = p.ck("vector")
            p.wait("scalar", tv)
            p.on("scalar", lambda sx: sx.activation(tB[:, 1024:2048], tB[:, 0:1024], AF.Silu))
            tsl = p.ck("scalar")
            p.wait("vector", tsl)
            p.wait("vector", tok_xx)
            p.wait("vector", tok_ss)
            p.on("vector", lambda v: v.tensor_mul(tB[:, 1024:2048], tB[:, 1024:2048], tC[:, 1024:2048]))
            p.on("vector", lambda v: v.tensor_add(tC[:, 0:1024], tC[:, 0:1024], tB[:, 1024:2048]))
            tcb = p.ck("vector")
            p.wait("gpsimd", tcb)
            tok_st = p.dma("gpsimd", d_x[ts(tt), :], tC[:, 0:1024])
            p.wait("sync", tok_st)
            p.wait("scalar", tcb)
        p.barrier()

    L2_D4 = [0.4829629131445341, 0.8365163037378079, 0.2241438680420134, -0.1294095225512604]
    L2_OFFS = sorted({(1 << j) * tau for j in range(11) for tau in range(4) if (1 << j) * tau < S})
    L2_TERMS = {d_: [(j, tau) for j in range(11) for tau in range(1, 4) if (1 << j) * tau == d_]
                for d_ in L2_OFFS if d_ > 0}

    def proj_fm_xnT(w_inmajor, col0, bias_dram, dst_fm, dout=1024, evac_act=AF.Identity):
        if bias_dram is not None:
            tb = p.dma("sync", sc1[:, 0:dout // 128],
                       bias_dram.rearrange("(c p) -> p c", p=128), allow_slow_non_contiguous=True)
            p.wait("scalar", tb)
        prev_te = None
        for oc in range(dout // 128):
            bufw = wa if oc % 2 == 0 else wb
            if prev_te is not None:
                p.wait("sync", prev_te)
            tokw = p.dma("sync", bufw.ap()[:, 0:1024].rearrange("p (c o) -> p c o", c=8),
                         w_inmajor.rearrange("(c r) o -> r c o", r=128)[:, :, col0 + oc * 128:col0 + oc * 128 + 128])
            p.wait("tensor", tokw)
            for qs in range(4):
                bi = qs % 2
                p.wait("tensor", bank_guard.get(bi))
                for c in range(NCH):
                    p.on("tensor", lambda te, c=c, qs=qs, bi=bi, bufw=bufw:
                         te.matmul(ps[bi][:, 0:512], bufw.ap()[:, c * 128:c * 128 + 128],
                                   xnT[:, c * S + qs * 512:c * S + qs * 512 + 512],
                                   start=(c == 0), stop=(c == NCH - 1)))
                tmm = p.ck("tensor")
                prev_te = tmm
                p.wait("scalar", tmm)
                if bias_dram is not None:
                    p.on("scalar", lambda sx, bi=bi, oc=oc, evac_act=evac_act: sx.activation(
                        tB[:, 0:512], ps[bi][:, 0:512], evac_act, bias=sc1[:, oc:oc + 1], scale=1.0))
                else:
                    p.on("scalar", lambda sx, bi=bi, evac_act=evac_act: sx.activation(
                        tB[:, 0:512], ps[bi][:, 0:512], evac_act))
                tev = p.ck("scalar")
                bank_guard[bi] = tev
                p.wait("gpsimd", tev)
                tok_st = p.dma("gpsimd", dst_fm[oc * 128:oc * 128 + 128, qs * 512:qs * 512 + 512], tB[:, 0:512])
                p.wait("scalar", tok_st)
        p.barrier()

    def proj_tm_xnT(w_inmajor, col0, bias_dram, dst_tm, dout=1024, evac_act=None,
                    tile_hook=None):
        if bias_dram is not None:
            tb = p.dma("sync", brow[0:1, 0:dout], bias_dram.unsqueeze(0)[0:1, col0:col0 + dout]
                       if False else bias_dram.unsqueeze(0)[0:1, 0:dout])
            p.wait("tensor", tb)
        nos = dout // 512
        wtoks = []
        for os_ in range(nos):
            buf = wa if os_ == 0 else wb
            wtoks.append(p.dma("sync", buf.ap()[:, 0:4096].rearrange("p (c o) -> p c o", c=8),
                               w_inmajor.rearrange("(c r) o -> r c o", r=128)[
                                   :, :, col0 + os_ * 512:col0 + os_ * 512 + 512]))
        p.wait("tensor", wtoks)
        for tt in range(NT):
            evs = []
            for os_ in range(nos):
                bi = (tt % 2) * 2 + os_
                buf = wa if os_ == 0 else wb
                p.wait("tensor", bank_guard.get(bi))
                for c in range(NCH):
                    p.on("tensor", lambda te, c=c, tt=tt, bi=bi, buf=buf:
                         te.matmul(ps[bi][:, 0:512], xnT[:, c * S + tt * 128:c * S + tt * 128 + 128],
                                   buf.ap()[:, c * 512:c * 512 + 512], start=(c == 0), stop=False))
                if bias_dram is not None:
                    p.on("tensor", lambda te, bi=bi, os_=os_: te.matmul(
                        ps[bi][:, 0:512], ones_row[0:1, 0:128],
                        brow[0:1, os_ * 512:os_ * 512 + 512], start=False, stop=True))
                else:
                    p.on("tensor", lambda te, bi=bi, os_=os_: te.matmul(
                        ps[bi][:, 0:512], ones_row[0:1, 0:128],
                        brow[0:1, os_ * 512:os_ * 512 + 512], start=False, stop=True))
                tmm = p.ck("tensor")
                eng = "vector" if evac_act is None else "scalar"
                p.wait(eng, tmm)
                if evac_act is None:
                    p.on("vector", lambda v, bi=bi, os_=os_: v.tensor_copy(tB[:, os_ * 512:os_ * 512 + 512], ps[bi][:, 0:512]))
                else:
                    p.on("scalar", lambda sx, bi=bi, os_=os_, evac_act=evac_act: sx.activation(
                        tB[:, os_ * 512:os_ * 512 + 512], ps[bi][:, 0:512], evac_act))
                tev = p.ck(eng)
                bank_guard[bi] = tev
                evs.append(tev)
            if tile_hook is not None:
                tev = tile_hook(tt, tB[:, 0:dout], evs)
                evs = [tev]
            p.wait("gpsimd", evs)
            tok_st = p.dma("gpsimd", dst_tm[ts(tt), 0:dout], tB[:, 0:dout])
            p.wait("scalar", tok_st)
            p.wait("vector", tok_st)
        p.barrier()

    def l2_attn():
        proj_fm_xnT(d_w["l2_qkv"], 0, din["L2_attn_b_qkv"][0:1024], d_q2)
        proj_fm_xnT(d_w["l2_qkv"], 1024, din["L2_attn_b_qkv"][1024:2048], d_k2)
        proj_tm_xnT(d_w["l2_qkv"], 2048, din["L2_attn_b_qkv"][2048:3072], d_v2)
        p.on("gpsimd", lambda g: g.iota(tA[:, 0:16], [[1, 16]], base=0, channel_multiplier=0,
                                        allow_small_or_imprecise_dtypes=True))
        p.on("gpsimd", lambda g: g.iota(sc3[:, 0:1], [[0, 1]], base=0, channel_multiplier=1,
                                        allow_small_or_imprecise_dtypes=True))
        tgp = p.ck("gpsimd")
        p.wait("vector", tgp)
        p.on("vector", lambda v: v.tensor_scalar(sc3[:, 1:2], sc3[:, 0:1], 63.5, None, OP.is_gt))
        p.ckw("vector")
        for c in range(8):
            p.on("vector", lambda v, c=c: v.tensor_scalar(sc3[:, 2:3], sc3[:, 1:2], float(2 * c), None, OP.add))
            p.ckw("vector")
            p.on("vector", lambda v, c=c: v.tensor_scalar(tF[:, c * 16:c * 16 + 16], tA[:, 0:16],
                                                          sc3[:, 2:3], None, OP.is_equal))
            tE16 = p.ckw("vector")
        p.wait("tensor", tE16)
        tpr = p.dma("sync", sc2[0:16, 0:11], din["L2_attn_scale_gain"].rearrange("j h -> h j"),
                    allow_slow_non_contiguous=True)
        tqs = p.dma("sync", tA[0:64, 32:43], d_w["l2_qscale"][0:64, 0:11])
        tqs2 = p.dma("sync", tA[64:128, 32:43], d_w["l2_qscale"][0:64, 0:11])
        p.wait("vector", tqs2)
        p.wait("vector", tpr)
        tib = p.dma("sync", sc2[0:16, 11:12], din["L2_attn_identity_bypass"].unsqueeze(1))
        p.wait("scalar", tib)
        p.on("scalar", lambda sx: sx.activation(sc2[0:16, 12:13], sc2[0:16, 11:12], AF.Exp))
        tso = p.ck("scalar")
        p.wait("vector", tso)
        p.on("vector", lambda v: v.tensor_scalar(sc2[0:16, 12:13], sc2[0:16, 12:13], 1.0, None, OP.add))
        tso2 = p.ck("vector")
        p.wait("scalar", tso2)
        p.on("scalar", lambda sx: sx.activation(sc2[0:16, 12:13], sc2[0:16, 12:13], AF.Ln))
        tso = p.ck("scalar")
        p.wait("vector", tso)
        qkbuf = tE
        for qs in range(4):
            p.wait("sync", ("vector", p.cnt["vector"]))
            tokq = p.dma("sync", wa[0:128, 0:4096],
                         d_q2.rearrange("(c r) n -> r c n", r=128)[:, :, qs * 512:qs * 512 + 512])
            p.wait("vector", tokq)
            for j in range(11):
                bi = 4 + (j % 2)
                p.wait("tensor", bank_guard.get(bi))
                tvp = None
                for c in range(NCH):
                    p.on("vector", lambda v, c=c, j=j: v.tensor_scalar(
                        tB[:, (c % 2) * 512:(c % 2) * 512 + 512], wa[0:128, c * 512:c * 512 + 512],
                        tA[:, 32 + j:33 + j], None, OP.mult))
                    tvp = p.ck("vector")
                    p.wait("tensor", tvp)
                    p.on("tensor", lambda te, c=c, j=j, bi=bi: te.matmul(
                        ps[bi][0:16, 0:512], tF[:, c * 16:c * 16 + 16],
                        tB[:, (c % 2) * 512:(c % 2) * 512 + 512], start=(c == 0), stop=(c == NCH - 1)))
                    if c < NCH - 1:
                        tmm_g = p.ck("tensor")
                        p.wait("vector", tmm_g)
                tmm = p.ck("tensor")
                p.wait("scalar", tmm)
                p.on("scalar", lambda sx, j=j, bi=bi: sx.activation(
                    ti32.ap().bitcast(F32)[0:16, 0:2048][0:16, 0:512] if False else
                    qg[0:16, j * 512:j * 512 + 512], ps[bi][0:16, 0:512], AF.Identity,
                    bias=sc2[0:16, j:j + 1], scale=1.0))
                tev = p.ck("scalar")
                bank_guard[bi] = tev
            p.wait("vector", tev)
            p.wait("scalar", tev)
            p.on("scalar", lambda sx: sx.activation(qg[0:16, 0:5632], qg[0:16, 0:5632], AF.Exp))
            tse = p.ck("scalar")
            p.wait("vector", tse)
            p.on("vector", lambda v: v.tensor_copy(tB[0:16, 0:512], qg[0:16, 0:512]))
            for j in range(1, 11):
                p.on("vector", lambda v, j=j: v.tensor_add(tB[0:16, 0:512], tB[0:16, 0:512],
                                                           qg[0:16, j * 512:j * 512 + 512]))
            p.on("vector", lambda v: v.reciprocal(tB[0:16, 512:1024], tB[0:16, 0:512]))
            for j in range(11):
                p.on("vector", lambda v, j=j: v.tensor_mul(qg[0:16, j * 512:j * 512 + 512],
                                                           qg[0:16, j * 512:j * 512 + 512], tB[0:16, 512:1024]))
            tgn = p.ck("vector")
            p.wait("gpsimd", tgn)
            for j in range(11):
                p.dma("gpsimd", d_gains[0:16, j * S + qs * 512:j * S + qs * 512 + 512],
                      qg[0:16, j * 512:j * 512 + 512])
            for di, d_ in enumerate(L2_OFFS):
                lo = qs * 512 - d_
                tokk = None
                if lo + 512 > 0:
                    a0 = max(0, lo)
                    p.wait("sync", ("vector", p.cnt["vector"]))
                    tokk = p.dma("sync", wb.ap()[:, 0:4096].rearrange("p (c n) -> p c n", c=8)[
                        :, :, (a0 - lo):512],
                        d_k2.rearrange("(c r) n -> r c n", r=128)[:, :, a0:lo + 512])
                if lo < 0:
                    zer = min(-lo, 512)
                    p.on("vector", lambda v, zer=zer: v.memset(
                        wb.ap()[:, 0:4096].rearrange("p (c n) -> p c n", c=8)[:, :, 0:zer], 0.0))
                tz = p.ck("vector") if lo < 0 else None
                p.wait("vector", tokk)
                bi = 4 + (di % 2)
                p.wait("tensor", bank_guard.get(bi))
                p.wait("tensor", tz)
                tvp = None
                for c in range(NCH):
                    p.on("vector", lambda v, c=c: v.tensor_mul(
                        tB[:, (c % 2) * 512:(c % 2) * 512 + 512], wa[0:128, c * 512:c * 512 + 512],
                        wb[0:128, c * 512:c * 512 + 512]))
                    tvp = p.ck("vector")
                    p.wait("tensor", tvp)
                    p.on("tensor", lambda te, c=c, bi=bi: te.matmul(
                        ps[bi][0:16, 0:512], tF[:, c * 16:c * 16 + 16],
                        tB[:, (c % 2) * 512:(c % 2) * 512 + 512], start=(c == 0), stop=(c == NCH - 1)))
                    if c < NCH - 1:
                        tmm_g = p.ck("tensor")
                        p.wait("vector", tmm_g)
                tmm = p.ck("tensor")
                p.wait("vector", tmm)
                p.on("vector", lambda v, bi=bi: v.tensor_copy(tB[0:16, 1024:1536], ps[bi][0:16, 0:512]))
                tqk = p.ck("vector")
                bank_guard[bi] = tqk
                p.wait("vector", ("tensor", p.cnt["tensor"]))
                if d_ == 0:
                    p.on("vector", lambda v: v.tensor_scalar(sc3[0:16, 3:4], sc2[0:16, 12:13],
                                                             L2_D4[0], None, OP.add))
                    p.ckw("vector")
                    p.on("vector", lambda v: v.tensor_scalar(tB[0:16, 1536:2048], tB[0:16, 1024:1536],
                                                             sc3[0:16, 3:4], None, OP.mult))
                else:
                    terms = L2_TERMS[d_]
                    j0, tau0 = terms[0]
                    p.on("vector", lambda v, j0=j0, tau0=tau0: v.tensor_scalar(
                        tB[0:16, 512:1024], qg[0:16, j0 * 512:j0 * 512 + 512], L2_D4[tau0], None, OP.mult))
                    for (j1, tau1) in terms[1:]:
                        p.on("vector", lambda v, j1=j1, tau1=tau1: v.scalar_tensor_tensor(
                            tB[0:16, 512:1024], qg[0:16, j1 * 512:j1 * 512 + 512], L2_D4[tau1],
                            tB[0:16, 512:1024], OP.mult, OP.add))
                    p.on("vector", lambda v: v.tensor_mul(tB[0:16, 1536:2048], tB[0:16, 1024:1536],
                                                          tB[0:16, 512:1024]))
                twd = p.ck("vector")
                p.wait("tensor", twd)
                for t4 in range(4):
                    bi2 = 6 + (t4 % 2)
                    p.wait("tensor", bank_guard.get(bi2))
                    p.on("tensor", lambda te, t4=t4, bi2=bi2: te.transpose(
                        ps[bi2][0:128, 0:16], tB[0:16, 1536 + t4 * 128:1536 + t4 * 128 + 128],
                        ident[0:16, 0:16]))
                    tmm2 = p.ck("tensor")
                    p.wait("vector", tmm2)
                    p.on("vector", lambda v, t4=t4, bi2=bi2, di=di: v.tensor_copy(
                        tC[:, di * 64 + t4 * 16:di * 64 + t4 * 16 + 16], ps[bi2][0:128, 0:16]))
                    bank_guard[bi2] = p.ck("vector")
            p.wait("gpsimd", bank_guard.get(6))
            p.wait("gpsimd", bank_guard.get(7))
            tok_sds = []
            for t4 in range(4):
                tok_sd = p.dma("gpsimd",
                               d_sd[qs * 512 + t4 * 128:qs * 512 + t4 * 128 + 128, 0:352].rearrange(
                                   "p (dd c) -> p dd c", dd=22),
                               tC.ap()[:, 0:22 * 64].rearrange("p (dd tc) -> p dd tc", dd=22)[
                                   :, :, t4 * 16:t4 * 16 + 16])
                tok_sds.append(tok_sd)
            p.wait("vector", tok_sds)
            p.wait("sync", tok_sds)
        p.barrier()
        for tt in range(NT):
            tok_s = p.dma("sync", tC[:, 0:352], d_sd[ts(tt), 0:352])
            p.wait("vector", tok_s)
            first = True
            for di, d_ in enumerate(L2_OFFS):
                lo = tt * 128 - d_
                if lo + 128 <= 0:
                    continue
                a0 = max(0, lo)
                tok_v = p.dma("sync", tB[(a0 - lo):128, 0:1024], d_v2[a0:lo + 128, 0:1024])
                p.wait("vector", tok_v)
                if lo < 0:
                    p.on("vector", lambda v, lo=lo: v.memset(tB[0:-lo, 0:1024], 0.0))
                sbc = tC.ap()[:, di * 16:di * 16 + 16] if False else None
                if first:
                    p.on("vector", lambda v, di=di: v.tensor_tensor(
                        tD.ap()[:, 0:1024].rearrange("p (h e) -> p h e", h=16),
                        tB.ap()[:, 0:1024].rearrange("p (h e) -> p h e", h=16),
                        tC.ap()[:, di * 16:di * 16 + 16].unsqueeze(2).to_broadcast([128, 16, 64]),
                        OP.mult))
                else:
                    p.on("vector", lambda v, di=di: v.tensor_mul(
                        tE.ap()[:, 0:1024].rearrange("p (h e) -> p h e", h=16),
                        tB.ap()[:, 0:1024].rearrange("p (h e) -> p h e", h=16),
                        tC.ap()[:, di * 16:di * 16 + 16].unsqueeze(2).to_broadcast([128, 16, 64])))
                    p.on("vector", lambda v: v.tensor_add(tD[:, 0:1024], tD[:, 0:1024], tE[:, 0:1024]))
                first = False
                tacc = p.ck("vector")
                p.wait("sync", tacc)
            p.wait("gpsimd", tacc)
            tok_g = p.dma("gpsimd", d_t1[ts(tt), 0:1024], tD[:, 0:1024])
            p.wait("vector", tok_g)
        p.barrier()

        def l2_gate_hook(tt, tile_ap, evs):
            p.wait("sync", ("tensor", p.cnt["tensor"]))
            tokg = p.dma("sync", tD[:, 0:1024], d_t1[ts(tt), 0:1024])
            p.wait("vector", tokg)
            p.wait("vector", evs)
            p.on("vector", lambda v, tile_ap=tile_ap: v.tensor_mul(tD[:, 0:1024], tD[:, 0:1024], tile_ap))
            tu = p.ck("vector")
            p.wait("tensor", tu)
            for c in range(NCH):
                bi = 6 + (c % 2)
                p.wait("tensor", bank_guard.get(bi))
                p.on("tensor", lambda te, c=c, bi=bi: te.transpose(
                    ps[bi][0:128, 0:128], tD[:, ts(c)], ident[:, :]))
                tmm = p.ck("tensor")
                p.wait("vector", tmm)
                p.on("vector", lambda v, c=c, bi=bi: v.tensor_copy(tE[:, c * 128:c * 128 + 128], ps[bi][0:128, 0:128]))
                bank_guard[bi] = p.ck("vector")
            tev = bank_guard[6 + ((NCH - 1) % 2)]
            p.wait("gpsimd", tev)
            tok_u = p.dma("gpsimd", d_comb[0:1024, ts(tt)].rearrange("(c q) t -> q c t", q=128),
                          tE.ap()[:, 0:1024].rearrange("q (c t) -> q c t", c=8))
            p.wait("vector", tok_u)
            return tev

        proj_tm_xnT(d_w["l2_gate"], 0, din["L2_attn_b_gate"], d_t2, dout=1024,
                    evac_act=AF.Sigmoid, tile_hook=l2_gate_hook)
        if l0_stop == "l2gate":
            return True
        proj_from_fm_dram(d_comb, d_w["l2_out"], din["L2_attn_b_out"], d_s, nch_in=8)

    def l2_ffn_and_combine():
        _stop = l0_stop
        tokw = p.dma("sync", wa[0:128, 0:64].rearrange("p (c o) -> p c o", c=8),
                     d_w["l2f_gw"].rearrange("(c r) o -> r c o", r=128))
        p.wait("tensor", tokw)
        for tt in range(NT):
            bi = 4 + (tt % 2)
            p.wait("tensor", bank_guard.get(bi))
            for c in range(NCH):
                p.on("tensor", lambda te, c=c, tt=tt, bi=bi: te.matmul(
                    ps[bi][0:128, 0:8], xnT[:, c * S + tt * 128:c * S + tt * 128 + 128],
                    wa.ap()[:, c * 8:c * 8 + 8], start=(c == 0), stop=(c == NCH - 1)))
            tmm = p.ck("tensor")
            p.wait("vector", tmm)
            p.on("vector", lambda v, bi=bi: v.tensor_copy(tA[:, 0:8], ps[bi][0:128, 0:8]))
            p.ckw("vector")
            p.on("vector", lambda v: v.reduce_max(sc3[:, 0:1], tA[:, 0:8], axis=AX.X))
            p.ckw("vector")
            p.on("vector", lambda v: v.tensor_scalar(sc3[:, 1:2], sc3[:, 0:1], -1.0, None, OP.mult))
            tvx = p.ck("vector")
            bank_guard[bi] = tvx
            p.wait("scalar", tvx)
            p.wait("scalar", ("tensor", p.cnt["tensor"]))
            p.on("scalar", lambda sx: sx.activation(tA[:, 8:16], tA[:, 0:8], AF.Exp,
                                                    bias=sc3[:, 1:2], scale=1.0, accum_out=accA[:, 0:1]))
            tse = p.ck("scalar")
            p.wait("vector", tse)
            p.on("vector", lambda v: v.reciprocal(sc3[:, 3:4], accA[:, 0:1]))
            p.ckw("vector")
            p.on("vector", lambda v: v.tensor_scalar(tA[:, 8:16], tA[:, 8:16], sc3[:, 3:4], None, OP.mult))
            tg8 = p.ckw("vector")
            p.wait("tensor", tg8)
            bi2 = 6 + (tt % 2)
            p.wait("tensor", bank_guard.get(bi2))
            p.on("tensor", lambda te, bi2=bi2: te.transpose(ps[bi2][0:8, 0:128], tA[:, 8:16], ident[:, :]))
            tmm2 = p.ck("tensor")
            p.wait("vector", tmm2)
            p.on("vector", lambda v, tt=tt, bi2=bi2: v.tensor_copy(qg[0:8, tt * 128:tt * 128 + 128],
                                                                   ps[bi2][0:8, 0:128]))
            bank_guard[bi2] = p.ck("vector")
            if l0_stop == "l2g8" and tt == 0:
                return True
        tglast = bank_guard[6 + ((NT - 1) % 2)]
        p.on("gpsimd", lambda g: g.iota(tB[0:8, 0:128], [[1, 128]], base=0, channel_multiplier=0,
                                        allow_small_or_imprecise_dtypes=True))
        p.on("gpsimd", lambda g: g.iota(tB[0:8, 128:129], [[0, 1]], base=0, channel_multiplier=16,
                                        allow_small_or_imprecise_dtypes=True))
        tgp = p.ck("gpsimd")
        p.wait("vector", tgp)
        p.on("vector", lambda v: v.tensor_scalar(tB[0:8, 256:384], tB[0:8, 0:128], tB[0:8, 128:129], None, OP.is_ge))
        p.ckw("vector")
        p.on("vector", lambda v: v.tensor_scalar(tB[0:8, 128:129], tB[0:8, 128:129], 16.0, None, OP.add))
        p.ckw("vector")
        p.on("vector", lambda v: v.tensor_scalar(tB[0:8, 384:512], tB[0:8, 0:128], tB[0:8, 128:129], None, OP.is_lt))
        p.ckw("vector")
        p.on("vector", lambda v: v.tensor_mul(tB[0:8, 0:128], tB[0:8, 256:384], tB[0:8, 384:512]))
        p.ckw("vector")
        p.on("vector", lambda v: v.tensor_scalar(tB[0:8, 0:128], tB[0:8, 0:128], 1.0 / 16.0, None, OP.mult))
        tE8 = p.ckw("vector")
        p.wait("tensor", tE8)
        p.wait("tensor", tglast)
        for qs in range(4):
            bi = 4 + (qs % 2)
            p.wait("tensor", bank_guard.get(bi))
            p.on("tensor", lambda te, qs=qs, bi=bi: te.matmul(
                ps[bi][0:128, 0:512], tB[0:8, 0:128], qg[0:8, qs * 512:qs * 512 + 512],
                start=True, stop=True))
            tmm = p.ck("tensor")
            p.wait("vector", tmm)
            p.on("vector", lambda v, qs=qs, bi=bi: v.tensor_copy(tD[:, qs * 512:qs * 512 + 512],
                                                                 ps[bi][0:128, 0:512]))
            bank_guard[bi] = p.ck("vector")
        p.wait("sync", ("tensor", p.cnt["tensor"]))
        tokU = p.dma("sync", wa[0:128, 0:1024].rearrange("p (c o) -> p c o", c=8),
                     d_w["l2f_U"].rearrange("(c r) o -> r c o", r=128))
        p.wait("tensor", tokU)
        for qs in range(4):
            bi = qs % 2
            p.wait("tensor", bank_guard.get(bi))
            for c in range(NCH):
                p.on("tensor", lambda te, c=c, qs=qs, bi=bi: te.matmul(
                    ps[bi][:, 0:512], wa.ap()[:, c * 128:c * 128 + 128],
                    xnT[:, c * S + qs * 512:c * S + qs * 512 + 512], start=(c == 0), stop=(c == NCH - 1)))
            tmm = p.ck("tensor")
            p.wait("vector", tmm)
            p.on("vector", lambda v, qs=qs, bi=bi: v.tensor_mul(
                tE[:, qs * 512:qs * 512 + 512], ps[bi][:, 0:512], tD[:, qs * 512:qs * 512 + 512]))
            bank_guard[bi] = p.ck("vector")
        tlast = bank_guard[1]
        tokV = p.dma("sync", wb[0:128, 0:1024], d_w["l2f_V"][0:128, 0:1024])
        p.wait("tensor", tokV)
        p.wait("tensor", tlast)
        p.wait("sync", tlast)
        for os_ in range(2):
            tokw = p.dma("sync", wa.ap()[:, 0:4096].rearrange("p (c o) -> p c o", c=8),
                         d_w["l2f_w"].rearrange("(c r) o -> r c o", r=128)[:, :, os_ * 512:os_ * 512 + 512])
            p.wait("tensor", tokw)
            last_tmm = None
            for tt in range(NT):
                bi = 4 + (tt % 2) * 2 + os_ % 2
                p.wait("tensor", bank_guard.get(bi))
                for c in range(NCH):
                    p.on("tensor", lambda te, c=c, tt=tt, bi=bi: te.matmul(
                        ps[bi][:, 0:512], xnT[:, c * S + tt * 128:c * S + tt * 128 + 128],
                        wa.ap()[:, c * 512:c * 512 + 512], start=(c == 0), stop=False))
                p.on("tensor", lambda te, tt=tt, bi=bi, os_=os_: te.matmul(
                    ps[bi][:, 0:512], tE[:, tt * 128:tt * 128 + 128],
                    wb.ap()[:, os_ * 512:os_ * 512 + 512], start=False, stop=True))
                tmm = p.ck("tensor")
                last_tmm = tmm
                p.wait("scalar", tmm)
                p.on("scalar", lambda sx, bi=bi: sx.activation(tC[:, 0:512], ps[bi][:, 0:512], AF.Silu))
                tev = p.ck("scalar")
                bank_guard[bi] = tev
                p.wait("gpsimd", tev)
                tok_st = p.dma("gpsimd", d_t2[ts(tt), os_ * 512:os_ * 512 + 512], tC[:, 0:512])
                p.wait("scalar", tok_st)
            p.wait("sync", last_tmm)
        p.barrier()
        for tt in range(NT):
            tokg = p.dma("sync", tB[:, 0:1024], d_t2[ts(tt), 0:1024])
            tok_xx = p.dma("sync", tC[:, 0:1024], d_x[ts(tt), :])
            tok_ss = p.dma("sync", tC[:, 1024:2048], d_s[ts(tt), :])
            p.wait("vector", tokg)
            p.wait("vector", tok_xx)
            p.wait("vector", tok_ss)
            p.on("vector", lambda v: v.tensor_mul(tB[:, 1024:2048], tB[:, 0:1024], tC[:, 1024:2048]))
            p.on("vector", lambda v: v.tensor_add(tC[:, 0:1024], tC[:, 0:1024], tB[:, 1024:2048]))
            tcb = p.ck("vector")
            p.wait("gpsimd", tcb)
            tok_st = p.dma("gpsimd", d_x[ts(tt), :], tC[:, 0:1024])
            p.wait("sync", tok_st)
        p.barrier()

    def layer2():
        emit_ln_and_transpose(store_xn_dram=True)
        if l2_attn():
            return
        if l2_ffn_and_combine():
            return

    FQ_LN2_HI = LN2_HI
    FQ_LN2_LO = LN2_LO
    INV_LN2 = 1.4426950408889634

    def emit_float_quantize(dst, src, n, exp_bits, man, rows=128):
        bias = 2 ** (exp_bits - 1) - 1
        min_exp = 1 - bias
        maxv = (2.0 - 2.0 ** (-man)) * 2.0 ** bias
        r_ = slice(0, rows)
        p.on("scalar", lambda sx, src=src: sx.activation(tB[r_, 0:n], src, AF.Abs))
        ts1 = p.ck("scalar")
        p.wait("vector", ts1)
        p.on("vector", lambda v: v.tensor_scalar(tB[r_, 0:n], tB[r_, 0:n], 1e-30, None, OP.max))
        tv1 = p.ck("vector")
        p.wait("scalar", tv1)
        p.on("scalar", lambda sx: sx.activation(tB[r_, 0:n], tB[r_, 0:n], AF.Ln))
        ts2 = p.ck("scalar")
        p.wait("vector", ts2)
        p.on("vector", lambda v: v.tensor_scalar(tB[r_, 0:n], tB[r_, 0:n], INV_LN2, None, OP.mult))
        p.on("vector", lambda v: v.tensor_copy(ti32[r_, 0:n], tB[r_, 0:n]))
        p.on("vector", lambda v: v.tensor_copy(tD[r_, 0:n], ti32[r_, 0:n]))
        p.on("vector", lambda v: v.tensor_sub(tB[r_, 0:n], tB[r_, 0:n], tD[r_, 0:n]))
        p.on("vector", lambda v: v.tensor_scalar(ti32[r_, 0:n].bitcast(F32), tB[r_, 0:n], 0.0, None, OP.is_lt))
        p.on("vector", lambda v: v.tensor_scalar(tB[r_, 0:n], tB[r_, 0:n], 1.0, None, OP.is_ge))
        p.on("vector", lambda v: v.tensor_sub(tD[r_, 0:n], tD[r_, 0:n], ti32[r_, 0:n].bitcast(F32)))
        p.on("vector", lambda v: v.tensor_add(tD[r_, 0:n], tD[r_, 0:n], tB[r_, 0:n]))
        p.on("vector", lambda v: v.tensor_scalar(tD[r_, 0:n], tD[r_, 0:n], float(min_exp), None, OP.max))
        p.on("vector", lambda v: v.tensor_scalar(tD[r_, 0:n], tD[r_, 0:n], -1.0, float(man), OP.mult, OP.add))
        tu = p.ck("vector")
        p.wait("scalar", tu)
        p.on("scalar", lambda sx: sx.activation(tB[r_, 0:n], tD[r_, 0:n], AF.Exp, scale=FQ_LN2_HI))
        ts3 = p.ck("scalar")
        p.wait("vector", ts3)
        p.on("vector", lambda v, src=src: v.tensor_mul(tB[r_, 0:n], tB[r_, 0:n], src))
        tv2 = p.ck("vector")
        p.wait("scalar", tv2)
        p.on("scalar", lambda sx: sx.activation(tE[r_, 0:n], tD[r_, 0:n], AF.Exp, scale=FQ_LN2_LO))
        ts4 = p.ck("scalar")
        p.wait("vector", ts4)
        p.on("vector", lambda v: v.tensor_mul(tB[r_, 0:n], tB[r_, 0:n], tE[r_, 0:n]))
        p.on("vector", lambda v: v.tensor_copy(ti32[r_, 0:n], tB[r_, 0:n]))
        p.on("vector", lambda v: v.tensor_copy(tE[r_, 0:n], ti32[r_, 0:n]))
        p.on("vector", lambda v: v.tensor_sub(tB[r_, 0:n], tB[r_, 0:n], tE[r_, 0:n]))
        p.on("vector", lambda v: v.tensor_scalar(ti32[r_, 0:n].bitcast(F32), tB[r_, 0:n], 0.5, None, OP.is_ge))
        p.on("vector", lambda v: v.tensor_scalar(tB[r_, 0:n], tB[r_, 0:n], -0.5, None, OP.is_lt))
        p.on("vector", lambda v: v.tensor_sub(tE[r_, 0:n], tE[r_, 0:n], tB[r_, 0:n]))
        p.on("vector", lambda v: v.tensor_add(tB[r_, 0:n], tE[r_, 0:n], ti32[r_, 0:n].bitcast(F32)))
        tv3 = p.ck("vector")
        p.wait("scalar", tv3)
        p.on("scalar", lambda sx: sx.activation(tE[r_, 0:n], tD[r_, 0:n], AF.Exp, scale=-FQ_LN2_HI))
        ts5 = p.ck("scalar")
        p.wait("vector", ts5)
        p.on("vector", lambda v: v.tensor_mul(tB[r_, 0:n], tB[r_, 0:n], tE[r_, 0:n]))
        tv4 = p.ck("vector")
        p.wait("scalar", tv4)
        p.on("scalar", lambda sx: sx.activation(tE[r_, 0:n], tD[r_, 0:n], AF.Exp, scale=-FQ_LN2_LO))
        ts6 = p.ck("scalar")
        p.wait("vector", ts6)
        p.on("vector", lambda v: v.tensor_mul(tB[r_, 0:n], tB[r_, 0:n], tE[r_, 0:n]))
        tvq = p.ck("vector")
        p.wait("scalar", tvq)
        p.on("scalar", lambda sx: sx.activation(tE[r_, 0:n], tB[r_, 0:n], AF.Abs))
        tab = p.ck("scalar")
        p.wait("vector", tab)
        p.on("vector", lambda v, maxv=maxv, n=n: v.tensor_scalar(tE[r_, 0:n], tE[r_, 0:n], maxv, None, OP.is_gt))
        tov = p.ck("vector")
        p.wait("scalar", tov)
        p.on("scalar", lambda sx: sx.activation(tE[r_, 0:n], tE[r_, 0:n], AF.Exp, scale=1e5))
        tex = p.ck("scalar")
        p.wait("vector", tex)
        p.on("vector", lambda v, dst=dst, n=n: v.tensor_mul(dst, tB[r_, 0:n], tE[r_, 0:n]))
        return p.ck("vector")

    def l3_attn():
        proj_tm_xnT(din["L3_attn_Wq"], 0, din["L3_attn_bq"], d_q2_tm)
        proj_tm_xnT(din["L3_attn_Wk"], 0, din["L3_attn_bk"], d_k2_tm)
        proj_tm_xnT(din["L3_attn_Wv"], 0, din["L3_attn_bv"], d_v2)
        V2 = tF
        for gp in range(8):
            g0 = 2 * gp
            toks = []
            for (dstT, srcD) in ((tD, d_q2_tm), (tC, d_k2_tm)):
                for hh in range(2):
                    g = g0 + hh
                    src = srcD[128 * g:128 * g + 128, :].rearrange("(c two) d -> c two d", two=2)
                    toks.append(p.dma("sync", dstT[64 * hh:64 * hh + 64, 0:1024], src[:, 0, :]))
                    toks.append(p.dma("sync", dstT[64 * hh:64 * hh + 64, 1024:2048], src[:, 1, :]))
            p.wait("tensor", toks)
            for hh in range(2):
                g = g0 + hh
                srcv = d_v2[128 * g:128 * g + 128, :].rearrange("(c two) d -> c two d", two=2)
                tv1 = p.dma("sync", tE[0:64, 0:1024], srcv[:, 0, :])
                tv2 = p.dma("sync", tE[0:64, 1024:2048], srcv[:, 1, :])
                p.wait("tensor", [tv1, tv2])
                voff = 66 * hh
                for kc in range(NT):
                    bi = 6 + (kc % 2)
                    p.wait("tensor", bank_guard.get(bi))
                    p.on("tensor", lambda te, kc=kc, bi=bi: te.transpose(
                        ps[bi][0:128, 0:64], tE[0:64, kc * 128:kc * 128 + 128], ident[0:64, 0:64]))
                    tmm = p.ck("tensor")
                    p.wait("vector", tmm)
                    p.on("vector", lambda v, kc=kc, bi=bi, voff=voff: v.tensor_copy(
                        V2[:, kc * 132 + voff:kc * 132 + voff + 64], ps[bi][0:128, 0:64]))
                    bank_guard[bi] = p.ck("vector")
                p.wait("sync", bank_guard[6 + ((NT - 1) % 2)])
            V2r = V2.ap().rearrange("p (t o) -> p t o", t=NT)
            p.on("vector", lambda v: v.memset(V2r[:, :, 64:65], 1.0))
            p.on("vector", lambda v: v.memset(V2r[:, :, 130:131], 1.0))
            tones = p.ck("vector")
            p.wait("tensor", tones)
            prev_exp_last = None
            prev_av_last = None
            for hh in range(2):
                qoff = 64 * hh
                voff = 66 * hh
                for qs in range(4):
                    nkc = NT
                    av_bank = ps[2 + hh]
                    exp_ready = {}
                    av_done = {}
                    for kc in range(nkc):
                        sb_bank = ps[kc % 2]
                        if kc >= 2:
                            p.wait("tensor", exp_ready[kc - 2])
                        elif kc == 0 and prev_exp_last is not None:
                            p.wait("tensor", prev_exp_last)
                        p.on("tensor", lambda te, kc=kc, qs=qs, qoff=qoff, sb_bank=sb_bank:
                             te.matmul(sb_bank[:, 0:512],
                                       tC[qoff:qoff + 64, kc * 128:kc * 128 + 128],
                                       tD[qoff:qoff + 64, qs * 512:qs * 512 + 512],
                                       start=True, stop=True))
                        tsc = p.ck("tensor")
                        p.wait("scalar", tsc)
                        if kc >= 4:
                            p.wait("scalar", av_done[kc - 4])
                        elif kc == 0 and prev_av_last is not None:
                            p.wait("scalar", prev_av_last)
                        ebuf = expb[:, ts(kc % 4, 512)]
                        p.on("scalar", lambda sx, ebuf=ebuf, sb_bank=sb_bank:
                             sx.activation(ebuf, sb_bank[:, 0:512], AF.Exp, scale=1.0 / 1024.0))
                        texp = p.ck("scalar")
                        exp_ready[kc] = texp
                        p.wait("tensor", texp)
                        p.on("tensor", lambda te, kc=kc, voff=voff, av_bank=av_bank, ebuf=ebuf, nkc=nkc:
                             te.matmul(av_bank[0:65, 0:512], V2[:, kc * 132 + voff:kc * 132 + voff + 65],
                                       ebuf, start=(kc == 0), stop=(kc == nkc - 1)))
                        av_done[kc] = p.ck("tensor")
                    p.wait("vector", av_done[nkc - 1])
                    p.on("vector", lambda v, av_bank=av_bank: v.reciprocal(
                        rrow[0:1, 1024:1536], av_bank[64:65, 0:512]))
                    trc = p.ck("vector")
                    p.wait("tensor", trc)
                    p.on("tensor", lambda te: te.matmul(ps[4][0:64, 0:512], ones_row[0:1, 0:64],
                                                        rrow[0:1, 1024:1536], start=True, stop=True))
                    tbc = p.ck("tensor")
                    p.wait("vector", tbc)
                    p.on("vector", lambda v: v.tensor_copy(tE[0:64, 512:1024], ps[4][0:64, 0:512]))
                    p.on("vector", lambda v, av_bank=av_bank: v.tensor_mul(
                        tE[0:64, 0:512], av_bank[0:64, 0:512], tE[0:64, 512:1024]))
                    tn = p.ck("vector")
                    p.wait("gpsimd", tn)
                    tok_park = p.dma("gpsimd", d_av[64 * (g0 + hh):64 * (g0 + hh) + 64, qs * 512:qs * 512 + 512],
                                     tE[0:64, 0:512])
                    p.wait("vector", tok_park)
                    p.wait("tensor", tok_park)
                    prev_exp_last = exp_ready[nkc - 1]
                    prev_av_last = av_done[nkc - 1]
            p.barrier()
        for g in range(16):
            tok_a = p.dma("sync", tC[0:64, 0:2048], d_av[64 * g:64 * g + 64, :])
            p.wait("tensor", tok_a)
            for m in range(16):
                bi = 6 + (m % 2)
                p.wait("tensor", bank_guard.get(bi))
                src = tC.ap()[0:64, 0:2048].rearrange("c (j m) -> c m j", m=16)
                p.on("tensor", lambda te, m=m, bi=bi, src=src: te.transpose(
                    ps[bi][0:128, 0:64], src[:, m, :], ident[0:64, 0:64]))
                tmm = p.ck("tensor")
                p.wait("vector", tmm)
                p.on("vector", lambda v, m=m, bi=bi: v.tensor_copy(tD[:, 64 * m:64 * m + 64], ps[bi][0:128, 0:64]))
                bank_guard[bi] = p.ck("vector")
            p.wait("tensor", bank_guard[6])
            p.wait("tensor", bank_guard[7])
            p.wait("sync", bank_guard[7])
            for c in range(NCH):
                bi = 6 + (c % 2)
                p.wait("tensor", bank_guard.get(bi))
                p.on("tensor", lambda te, c=c, bi=bi: te.transpose(
                    ps[bi][0:128, 0:128], tD[:, ts(c)], ident[:, :]))
                tmm = p.ck("tensor")
                p.wait("vector", tmm)
                p.on("vector", lambda v, c=c, bi=bi: v.tensor_copy(tE[:, c * 128:c * 128 + 128], ps[bi][0:128, 0:128]))
                bank_guard[bi] = p.ck("vector")
            p.wait("gpsimd", bank_guard[6])
            p.wait("gpsimd", bank_guard[7])
            tok_u = p.dma("gpsimd", d_comb[0:1024, g * 128:g * 128 + 128].rearrange("(c q) t -> q c t", q=128),
                          tE.ap()[:, 0:1024].rearrange("q (c t) -> q c t", c=8))
            p.wait("vector", tok_u)
        p.barrier()
        proj_from_fm_dram(d_comb, din["L3_attn_Wo"], din["L3_attn_bo"], d_s, nch_in=8)

    def l3_ffn_and_combine():
        for ri in range(32):
            tok = p.dma("sync", wa[0:128, 0:1024], d_w["l3f_w2"][ts(ri), 0:1024])
            p.wait("vector", tok)
            p.wait("scalar", tok)
            tq = emit_float_quantize(tA[:, 0:1024], wa[0:128, 0:1024], 1024, 4, 3)
            p.wait("gpsimd", tq)
            tok_s = p.dma("gpsimd", d_wq2[ts(ri), 0:1024], tA[0:128, 0:1024])
            p.wait("vector", tok_s)
            p.wait("scalar", tok_s)
            p.wait("sync", tq)
        tokb = p.dma("sync", qg[0:1, 0:1024], din["L3_ffn_b2"].unsqueeze(0))
        p.wait("vector", tokb)
        p.wait("scalar", tokb)
        tq = emit_float_quantize(qg[0:1, 0:1024], qg[0:1, 0:1024], 1024, 5, 10, rows=1)
        p.wait("tensor", tq)
        tokb1 = p.dma("sync", b1col[:, 0:16], din["L3_ffn_b1"].rearrange("(c p) -> p c", p=128)[:, 0:16],
                      allow_slow_non_contiguous=True)
        tokb1b = p.dma("sync", convb[:, 0:16], din["L3_ffn_b1"].rearrange("(c p) -> p c", p=128)[:, 16:32],
                       allow_slow_non_contiguous=True)
        p.wait("scalar", [tokb1, tokb1b])
        for hc in range(32):
            bufw = wa if hc % 2 == 0 else wb
            p.wait("sync", ("tensor", p.cnt["tensor"]))
            tokw = p.dma("sync", bufw.ap()[:, 0:1024].rearrange("p (c o) -> p c o", c=8),
                         d_w["l3f_w1"].rearrange("(c r) o -> r c o", r=128)[:, :, ts(hc)])
            p.wait("tensor", tokw)
            tev = None
            for qs in range(4):
                bi = qs % 4
                p.wait("tensor", bank_guard.get(bi))
                for c in range(NCH):
                    p.on("tensor", lambda te, c=c, qs=qs, bi=bi, bufw=bufw: te.matmul(
                        ps[bi][:, 0:512], bufw.ap()[:, c * 128:c * 128 + 128],
                        xnT[:, c * S + qs * 512:c * S + qs * 512 + 512],
                        start=(c == 0), stop=(c == NCH - 1)))
                tmm = p.ck("tensor")
                p.wait("scalar", tmm)
                bcol = b1col if hc < 16 else convb
                p.on("scalar", lambda sx, bi=bi, hc=hc, qs=qs, bcol=bcol: sx.activation(
                    tC[:, qs * 512:qs * 512 + 512], ps[bi][:, 0:512], AF.Relu,
                    bias=bcol[:, (hc % 16):(hc % 16) + 1], scale=1.0))
                tev = p.ck("scalar")
                bank_guard[bi] = tev
            p.wait("vector", tev)
            tq = emit_float_quantize(tC[:, 0:2048], tC[:, 0:2048], 2048, 4, 3)
            p.wait("gpsimd", tq)
            tok_st = p.dma("gpsimd", d_t1[ts(hc), 0:2048], tC[0:128, 0:2048])
            p.wait("scalar", tok_st)
            p.wait("vector", tok_st)
        p.barrier()
        for tg in range(4):
            for hc in range(32):
                bufw = wa if hc % 2 == 0 else wb
                tokw = p.dma("sync", bufw[0:128, 0:1024], d_wq2[ts(hc), 0:1024])
                tokh = p.dma("sync", tC[:, 0:512], d_t1[ts(hc), tg * 512:tg * 512 + 512])
                p.wait("tensor", [tokw, tokh])
                for tt4 in range(4):
                    for os_ in range(2):
                        bi = tt4 * 2 + os_
                        if hc == 0:
                            p.wait("tensor", bank_guard.get(bi))
                        p.on("tensor", lambda te, tt4=tt4, os_=os_, bi=bi, hc=hc, bufw=bufw:
                             te.matmul(ps[bi][:, 0:512], tC[:, tt4 * 128:tt4 * 128 + 128],
                                       bufw.ap()[:, os_ * 512:os_ * 512 + 512],
                                       start=(hc == 0), stop=False))
                tmm = p.ck("tensor")
                p.wait("sync", tmm)
            for tt4 in range(4):
                tt = tg * 4 + tt4
                for os_ in range(2):
                    bi = tt4 * 2 + os_
                    p.on("tensor", lambda te, bi=bi, os_=os_: te.matmul(
                        ps[bi][:, 0:512], ones_row[0:1, 0:128],
                        qg[0:1, os_ * 512:os_ * 512 + 512], start=False, stop=True))
                    tmm = p.ck("tensor")
                    p.wait("vector", tmm)
                    p.on("vector", lambda v, bi=bi, os_=os_: v.tensor_copy(
                        tC[:, os_ * 512:os_ * 512 + 512], ps[bi][:, 0:512]))
                    tev = p.ck("vector")
                    bank_guard[bi] = tev
                p.wait("scalar", tev)
                tq = emit_float_quantize(tC[:, 0:1024], tC[:, 0:1024], 1024, 5, 10)
                p.wait("scalar", tq)
                p.on("scalar", lambda sx: sx.activation(tC[:, 1024:2048], tC[:, 0:1024], AF.Silu))
                tsl = p.ck("scalar")
                p.wait("sync", tq)
                tok_xx = p.dma("sync", tD[:, 0:1024], d_x[ts(tt), :])
                tok_ss = p.dma("sync", tD[:, 1024:2048], d_s[ts(tt), :])
                p.wait("vector", [tsl, tok_xx, tok_ss])
                p.on("vector", lambda v: v.tensor_mul(tC[:, 1024:2048], tC[:, 1024:2048], tD[:, 1024:2048]))
                p.on("vector", lambda v: v.tensor_add(tD[:, 0:1024], tD[:, 0:1024], tC[:, 1024:2048]))
                tcb = p.ck("vector")
                p.wait("gpsimd", tcb)
                dst = dout if final_store_combine else d_x
                tok_st = p.dma("gpsimd", dst[ts(tt), :], tD[:, 0:1024])
                p.wait("sync", tok_st)
                p.wait("scalar", tok_st)
        p.barrier()

    def layer3():
        emit_ln_and_transpose(store_xn_dram=False)
        l3_attn()
        l3_ffn_and_combine()

    tok_x = p.dma("sync", d_x[:, :], din["L0_attn_x"][:, :])
    p.wait("sync", tok_x)

    def emit_round(eng, dst, src, n, tmpf, tmpi):
        p.on(eng, lambda v: v.tensor_copy(tmpi[:, 0:n], src))
        p.on(eng, lambda v: v.tensor_copy(tmpf[:, 0:n], tmpi[:, 0:n]))
        p.on(eng, lambda v: v.tensor_sub(dst, src, tmpf[:, 0:n]))
        p.on(eng, lambda v: v.tensor_scalar(tmpi[:, 0:n].bitcast(F32), dst, 0.5, None, OP.is_ge))
        p.on(eng, lambda v: v.tensor_scalar(dst, dst, -0.5, None, OP.is_lt))
        p.on(eng, lambda v: v.tensor_sub(tmpf[:, 0:n], tmpf[:, 0:n], dst))
        p.on(eng, lambda v: v.tensor_add(dst, tmpf[:, 0:n], tmpi[:, 0:n].bitcast(F32)))

    def emit_weight_quant(dw_in, rows, cols, d_out):
        nslab = rows // 128
        p.on("vector", lambda v: v.memset(sc1[:, 0:1], 0.0))
        tv = None
        for ri in range(nslab):
            buf = wa if ri % 2 == 0 else wb
            tok = p.dma("sync", buf[0:128, 0:cols], dw_in[ts(ri), 0:cols])
            p.wait("vector", tok)
            p.on("vector", lambda v, buf=buf, cols=cols: v.reduce_sum(
                sc2[:, 0:1], buf[0:128, 0:cols], axis=AX.X, apply_absolute_value=True))
            p.ckw("vector")
            p.on("vector", lambda v: v.tensor_add(sc1[:, 0:1], sc1[:, 0:1], sc2[:, 0:1]))
            tv = p.ckw("vector")
            p.wait("sync", tv)
        p.wait("tensor", tv)
        p.wait("tensor", bank_guard.get(5))
        p.on("tensor", lambda te: te.transpose(ps[5][0:1, 0:128], sc1[:, 0:1], ident[:, :]))
        tmm = p.ck("tensor")
        p.wait("vector", tmm)
        p.on("vector", lambda v: v.reduce_sum(rrow[0:1, 1536:1537], ps[5][0:1, 0:128], axis=AX.X))
        p.ckw("vector")
        p.on("vector", lambda v, rows=rows, cols=cols: v.tensor_scalar(
            rrow[0:1, 1537:1538], rrow[0:1, 1536:1537], 1.0 / (rows * cols), None, OP.mult))
        p.ckw("vector")
        p.on("vector", lambda v: v.tensor_scalar(
            rrow[0:1, 1537:1538], rrow[0:1, 1537:1538], 1e-5, None, OP.max))
        p.ckw("vector")
        p.on("vector", lambda v: v.reciprocal(rrow[0:1, 1538:1539], rrow[0:1, 1537:1538]))
        trc = p.ckw("vector")
        bank_guard[5] = trc
        p.wait("tensor", trc)
        p.wait("tensor", bank_guard.get(4))
        p.on("tensor", lambda te: te.matmul(ps[4][0:128, 0:2], ones_row[0:1, 0:128],
                                            rrow[0:1, 1537:1539], start=True, stop=True))
        tmm = p.ck("tensor")
        p.wait("vector", tmm)
        p.on("vector", lambda v: v.tensor_copy(sc3[:, 0:2], ps[4][0:128, 0:2]))
        tq = p.ck("vector")
        bank_guard[4] = tq
        p.wait("tensor", tq)
        for ri in range(nslab):
            buf = wa if ri % 2 == 0 else wb
            tok = p.dma("sync", buf[0:128, 0:cols], dw_in[ts(ri), 0:cols])
            p.wait("vector", tok)
            for cg in range(0, cols, 2048):
                w_ = min(2048, cols - cg)
                p.on("vector", lambda v, buf=buf, cg=cg, w_=w_: v.tensor_scalar(
                    tB[:, 0:w_], buf[0:128, cg:cg + w_], sc3[:, 1:2], None, OP.mult))
                emit_round("vector", tA[:, 0:w_], tB[:, 0:w_], w_, tE, ti32)
                p.on("vector", lambda v, w_=w_: v.tensor_scalar(
                    tA[:, 0:w_], tA[:, 0:w_], -1.0, 1.0, OP.max, OP.min))
                p.on("vector", lambda v, w_=w_: v.tensor_scalar(
                    tA[:, 0:w_], tA[:, 0:w_], sc3[:, 0:1], None, OP.mult))
                tq = p.ck("vector")
                p.wait("gpsimd", tq)
                tok_s = p.dma("gpsimd", d_out[ts(ri), cg:cg + w_], tA[0:128, 0:w_])
                p.wait("vector", tok_s)
            p.wait("sync", tq)
        p.barrier()

    def l1_ffn_and_combine():
        emit_weight_quant(d_w["l1f_w1"], D, 4096, d_wq1)
        emit_weight_quant(d_w["l1f_w2"], 4096, D, d_wq2)
        tok1 = p.dma("sync", convw.ap().rearrange("p (c k) -> p c k", c=32)[:, :, 0:3],
                     din["L1_ffn_conv_w"].squeeze(1).rearrange("(c p) k -> p c k", p=128))
        tok2 = p.dma("sync", convb[:, 0:32],
                     din["L1_ffn_conv_b"].rearrange("(c p) -> p c", p=128),
                     allow_slow_non_contiguous=True)
        p.wait("vector", tok1)
        p.wait("scalar", tok2)
        for tt in range(NT):
            tok = p.dma("sync", tB[:, 0:1024], d_xn[ts(tt), :])
            p.wait("vector", tok)
            p.on("vector", lambda v: v.reduce_max(sc1[:, 0:1], tB[:, 0:1024], axis=AX.X,
                                                  apply_absolute_value=True))
            p.ckw("vector")
            p.on("vector", lambda v: v.tensor_scalar(sc1[:, 0:1], sc1[:, 0:1], 1e-5, None, OP.max))
            p.ckw("vector")
            p.on("vector", lambda v: v.reciprocal(sc2[:, 0:1], sc1[:, 0:1]))
            p.ckw("vector")
            p.on("vector", lambda v: v.tensor_scalar(sc2[:, 0:1], sc2[:, 0:1], 127.0, None, OP.mult))
            p.ckw("vector")
            p.on("vector", lambda v: v.reciprocal(sc3[:, 0:1], sc2[:, 0:1]))
            p.ckw("vector")
            p.on("vector", lambda v: v.tensor_scalar(tC[:, 0:1024], tB[:, 0:1024], sc2[:, 0:1], None, OP.mult))
            emit_round("vector", tD[:, 0:1024], tC[:, 0:1024], 1024, tE, ti32)
            p.on("vector", lambda v: v.tensor_scalar(tD[:, 0:1024], tD[:, 0:1024], -128.0, 127.0, OP.max, OP.min))
            p.on("vector", lambda v: v.tensor_scalar(tD[:, 0:1024], tD[:, 0:1024], sc3[:, 0:1], None, OP.mult))
            tq = p.ck("vector")
            p.wait("tensor", tq)
            for c in range(NCH):
                bi = 6 + (c % 2)
                p.wait("tensor", bank_guard.get(bi))
                p.on("tensor", lambda te, c=c, bi=bi: te.transpose(ps[bi][0:128, 0:128], tD[:, ts(c)], ident[:, :]))
                tmm = p.ck("tensor")
                p.wait("vector", tmm)
                p.on("vector", lambda v, c=c, tt=tt, bi=bi: v.tensor_copy(
                    xnT[:, c * S + tt * 128:c * S + tt * 128 + 128], ps[bi][0:128, 0:128]))
                bank_guard[bi] = p.ck("vector")
            p.wait("sync", bank_guard[7])
        p.barrier()
        p.on("vector", lambda v: v.memset(tA[:, 0:2048], 0.0))
        tamax = p.ck("vector")
        for hc in range(32):
            tok_w = p.dma("sync", (wa if hc % 2 == 0 else wb).ap()[:, 0:1024].rearrange(
                "p (c o) -> p c o", c=8), d_wq1.rearrange("(c r) o -> r c o", r=128)[:, :, ts(hc)])
            p.wait("tensor", tok_w)
            buf = wa if hc % 2 == 0 else wb
            for qs in range(4):
                bi = qs % 4
                p.wait("tensor", bank_guard.get(bi))
                for c in range(NCH):
                    p.on("tensor", lambda te, c=c, qs=qs, bi=bi, buf=buf:
                         te.matmul(ps[bi][:, 0:512], buf.ap()[:, c * 128:c * 128 + 128],
                                   xnT[:, c * S + qs * 512:c * S + qs * 512 + 512],
                                   start=(c == 0), stop=(c == NCH - 1)))
                tmm = p.ck("tensor")
                p.wait("vector", tmm)
                p.on("vector", lambda v, qs=qs, bi=bi: v.tensor_copy(tC[:, 1 + qs * 512:1 + qs * 512 + 512], ps[bi][:, 0:512]))
                bank_guard[bi] = p.ck("vector")
            p.on("vector", lambda v: v.memset(tC[:, 0:1], 0.0))
            p.on("vector", lambda v: v.memset(tC[:, 2049:2050], 0.0))
            p.on("vector", lambda v, hc=hc: v.tensor_scalar(tD[:, 0:2048], tC[:, 0:2048], convw[:, 4 * hc:4 * hc + 1], None, OP.mult))
            p.on("vector", lambda v, hc=hc: v.scalar_tensor_tensor(tD[:, 0:2048], tC[:, 1:2049], convw[:, 4 * hc + 1:4 * hc + 2], tD[:, 0:2048], OP.mult, OP.add))
            p.on("vector", lambda v, hc=hc: v.scalar_tensor_tensor(tD[:, 0:2048], tC[:, 2:2050], convw[:, 4 * hc + 2:4 * hc + 3], tD[:, 0:2048], OP.mult, OP.add))
            tcv = p.ck("vector")
            p.wait("scalar", tcv)
            p.on("scalar", lambda sx, hc=hc: sx.activation(tE[:, 0:2048], tD[:, 0:2048], AF.Silu,
                                                           bias=convb[:, hc:hc + 1], scale=1.0))
            tsl = p.ck("scalar")
            p.wait("vector", tsl)
            p.on("vector", lambda v: v.tensor_scalar(tB[:, 0:2048], tE[:, 0:2048], -1.0, None, OP.mult))
            p.on("vector", lambda v: v.tensor_max(tB[:, 0:2048], tB[:, 0:2048], tE[:, 0:2048]))
            p.on("vector", lambda v: v.tensor_max(tA[:, 0:2048], tA[:, 0:2048], tB[:, 0:2048]))
            tmx = p.ck("vector")
            p.wait("gpsimd", tsl)
            tok_st = p.dma("gpsimd", d_t1[ts(hc), 0:2048], tE[:, 0:2048])
            p.wait("scalar", tok_st)
            p.wait("sync", tmx)
        p.barrier()
        for tt in range(NT):
            bi = 6 + (tt % 2)
            p.wait("tensor", bank_guard.get(bi))
            p.on("tensor", lambda te, tt=tt, bi=bi: te.transpose(ps[bi][0:128, 0:128], tA[:, ts(tt)], ident[:, :]))
            tmm = p.ck("tensor")
            p.wait("vector", tmm)
            p.on("vector", lambda v, tt=tt, bi=bi: v.reduce_max(sc1[:, tt:tt + 1], ps[bi][0:128, 0:128], axis=AX.X))
            bank_guard[bi] = p.ckw("vector")
        for tt in range(NT):
            p.on("vector", lambda v, tt=tt: v.tensor_scalar(sc1[:, tt:tt + 1], sc1[:, tt:tt + 1], 1e-5, None, OP.max))
        p.ckw("vector")
        p.on("vector", lambda v: v.reciprocal(sc2[:, 0:NT], sc1[:, 0:NT]))
        p.ckw("vector")
        p.on("vector", lambda v: v.tensor_scalar(sc2[:, 0:NT], sc2[:, 0:NT], 127.0, None, OP.mult))
        p.ckw("vector")
        p.on("vector", lambda v: v.reciprocal(sc3[:, 0:NT], sc2[:, 0:NT]))
        tsc_ = p.ckw("vector")
        p.wait("tensor", tsc_)
        for (srccol, dstbase) in ((sc2, 0), (sc3, 0)):
            for tt in range(NT):
                bi = 6 + (tt % 2)
                p.wait("tensor", bank_guard.get(bi))
                p.on("tensor", lambda te, srccol=srccol, tt=tt, bi=bi: te.transpose(
                    ps[bi][0:1, 0:128], srccol[:, tt:tt + 1], ident[:, :]))
                tmm = p.ck("tensor")
                p.wait("vector", tmm)
                if srccol is sc2:
                    p.on("vector", lambda v, tt=tt, bi=bi: v.tensor_copy(
                        rrow[0:1, tt * 128:tt * 128 + 128], ps[bi][0:1, 0:128]))
                else:
                    p.on("vector", lambda v, tt=tt, bi=bi: v.tensor_copy(
                        brow[0:1, tt * 128:tt * 128 + 128], ps[bi][0:1, 0:128]))
                bank_guard[bi] = p.ck("vector")
        tsc2 = p.ck("vector") if p._dirty["vector"] else None
        for (srcrow, dstt) in ((rrow, tA), (brow, tB)):
            for qs in range(4):
                bi = 4 + (qs % 2)
                p.wait("tensor", bank_guard.get(bi))
                p.on("tensor", lambda te, srcrow=srcrow, qs=qs, bi=bi: te.matmul(
                    ps[bi][0:128, 0:512], ones_row[0:1, 0:128],
                    srcrow[0:1, qs * 512:qs * 512 + 512], start=True, stop=True))
                tmm = p.ck("tensor")
                p.wait("vector", tmm)
                p.on("vector", lambda v, qs=qs, bi=bi, dstt=dstt: v.tensor_copy(
                    dstt[:, qs * 512:qs * 512 + 512], ps[bi][0:128, 0:512]))
                bank_guard[bi] = p.ck("vector")
        p.barrier()
        for tg in range(4):
            for hc in range(32):
                buf = wa if hc % 2 == 0 else wb
                tokw = p.dma("sync", buf[0:128, 0:1024], d_wq2[ts(hc), 0:1024])
                tokh = p.dma("sync", tC[:, 0:512], d_t1[ts(hc), tg * 512:tg * 512 + 512])
                p.wait("vector", tokh)
                p.on("vector", lambda v, tg=tg: v.tensor_mul(tD[:, 0:512], tC[:, 0:512],
                                                             tA[:, tg * 512:tg * 512 + 512]))
                emit_round("vector", tD[:, 0:512], tD[:, 0:512], 512, tE, ti32)
                p.on("vector", lambda v: v.tensor_scalar(tD[:, 0:512], tD[:, 0:512], -128.0, 127.0, OP.max, OP.min))
                p.on("vector", lambda v, tg=tg: v.tensor_mul(tD[:, 0:512], tD[:, 0:512],
                                                             tB[:, tg * 512:tg * 512 + 512]))
                tq = p.ck("vector")
                p.wait("tensor", tq)
                p.wait("tensor", tokw)
                for tt4 in range(4):
                    for os_ in range(2):
                        bi = tt4 * 2 + os_
                        if hc == 0:
                            p.wait("tensor", bank_guard.get(bi))
                        p.on("tensor", lambda te, tt4=tt4, os_=os_, bi=bi, hc=hc, buf=buf:
                             te.matmul(ps[bi][:, 0:512], tD[:, tt4 * 128:tt4 * 128 + 128],
                                       buf.ap()[:, os_ * 512:os_ * 512 + 512],
                                       start=(hc == 0), stop=(hc == 31)))
                tmm = p.ck("tensor")
                p.wait("sync", tmm)
                p.wait("vector", tmm)
            p.wait("scalar", tmm)
            for tt4 in range(4):
                tt = tg * 4 + tt4
                for os_ in range(2):
                    bi = tt4 * 2 + os_
                    p.on("scalar", lambda sx, bi=bi, os_=os_: sx.activation(
                        tF[:, os_ * 512:os_ * 512 + 512], ps[bi][:, 0:512], AF.Silu))
                tsl = p.ck("scalar")
                for bi in (tt4 * 2, tt4 * 2 + 1):
                    bank_guard[bi] = tsl
                tok_xx = p.dma("sync", tC[:, 0:1024], d_x[ts(tt), :])
                tok_ss = p.dma("sync", tC[:, 1024:2048], d_s[ts(tt), :])
                p.wait("vector", tsl)
                p.wait("vector", tok_xx)
                p.wait("vector", tok_ss)
                p.on("vector", lambda v: v.tensor_mul(tF[:, 1024:2048], tC[:, 1024:2048], tF[:, 0:1024]))
                p.on("vector", lambda v: v.tensor_add(tC[:, 0:1024], tC[:, 0:1024], tF[:, 1024:2048]))
                tcb = p.ck("vector")
                p.wait("gpsimd", tcb)
                tok_st = p.dma("gpsimd", d_x[ts(tt), :], tC[:, 0:1024])
                p.wait("sync", tok_st)
                p.wait("scalar", tcb)
        p.barrier()

    def l1_qkv_cols(h):
        return (192 * h, 192 * h + 64, 192 * h + 128)

    def layer1():
        emit_ln_and_transpose(store_xn_dram=True)
        emit_mha(din["L1_attn_W_qkv"], l1_qkv_cols, din["L1_attn_b_qkv"],
                 din["L1_attn_W_out"], din["L1_attn_b_out"],
                 causal=True, dst_s=d_s, scale=0.125)
        l1_ffn_and_combine()

    if 0 in layers:
        layer0()
    if 2 in layers:
        layer2()
    if 1 in layers:
        layer1()
    if 3 in layers:
        layer3()

    p.barrier()
    if dbg == "s":
        tok2 = None
        for tt in range(NT):
            tok = p.dma("sync", tA[:, 0:1024], d_s[ts(tt), :])
            p.wait("gpsimd", tok)
            tok2 = p.dma("gpsimd", dout[ts(tt), :], tA[:, 0:1024])
            p.wait("sync", tok2)
        last = tok2
    elif dbg == "xn":
        tok2 = None
        for tt in range(NT):
            tok = p.dma("sync", tA[:, 0:1024], d_xn[ts(tt), :])
            p.wait("gpsimd", tok)
            tok2 = p.dma("gpsimd", dout[ts(tt), :], tA[:, 0:1024])
            p.wait("sync", tok2)
        last = tok2
    elif final_store_combine:
        last = None
    else:
        last = p.dma("gpsimd", dout[:, :], d_x[:, :])
    if last is not None:
        p.wait("gpsimd", last)

    with nc.Block() as block:
        @block.tensor
        def _(e):
            p.emit("tensor", e)

        @block.vector
        def _(e):
            p.emit("vector", e)

        @block.scalar
        def _(e):
            p.emit("scalar", e)

        @block.gpsimd
        def _(e):
            p.emit("gpsimd", e)

        @block.sync
        def _(e):
            p.emit("sync", e)

    es.close()
    return nc


def kernel(**inputs):
    x = np.ascontiguousarray(inputs["L0_attn_x"])
    nc = bass.Bass("TRN2", target_bir_lowering=False, detect_race_conditions=False)
    build_core_program(nc)
    in_maps = []
    for b in range(8):
        m = {k: np.ascontiguousarray(v) for k, v in inputs.items() if k != "L0_attn_x"}
        m["L0_attn_x"] = x[b]
        in_maps.append(m)
    res = run_bass_kernel_spmd(nc, in_maps, core_ids=list(range(8)))
    return np.stack([r["out"] for r in res.results], axis=0)


# version 05
# speedup vs baseline: 1.0188x; 1.0188x over previous
import contextlib
import numpy as np
import concourse.bass as bass
import concourse.mybir as mybir
from concourse.bass_utils import run_bass_kernel_spmd

F32 = mybir.dt.float32
I32 = mybir.dt.int32
U8 = mybir.dt.uint8
AF = mybir.ActivationFunctionType
OP = mybir.AluOpType
AX = mybir.AxisListType

S = 2048
D = 1024
NT = S // 128
NCH = D // 128
H = 16
DH = 64
LN2_HI = float(np.float32(0.6931471805599453))
LN2_LO = float(np.float64(0.6931471805599453094172321214581765680755)
               - np.float64(np.float32(0.6931471805599453)))


def ts(i, n=128):
    return slice(i * n, (i + 1) * n)


class P:
    ENGS = ("tensor", "vector", "scalar", "gpsimd", "sync")

    NDS = 8

    def __init__(self, nc, es):
        self.nc = nc
        self.steps = []
        self.cnt = {e: 0 for e in self.ENGS}
        self.dma_cnt = {"sync": 0, "gpsimd": 0, "scalar": 0}
        self.sem = {e: es.enter_context(nc.semaphore("sp_" + e)) for e in self.ENGS}
        self.dsem = {q: [es.enter_context(nc.semaphore("sd_%s%d" % (q, i)))
                         for i in range(self.NDS)] for q in self.dma_cnt}
        self._last = {}
        self._dirty = {e: False for e in self.ENGS}

    def on(self, eng, fn):
        self._dirty[eng] = True

        def run(engine, eng=eng, fn=fn):
            r = fn(engine)
            if r is not None:
                self._last[eng] = r
        self.steps.append((eng, run))

    def ck(self, eng):
        assert self._dirty[eng], "checkpoint on engine with no new ops: " + eng
        self._dirty[eng] = False
        self.cnt[eng] += 1
        sem = self.sem[eng]

        def run(engine, eng=eng, sem=sem):
            self._last[eng].then_inc(sem, 1)
        self.steps.append((eng, run))
        return (eng, self.cnt[eng])

    def ckw(self, eng):
        t = self.ck(eng) if self._dirty[eng] else (eng, self.cnt[eng])
        self.wait(eng, t)
        return t

    def wait(self, eng, token):
        if token is None:
            return
        if isinstance(token, list):
            best = {}
            for t in token:
                if t is None:
                    continue
                src, val = t
                key = src if not src.startswith("d") else (src, (val - 1) % self.NDS)
                if key not in best or best[key][1] < val:
                    best[key] = t
            for t in best.values():
                self.wait(eng, t)
            return
        src, val = token
        if src.startswith("d") and src[1:] in self.dsem:
            idx = val - 1
            sem = self.dsem[src[1:]][idx % self.NDS]
            v = 16 * (idx // self.NDS + 1)
        else:
            sem, v = self.sem[src], val

        def run(engine, sem=sem, v=v):
            engine.wait_ge(sem, v)
        self.steps.append((eng, run))

    def dma(self, q, out, in_, **kw):
        idx = self.dma_cnt[q]
        self.dma_cnt[q] += 1
        sem = self.dsem[q][idx % self.NDS]
        prev_round = idx // self.NDS

        def run(engine, out=out, in_=in_, kw=kw, sem=sem, prev_round=prev_round):
            if prev_round > 0:
                engine.wait_ge(sem, 16 * prev_round)
            engine.dma_start(out=out, in_=in_, **kw).then_inc(sem, 16)
        self.steps.append((q, run))
        return ("d" + q, self.dma_cnt[q])

    def barrier(self):
        for e in self.ENGS:
            if self._dirty[e]:
                self.ck(e)
        toks = [(e, self.cnt[e]) for e in self.ENGS if self.cnt[e] > 0]
        dtoks = []
        for q, c in self.dma_cnt.items():
            for i in range(max(0, c - self.NDS), c):
                dtoks.append(("d" + q, i + 1))
        for e in self.ENGS:
            for t in toks:
                if t[0] != e:
                    self.wait(e, t)
            for t in dtoks:
                self.wait(e, t)

    def emit(self, engine_name, engine):
        for eng, fn in self.steps:
            if eng == engine_name:
                fn(engine)


def build_core_program(nc, dbg=None, layers=(0, 1, 2, 3), l0_stop=None):
    final_store_combine = (3 in layers) and (dbg is None)
    es = contextlib.ExitStack()
    din = {}
    spec = {
        "L0_attn_x": (S, D),
        "L0_attn_lt_w_in": (3 * D, D), "L0_attn_lt_b_in": (3 * D,),
        "L0_attn_lt_w_out": (D, D), "L0_attn_lt_b_out": (D,),
        "L0_attn_st_w_in": (3 * D, D), "L0_attn_st_b_in": (3 * D,),
        "L0_attn_st_w_out": (D, D), "L0_attn_st_b_out": (D,),
        "L0_attn_protos": (8, D), "L0_attn_fus_w": (D, 2 * D), "L0_attn_fus_b": (D,),
        "L0_ffn_router_w": (D, 16),
        "L0_ffn_w1": (16, D, 2048), "L0_ffn_b1": (16, 2048),
        "L0_ffn_w2": (16, 2048, D), "L0_ffn_b2": (16, D),
        "L1_attn_W_qkv": (D, 3 * D), "L1_attn_b_qkv": (3 * D,),
        "L1_attn_W_out": (D, D), "L1_attn_b_out": (D,),
        "L1_ffn_w1": (4096, D), "L1_ffn_conv_w": (4096, 1, 3), "L1_ffn_conv_b": (4096,),
        "L1_ffn_w2": (D, 4096),
        "L2_attn_W_qkv": (3 * D, D), "L2_attn_b_qkv": (3 * D,),
        "L2_attn_W_out": (D, D), "L2_attn_b_out": (D,),
        "L2_attn_W_gate": (D, D), "L2_attn_b_gate": (D,),
        "L2_attn_scale_gain": (11, 16), "L2_attn_W_qscale": (11, 64),
        "L2_attn_identity_bypass": (16,),
        "L2_ffn_weight": (D, D), "L2_ffn_gate_w": (8, D),
        "L2_ffn_lora_U": (8, 16, D), "L2_ffn_lora_V": (8, D, 16),
        "L3_attn_Wq": (D, D), "L3_attn_bq": (D,),
        "L3_attn_Wk": (D, D), "L3_attn_bk": (D,),
        "L3_attn_Wv": (D, D), "L3_attn_bv": (D,),
        "L3_attn_Wo": (D, D), "L3_attn_bo": (D,),
        "L3_ffn_w1": (4096, D), "L3_ffn_b1": (4096,),
        "L3_ffn_w2": (D, 4096), "L3_ffn_b2": (D,),
    }
    for name, shp in spec.items():
        din[name] = nc.dram_tensor(name, list(shp), F32, kind="ExternalInput").ap()
    dout = nc.dram_tensor("out", [S, D], F32, kind="ExternalOutput").ap()

    dbg_outs = set() if dbg != "dump" else {
        "d_s", "d_xn", "d_q2", "d_k2", "d_v2", "d_gains", "d_sd", "d_t1", "d_t2", "d_comb", "d_av"}
    dram = lambda n, shp: nc.dram_tensor(
        n, list(shp), F32, kind=("ExternalOutput" if n in dbg_outs else "Internal")).ap()
    d_s = dram("d_s", (S, D))
    d_x = dram("d_x", (S, D))
    d_xn = dram("d_xn", (S, D))
    d_av = dram("d_av", (D, S))
    d_t1 = dram("d_t1", (4096, 2048))
    d_t2 = dram("d_t2", (4096, 2048))
    d_wq1 = dram("d_wq1", (D, 4096))
    d_wq2 = dram("d_wq2", (4096, D))
    d_comb = dram("d_comb", (2 * D, S))
    d_q2 = dram("d_q2", (D, S))
    d_k2 = dram("d_k2", (D, S))
    d_q2_tm = dram("d_q2_tm", (S, D))
    d_k2_tm = dram("d_k2_tm", (S, D))
    d_v2 = dram("d_v2", (S, D))
    d_gains = dram("d_gains", (16, 11 * S))
    d_sd = dram("d_sd", (S, 358))

    tw_specs = {
        "lt_w_in": ("L0_attn_lt_w_in", 3 * D, D), "lt_w_out": ("L0_attn_lt_w_out", D, D),
        "st_w_in": ("L0_attn_st_w_in", 3 * D, D), "st_w_out": ("L0_attn_st_w_out", D, D),
        "fus_w": ("L0_attn_fus_w", D, 2 * D), "protos": ("L0_attn_protos", 8, D),
        "l1f_w1": ("L1_ffn_w1", 4096, D), "l1f_w2": ("L1_ffn_w2", D, 4096),
        "l2_qkv": ("L2_attn_W_qkv", 3 * D, D), "l2_out": ("L2_attn_W_out", D, D),
        "l2_gate": ("L2_attn_W_gate", D, D), "l2_qscale": ("L2_attn_W_qscale", 11, 64),
        "l2f_w": ("L2_ffn_weight", D, D), "l2f_gw": ("L2_ffn_gate_w", 8, D),
        "l3f_w1": ("L3_ffn_w1", 4096, D), "l3f_w2": ("L3_ffn_w2", D, 4096),
        "l2f_U": (None, 128, D), "l2f_V": (None, 128, D),
    }
    d_w = {}
    for k, (src, r, c) in tw_specs.items():
        if k == "l2f_U":
            d_w[k] = dram("d_w_" + k, (D, 128))
        elif k == "l2f_V":
            d_w[k] = dram("d_w_" + k, (128, D))
        else:
            d_w[k] = dram("d_w_" + k, (c, r))

    sbuf = {}

    def SB(name, shape, dt=F32):
        sbuf[name] = nc.alloc_sbuf_tensor(name, shape, dt)
        return sbuf[name]

    xnT = SB("xnT", [128, NCH * S])
    ident = SB("ident", [128, 128])
    ones_row = SB("ones_row", [1, 128])
    ln_nmean = SB("ln_nmean", [128, NT])
    ln_rstd = SB("ln_rstd", [128, NT])
    wa = SB("wa", [128, 4096])
    wb = SB("wb", [128, 4096])
    tA = SB("tA", [128, 2048])
    tB = SB("tB", [128, 2048])
    tC = SB("tC", [128, 2056])
    tD = SB("tD", [128, 2048])
    tE = SB("tE", [128, 2048])
    tF = SB("tF", [128, 2112])
    expb = SB("expb", [128, 4 * 512])
    sc1 = SB("sc1", [128, 16])
    sc2 = SB("sc2", [128, 16])
    sc3 = SB("sc3", [128, 16])
    rrow = SB("rrow", [1, 2048])
    brow = SB("brow", [1, 2048])
    xtile = SB("xtile", [128, 1024])
    big = SB("big", [128, 6144])
    segq_bc = big.ap()[:, 0:2048]
    sing_bc = big.ap()[:, 2048:4096]
    masks = big.ap()[:, 4096:6144]
    qg = big.ap()
    ti32 = SB("ti32", [128, 2048], I32)
    ones_col = SB("ones_col", [128, 1])
    seg_tm = SB("seg_tm", [128, 16])
    ohR_tm = SB("ohR_tm", [128, 256])
    wtok_tm = SB("wtok_tm", [128, 16])
    b1col = SB("b1col", [128, 16])
    accA = SB("accA", [128, 1])
    convw = SB("convw", [128, 128])
    convb = SB("convb", [128, 32])

    ps = [nc.alloc_psum_tensor("psb%d" % i, [128, 512], F32) for i in range(8)]

    p = P(nc, es)
    bank_guard = {}

    def _gp_init(g):
        g.memset(ones_row[:, :], 1.0)
        g.memset(ones_col[:, :], 1.0)
        return g.iota(tA[:, 0:512], [[1, 512]], base=0, channel_multiplier=-1,
                      allow_small_or_imprecise_dtypes=True)
    p.on("gpsimd", _gp_init)
    t = p.ck("gpsimd")
    p.wait("vector", t)
    p.on("vector", lambda v: v.tensor_scalar(ident[:, :], tA[:, 0:128], 0.0, None, OP.is_equal))
    for r in range(4):
        p.on("vector", lambda v, r=r: v.tensor_scalar(
            masks[:, ts(r, 512)], tA[:, 0:512], float(128 * r), None, OP.is_ge))
    t_const = p.ck("vector")
    p.wait("tensor", t_const)
    p.wait("gpsimd", t_const)
    p.wait("scalar", t_const)

    pret_guard = {}

    def pretranspose(src, rows, cols, dst):
        if cols > 2048:
            assert cols % 2048 == 0
            for cg in range(cols // 2048):
                pretranspose(src[:, cg * 2048:(cg + 1) * 2048], rows, 2048,
                             dst[cg * 2048:(cg + 1) * 2048, :])
            return
        ncol = (cols + 127) // 128
        nslab = (rows + 127) // 128
        for ri in range(nslab):
            r0, r1 = ri * 128, min(rows, ri * 128 + 128)
            rl = r1 - r0
            bname = "wa" if ri % 2 == 0 else "wb"
            ename = "tA" if ri % 2 == 0 else "tB"
            buf = wa if ri % 2 == 0 else wb
            ebuf = tA if ri % 2 == 0 else tB
            p.wait("sync", pret_guard.get(bname))
            tok_l = p.dma("sync", buf[0:rl, 0:cols], src[r0:r1, 0:cols])
            p.wait("tensor", tok_l)
            p.wait("vector", pret_guard.get(ename))
            prev_ev = {}
            for ci in range(ncol):
                c0, c1 = ci * 128, min(cols, ci * 128 + 128)
                bi = 6 + (ci % 2)
                bank = ps[bi]
                p.wait("tensor", bank_guard.get(bi))
                p.on("tensor", lambda te, rl=rl, c0=c0, c1=c1, bank=bank, buf=buf:
                     te.transpose(bank[0:c1 - c0, 0:rl], buf[0:rl, c0:c1], ident[0:rl, 0:rl]))
                tmm = p.ck("tensor")
                p.wait("vector", tmm)
                p.on("vector", lambda v, rl=rl, c0=c0, c1=c1, bank=bank, ci=ci, ebuf=ebuf:
                     v.tensor_copy(ebuf[0:c1 - c0, ci * 128:ci * 128 + rl], bank[0:c1 - c0, 0:rl]))
                prev_ev[ci] = p.ck("vector")
                bank_guard[bi] = prev_ev[ci]
                pret_guard[bname] = tmm
            p.wait("gpsimd", prev_ev[ncol - 1])
            if cols >= 128:
                out_ap = dst[0:cols, r0:r1].rearrange("(c q) r -> q c r", q=128)
                in_ap = ebuf.ap().rearrange("q (c r) -> q c r", r=128)[0:128, 0:ncol, 0:rl]
            else:
                out_ap = dst[0:cols, r0:r1]
                in_ap = ebuf.ap()[0:cols, 0:rl]
            pret_guard[ename] = p.dma("gpsimd", out_ap, in_ap)

    need_prep = []
    if 0 in layers:
        need_prep += ["lt_w_in", "lt_w_out", "st_w_in", "st_w_out", "fus_w", "protos"]
    if 1 in layers:
        need_prep += ["l1f_w1", "l1f_w2"]
    if 2 in layers:
        need_prep += ["l2_qkv", "l2_out", "l2_gate", "l2_qscale", "l2f_w", "l2f_gw"]
    if 3 in layers:
        need_prep += ["l3f_w1", "l3f_w2"]
    for k in need_prep:
        srcname, rows, cols = tw_specs[k]
        pretranspose(din[srcname], rows, cols, d_w[k])
    if 2 in layers:
        pretranspose(din["L2_ffn_lora_U"].rearrange("e r d -> (e r) d"), 128, D, d_w["l2f_U"])
        for e in range(8):
            pretranspose(din["L2_ffn_lora_V"][e], D, 16, d_w["l2f_V"][16 * e:16 * e + 16, :])
    if need_prep or 2 in layers:
        p.barrier()

    ln_eps = 1e-5

    def emit_ln_and_transpose(store_xn_dram=True, src=None):
        src = src if src is not None else d_x
        tok_dxn = None
        tok_xt_prev = None
        for tt in range(NT):
            xt = xtile[:, 0:1024]
            tok_xt = p.dma("sync", xtile[:, 0:1024], src[ts(tt), :])
            p.wait("vector", tok_xt)
            p.wait("scalar", tok_xt)
            p.on("vector", lambda v, xt=xt, tt=tt: v.reduce_sum(sc1[:, tt:tt + 1], xt, axis=AX.X))
            p.ckw("vector")
            p.on("vector", lambda v, tt=tt: v.tensor_scalar(
                ln_nmean[:, tt:tt + 1], sc1[:, tt:tt + 1], -1.0 / D, None, OP.mult))
            tv = p.ck("vector")
            p.wait("scalar", tv)
            if tok_dxn is not None:
                p.wait("scalar", tok_dxn)
            p.on("scalar", lambda s, xt=xt, tt=tt: s.activation(
                tB[:, 0:1024], xt, AF.Square, bias=ln_nmean[:, tt:tt + 1], scale=1.0,
                accum_out=accA[:, 0:1]))
            tsq = p.ck("scalar")
            p.wait("vector", tsq)
            p.on("vector", lambda v, tt=tt: v.tensor_scalar(
                sc2[:, tt:tt + 1], accA[:, 0:1], 1.0 / D, ln_eps, OP.mult, OP.add))
            tv2 = p.ck("vector")
            p.wait("scalar", tv2)
            p.on("scalar", lambda s, tt=tt: s.activation(sc3[:, tt:tt + 1], sc2[:, tt:tt + 1], AF.Sqrt))
            tsq2 = p.ck("scalar")
            p.wait("vector", tsq2)
            p.on("vector", lambda v, tt=tt: v.reciprocal(sc1[:, tt:tt + 1], sc3[:, tt:tt + 1]))
            p.ckw("vector")
            p.on("vector", lambda v, tt=tt: v.tensor_mul(sc3[:, tt:tt + 1], sc1[:, tt:tt + 1], sc1[:, tt:tt + 1]))
            p.ckw("vector")
            p.on("vector", lambda v, tt=tt: v.tensor_mul(sc3[:, tt:tt + 1], sc3[:, tt:tt + 1], sc2[:, tt:tt + 1]))
            p.ckw("vector")
            p.on("vector", lambda v, tt=tt: v.tensor_scalar(
                sc3[:, tt:tt + 1], sc3[:, tt:tt + 1], -0.5, 1.5, OP.mult, OP.add))
            p.ckw("vector")
            p.on("vector", lambda v, tt=tt: v.tensor_mul(ln_rstd[:, tt:tt + 1], sc1[:, tt:tt + 1], sc3[:, tt:tt + 1]))
            p.ckw("vector")
            p.on("vector", lambda v, tt=tt: v.tensor_mul(sc2[:, tt:tt + 1], ln_nmean[:, tt:tt + 1], ln_rstd[:, tt:tt + 1]))
            tvr = p.ck("vector")
            p.wait("scalar", tvr)
            p.on("scalar", lambda s, xt=xt, tt=tt: s.activation(
                tB[:, 0:1024], xt, AF.Identity, bias=sc2[:, tt:tt + 1], scale=ln_rstd[:, tt:tt + 1]))
            tap = p.ck("scalar")
            if store_xn_dram:
                p.wait("gpsimd", tap)
                tok_dxn = p.dma("gpsimd", d_xn[ts(tt), :], tB[:, 0:1024])
            p.wait("tensor", tap)
            evs = {}
            for c in range(NCH):
                bi = 6 + (c % 2)
                bank = ps[bi]
                p.wait("tensor", bank_guard.get(bi))
                p.on("tensor", lambda te, c=c, bank=bank: te.transpose(
                    bank[0:128, 0:128], tB[:, ts(c)], ident[:, :]))
                tmm = p.ck("tensor")
                p.wait("vector", tmm)
                p.on("vector", lambda v, c=c, tt=tt, bank=bank: v.tensor_copy(
                    xnT[:, c * S + tt * 128: c * S + tt * 128 + 128], bank[0:128, 0:128]))
                evs[c] = p.ck("vector")
                bank_guard[bi] = evs[c]
            p.wait("scalar", evs[NCH - 1])
            p.wait("sync", tap)
        p.barrier()

    def load_w_slices(dw, os0, width, buf, nch=NCH, bufoff=0):
        src = dw.rearrange("(c r) o -> r c o", r=128)[:, :, os0:os0 + width]
        dst = buf.ap()[:, bufoff:bufoff + nch * width].rearrange("p (c o) -> p c o", c=nch)
        return p.dma("sync", dst, src)

    def bias_row(dbias, n, off=0, buf=None):
        buf = buf or brow
        return p.dma("sync", buf[0:1, off:off + n], dbias.unsqueeze(0)[0:1, 0:n])

    def emit_mha(w_qkv_inmajor, qkv_perhead_cols, b_qkv, w_out_inmajor, b_out,
                 causal=False, stabilize=False, mask_fm=None, dst_s=None,
                 scale=0.125, proj_fm_dst=None, proj_post_evac=None):
        V2 = tF
        for hp in range(8):
            h0 = 2 * hp
            wtoks = []
            for j, kind in enumerate((0, 1, 2)):
                for hh in range(2):
                    qc, kc_, vc = qkv_perhead_cols(h0 + hh)
                    col0 = (qc, kc_, vc)[kind]
                    src = w_qkv_inmajor.rearrange("(c r) o -> r c o", r=128)[:, :, col0:col0 + 64]
                    dstap = wa.ap()[:, kind * 1024:kind * 1024 + 8 * 128].rearrange(
                        "p (c o) -> p c o", c=8)[:, :, hh * 64:hh * 64 + 64]
                    wtoks.append(p.dma("sync", dstap, src))
            btoks = []
            for hh in range(2):
                qc, kc_, vc = qkv_perhead_cols(h0 + hh)
                btoks.append(p.dma("sync", sc1[hh * 64:hh * 64 + 64, 8:9], b_qkv[qc:qc + 64].unsqueeze(1)))
                btoks.append(p.dma("sync", sc2[hh * 64:hh * 64 + 64, 8:9], b_qkv[kc_:kc_ + 64].unsqueeze(1)))
                btoks.append(p.dma("sync", rrow[0:1, hh * 64:hh * 64 + 64], b_qkv[vc:vc + 64].unsqueeze(0)))
            p.wait("tensor", wtoks)
            p.wait("tensor", btoks)
            p.wait("scalar", btoks)
            for dst, woff, bcol in ((tD, 0, sc1), (tC, 1024, sc2)):
                evtoks = {}
                for qs in range(4):
                    bank = ps[4 + (qs % 2)]
                    if qs >= 2:
                        p.wait("tensor", evtoks[qs - 2])
                    for c in range(NCH):
                        p.on("tensor", lambda te, c=c, qs=qs, woff=woff, bank=bank:
                             te.matmul(bank[:, 0:512], wa.ap()[:, woff + c * 128:woff + c * 128 + 128],
                                       xnT[:, c * S + qs * 512:c * S + qs * 512 + 512],
                                       start=(c == 0), stop=(c == NCH - 1)))
                    tmm = p.ck("tensor")
                    p.wait("scalar", tmm)
                    p.on("scalar", lambda s, dst=dst, qs=qs, bank=bank, bcol=bcol:
                         s.activation(dst[:, ts(qs, 512)], bank[:, 0:512], AF.Identity,
                                      bias=bcol[:, 8:9], scale=1.0))
                    evtoks[qs] = p.ck("scalar")
                p.wait("tensor", evtoks[2])
                p.wait("tensor", evtoks[3])
            evtoks = {}
            for tt in range(NT):
                bank = ps[6]
                if tt >= 1:
                    p.wait("tensor", evtoks[tt - 1])
                for c in range(NCH):
                    p.on("tensor", lambda te, c=c, tt=tt, bank=bank:
                         te.matmul(bank[:, 0:128], xnT[:, c * S + tt * 128:c * S + tt * 128 + 128],
                                   wa.ap()[:, 2048 + c * 128:2048 + c * 128 + 128],
                                   start=(c == 0), stop=False))
                p.on("tensor", lambda te, tt=tt, bank=bank:
                     te.matmul(bank[:, 0:128], ones_row[0:1, 0:128], rrow[0:1, 0:128],
                               start=False, stop=True))
                tmm = p.ck("tensor")
                p.wait("vector", tmm)
                p.on("vector", lambda v, tt=tt, bank=bank: v.tensor_copy(V2[:, tt * 132:tt * 132 + 64], bank[:, 0:64]))
                p.on("vector", lambda v, tt=tt, bank=bank: v.tensor_copy(V2[:, tt * 132 + 66:tt * 132 + 130], bank[:, 64:128]))
                evtoks[tt] = p.ck("vector")
            V2r = V2.ap().rearrange("p (t o) -> p t o", t=NT)
            p.on("vector", lambda v: v.memset(V2r[:, :, 64:65], 1.0))
            p.on("vector", lambda v: v.memset(V2r[:, :, 130:131], 1.0))
            tones = p.ck("vector")
            p.wait("tensor", tones)

            prev_exp_last = None
            prev_av_last = None
            for hh in range(2):
                qoff = 64 * hh
                voff = 66 * hh
                for qs in range(4):
                    nkc = (4 * qs + 4) if causal else NT
                    av_bank = ps[2 + hh]
                    exp_ready = {}
                    av_done = {}
                    for kc in range(nkc):
                        sb_bank = ps[kc % 2]
                        if kc >= 2:
                            p.wait("tensor", exp_ready[kc - 2])
                        elif kc == 0 and prev_exp_last is not None:
                            p.wait("tensor", prev_exp_last)
                        p.on("tensor", lambda te, kc=kc, qs=qs, qoff=qoff, sb_bank=sb_bank:
                             te.matmul(sb_bank[:, 0:512],
                                       tC[qoff:qoff + 64, kc * 128:kc * 128 + 128],
                                       tD[qoff:qoff + 64, qs * 512:qs * 512 + 512],
                                       start=True, stop=True))
                        tsc = p.ck("tensor")
                        p.wait("scalar", tsc)
                        if kc >= 4:
                            p.wait("scalar", av_done[kc - 4])
                        elif kc == 0 and prev_av_last is not None:
                            p.wait("scalar", prev_av_last)
                        ebuf = expb[:, ts(kc % 4, 512)]
                        p.on("scalar", lambda s, ebuf=ebuf, sb_bank=sb_bank:
                             s.activation(ebuf, sb_bank[:, 0:512], AF.Exp, scale=scale))
                        texp = p.ck("scalar")
                        if causal and kc >= 4 * qs:
                            r = kc - 4 * qs
                            p.wait("vector", texp)
                            p.on("vector", lambda v, ebuf=ebuf, r=r:
                                 v.tensor_mul(ebuf, ebuf, masks[:, ts(r, 512)]))
                            texp = p.ck("vector")
                        if mask_fm is not None:
                            p.wait("vector", texp)
                            mask_fm(kc, qs, ebuf)
                            texp = p.ck("vector")
                        exp_ready[kc] = texp
                        p.wait("tensor", texp)
                        p.on("tensor", lambda te, kc=kc, voff=voff, av_bank=av_bank, ebuf=ebuf, nkc=nkc:
                             te.matmul(av_bank[0:65, 0:512], V2[:, kc * 132 + voff:kc * 132 + voff + 65],
                                       ebuf, start=(kc == 0), stop=(kc == nkc - 1)))
                        av_done[kc] = p.ck("tensor")
                    p.wait("vector", av_done[nkc - 1])
                    p.on("vector", lambda v, av_bank=av_bank: v.reciprocal(
                        rrow[0:1, 1024:1536], av_bank[64:65, 0:512]))
                    trc = p.ck("vector")
                    p.wait("tensor", trc)
                    p.on("tensor", lambda te: te.matmul(ps[4][0:64, 0:512], ones_row[0:1, 0:64],
                                                        rrow[0:1, 1024:1536], start=True, stop=True))
                    tbc = p.ck("tensor")
                    p.wait("vector", tbc)
                    p.on("vector", lambda v: v.tensor_copy(tE[0:64, 512:1024], ps[4][0:64, 0:512]))
                    p.on("vector", lambda v, av_bank=av_bank: v.tensor_mul(
                        tE[0:64, 0:512], av_bank[0:64, 0:512], tE[0:64, 512:1024]))
                    tn = p.ck("vector")
                    p.wait("gpsimd", tn)
                    tok_park = p.dma("gpsimd", d_av[64 * (h0 + hh):64 * (h0 + hh) + 64, qs * 512:qs * 512 + 512],
                                     tE[0:64, 0:512])
                    p.wait("vector", tok_park)
                    p.wait("tensor", tok_park)
                    prev_exp_last = exp_ready[nkc - 1]
                    prev_av_last = av_done[nkc - 1]
            p.barrier()

        if proj_fm_dst is None:
            proj_from_fm_dram(d_av, w_out_inmajor, b_out, dst_s)
        else:
            dstd, row0 = proj_fm_dst
            proj_fm(d_av, w_out_inmajor, b_out, dstd, post_evac=proj_post_evac, dst_row0=row0)

    def proj_from_fm_dram(d_src_fm, w_inmajor, b_dram, dst_dram, nch_in=8, dout=1024):
        tb = bias_row(b_dram, dout, off=0) if b_dram is not None else None
        if tb is not None:
            p.wait("tensor", tb)
        nos = dout // 512
        nhalf = (nch_in + 7) // 8
        prev_te = None
        prev_ev = None
        for tg in range(4):
            for half in range(nhalf):
                nchh = min(8, nch_in - half * 8)
                if prev_te is not None:
                    p.wait("sync", prev_te)
                toks = []
                for os_, buf in ((0, wa), (1, wb))[:nos]:
                    src = w_inmajor.rearrange("(c r) o -> r c o", r=128)[
                        :, half * 8:half * 8 + nchh, os_ * 512:os_ * 512 + 512]
                    dstap = buf.ap()[:, 0:nchh * 512].rearrange("p (c o) -> p c o", c=nchh)
                    toks.append(p.dma("sync", dstap, src))
                for c in range(nchh):
                    dstb = tC if c < 4 else tD
                    toks.append(p.dma("sync", dstb[:, ts(c % 4, 512)],
                                      d_src_fm[ts(half * 8 + c), tg * 512:tg * 512 + 512]))
                p.wait("tensor", toks)
                if prev_ev is not None and half == 0:
                    p.wait("tensor", prev_ev)
                for tt4 in range(4):
                    for os_ in range(nos):
                        bank = ps[tt4 * 2 + os_]
                        wbuf = wa if os_ == 0 else wb
                        for c in range(nchh):
                            dstb = tC if c < 4 else tD
                            p.on("tensor", lambda te, c=c, tt4=tt4, bank=bank, wbuf=wbuf, dstb=dstb, half=half:
                                 te.matmul(bank[:, 0:512],
                                           dstb[:, (c % 4) * 512 + tt4 * 128:(c % 4) * 512 + tt4 * 128 + 128],
                                           wbuf.ap()[:, c * 512:c * 512 + 512],
                                           start=(half == 0 and c == 0), stop=False))
                prev_te = p.ck("tensor")
            evs = []
            for tt4 in range(4):
                tt = tg * 4 + tt4
                for os_ in range(nos):
                    bank = ps[tt4 * 2 + os_]
                    p.on("tensor", lambda te, bank=bank, os_=os_:
                         te.matmul(bank[:, 0:512], ones_row[0:1, 0:128],
                                   brow[0:1, os_ * 512:os_ * 512 + 512], start=False, stop=True)
                         if b_dram is not None else
                         te.matmul(bank[:, 0:512], ones_row[0:1, 0:128],
                                   brow[0:1, os_ * 512:os_ * 512 + 512], start=False, stop=True))
                    tmm = p.ck("tensor")
                    last_tmm = tmm
                    p.wait("vector", tmm)
                    dstt = tE if os_ == 0 else tF
                    p.on("vector", lambda v, bank=bank, tt4=tt4, dstt=dstt:
                         v.tensor_copy(dstt[:, tt4 * 512:tt4 * 512 + 512], bank[:, 0:512]))
                    evs.append(p.ck("vector"))
                p.wait("gpsimd", evs[-1])
                tok_s1 = p.dma("gpsimd", dst_dram[ts(tt), 0:512], tE[:, tt4 * 512:tt4 * 512 + 512])
                if nos > 1:
                    tok_s1 = p.dma("gpsimd", dst_dram[ts(tt), 512:1024], tF[:, tt4 * 512:tt4 * 512 + 512])
                p.wait("vector", tok_s1)
            prev_te = last_tmm
            prev_ev = evs[-1]
        p.barrier()

    def proj_fm(d_src_fm, w_inmajor, b_dram, dst_fm_dram, nch_in=8, dout=1024,
                post_evac=None, dst_row0=0):
        tb = p.dma("sync", sc1[:, 0:dout // 128],
                   b_dram.rearrange("(c p) -> p c", p=128), allow_slow_non_contiguous=True)
        p.wait("scalar", tb)
        prev_te = None
        for oc in range(dout // 128):
            bufw = wa if oc % 2 == 0 else wb
            if prev_te is not None:
                p.wait("sync", prev_te)
            tokw = p.dma("sync", bufw.ap()[:, 0:nch_in * 128].rearrange("p (c o) -> p c o", c=nch_in),
                         w_inmajor.rearrange("(c r) o -> r c o", r=128)[:, :, ts(oc)])
            p.wait("tensor", tokw)
            for qs in range(4):
                toks = []
                if prev_te is not None:
                    p.wait("sync", prev_te)
                for c in range(nch_in):
                    dstb = (tC, tD)[c // 4]
                    toks.append(p.dma("sync", dstb[:, ts(c % 4, 512)],
                                      d_src_fm[ts(c), qs * 512:qs * 512 + 512]))
                p.wait("tensor", toks)
                bi = qs % 2
                p.wait("tensor", bank_guard.get(bi))
                for c in range(nch_in):
                    dstb = (tC, tD)[c // 4]
                    p.on("tensor", lambda te, c=c, qs=qs, bi=bi, bufw=bufw, dstb=dstb:
                         te.matmul(ps[bi][:, 0:512], bufw.ap()[:, c * 128:c * 128 + 128],
                                   dstb[:, ts(c % 4, 512)], start=(c == 0), stop=(c == nch_in - 1)))
                tmm = p.ck("tensor")
                prev_te = tmm
                p.wait("scalar", tmm)
                p.on("scalar", lambda sx, bi=bi, oc=oc: sx.activation(
                    tB[:, 0:512], ps[bi][:, 0:512], AF.Identity, bias=sc1[:, oc:oc + 1], scale=1.0))
                tev = p.ck("scalar")
                bank_guard[bi] = tev
                if post_evac is not None:
                    tev = post_evac(oc, qs, tB[:, 0:512], tev)
                p.wait("gpsimd", tev)
                tok_st = p.dma("gpsimd", dst_fm_dram[dst_row0 + oc * 128:dst_row0 + oc * 128 + 128,
                                                     qs * 512:qs * 512 + 512], tB[:, 0:512])
                p.wait("scalar", tok_st)
                p.wait("vector", tok_st)
        p.barrier()

    def l0_seg_and_router():
        tokp = p.dma("sync", wa[0:128, 0:8 * 8].rearrange("p (c o) -> p c o", c=8),
                     d_w["protos"].rearrange("(c r) o -> r c o", r=128))
        tokr = p.dma("sync", wb[0:128, 0:8 * 16].rearrange("p (c o) -> p c o", c=8),
                     din["L0_ffn_router_w"].rearrange("(c r) o -> r c o", r=128))
        tok8 = None
        p.on("gpsimd", lambda g: g.iota(tB[:, 0:16], [[1, 16]], base=0, channel_multiplier=0,
                                        allow_small_or_imprecise_dtypes=True))
        tio = p.ck("gpsimd")
        p.wait("tensor", tokp)
        p.wait("tensor", tokr)
        p.wait("vector", tio)
        for tt in range(NT):
            bi = 4 + (tt % 2)
            p.wait("tensor", bank_guard.get(bi))
            for c in range(NCH):
                p.on("tensor", lambda te, c=c, tt=tt, bi=bi: te.matmul(
                    ps[bi][:, 0:8], wa.ap()[:, c * 8:c * 8 + 8],
                    xnT[:, c * S + tt * 128:c * S + tt * 128 + 128].rearrange("p t -> p t") if False else xnT[:, c * S + tt * 128:c * S + tt * 128 + 128],
                    start=(c == 0), stop=False) if False else te.matmul(
                    ps[bi][0:128, 0:8], xnT[:, c * S + tt * 128:c * S + tt * 128 + 128],
                    wa.ap()[:, c * 8:c * 8 + 8], start=(c == 0), stop=(c == NCH - 1)))
            tmm = p.ck("tensor")
            p.wait("vector", tmm)
            p.on("vector", lambda v, bi=bi: v.tensor_copy(tA[:, 0:8], ps[bi][0:128, 0:8]))
            p.ckw("vector")
            p.on("vector", lambda v: v.reduce_max(sc2[:, 0:1], tA[:, 0:8], axis=AX.X))
            p.ckw("vector")
            p.on("vector", lambda v, tt=tt: v.tensor_scalar(tA[:, 8:16], tA[:, 0:8], sc2[:, 0:1], None, OP.is_equal))
            p.ckw("vector")
            p.on("vector", lambda v, tt=tt: v.tensor_mul(tA[:, 16:24], tA[:, 8:16], tB[:, 0:8]))
            p.ckw("vector")
            p.on("vector", lambda v, tt=tt: v.reduce_sum(seg_tm[:, tt:tt + 1], tA[:, 16:24], axis=AX.X))
            p.ckw("vector")
            p.on("vector", lambda v, tt=tt: v.tensor_copy(tF[:, tt * 8:tt * 8 + 8], tA[:, 8:16]))
            tv = p.ck("vector")
            bank_guard[bi] = tv
            last_tv = tv
            bi2 = 6 + (tt % 2)
            p.wait("tensor", bank_guard.get(bi2))
            for c in range(NCH):
                p.on("tensor", lambda te, c=c, tt=tt, bi2=bi2: te.matmul(
                    ps[bi2][0:128, 0:16], xnT[:, c * S + tt * 128:c * S + tt * 128 + 128],
                    wb.ap()[:, c * 16:c * 16 + 16], start=(c == 0), stop=(c == NCH - 1)))
            tmm2 = p.ck("tensor")
            p.wait("vector", tmm2)
            p.on("vector", lambda v, bi2=bi2: v.tensor_copy(tA[:, 32:48], ps[bi2][0:128, 0:16]))
            p.ckw("vector")
            p.on("vector", lambda v: v.reduce_max(sc3[:, 0:1], tA[:, 32:48], axis=AX.X))
            p.ckw("vector")
            p.on("vector", lambda v, tt=tt: v.tensor_scalar(ohR_tm[:, tt * 16:tt * 16 + 16], tA[:, 32:48], sc3[:, 0:1], None, OP.is_equal))
            tv2 = p.ck("vector")
            bank_guard[bi2] = tv2
            p.wait("scalar", tv2)
            p.on("scalar", lambda sx, tt=tt: sx.activation(wtok_tm[:, tt:tt + 1], sc3[:, 0:1], AF.Sigmoid))
            tsg = p.ck("scalar")
            p.wait("vector", tsg)
        bi = 4
        p.wait("tensor", bank_guard.get(bi))
        p.wait("tensor", last_tv)
        for tt in range(NT):
            p.on("tensor", lambda te, tt=tt, bi=bi: te.matmul(
                ps[bi][0:1, 0:8], ones_col[:, 0:1], tF[:, tt * 8:tt * 8 + 8],
                start=(tt == 0), stop=(tt == NT - 1)))
        tmm = p.ck("tensor")
        p.wait("vector", tmm)
        p.on("vector", lambda v, bi=bi: v.tensor_copy(rrow[0:1, 1600:1608], ps[bi][0:1, 0:8]))
        tcv = p.ck("vector")
        bank_guard[bi] = tcv
        p.wait("tensor", tcv)
        bi = 5
        p.wait("tensor", bank_guard.get(bi))
        p.on("tensor", lambda te, bi=bi: te.matmul(ps[bi][0:128, 0:8], ones_row[0:1, 0:128],
                                                   rrow[0:1, 1600:1608], start=True, stop=True))
        tmm = p.ck("tensor")
        p.wait("vector", tmm)
        p.on("vector", lambda v, bi=bi: v.tensor_copy(tA[:, 0:8], ps[bi][0:128, 0:8]))
        tbc = p.ck("vector")
        bank_guard[bi] = tbc
        for tt in range(NT):
            p.on("vector", lambda v, tt=tt: v.tensor_mul(tB[:, 32:40], tF[:, tt * 8:tt * 8 + 8], tA[:, 0:8]))
            p.ckw("vector")
            p.on("vector", lambda v, tt=tt: v.reduce_sum(sc2[:, tt:tt + 1], tB[:, 32:40], axis=AX.X))
            p.ckw("vector")
            p.on("vector", lambda v, tt=tt: v.tensor_scalar(sc2[:, tt:tt + 1], sc2[:, tt:tt + 1], 1.0, None, OP.is_equal))
            tv = p.ckw("vector")
        p.wait("tensor", tv)
        for (srccol, dst, dstoff) in ((seg_tm, rrow, 0), (sc2, brow, 0)):
            for tt in range(NT):
                bi = 6 + (tt % 2)
                p.wait("tensor", bank_guard.get(bi))
                p.on("tensor", lambda te, srccol=srccol, tt=tt, bi=bi: te.transpose(
                    ps[bi][0:1, 0:128], srccol[:, tt:tt + 1], ident[:, :]))
                tmm = p.ck("tensor")
                p.wait("vector", tmm)
                p.on("vector", lambda v, tt=tt, bi=bi, dst=dst, dstoff=dstoff: v.tensor_copy(
                    dst[0:1, dstoff + tt * 128:dstoff + tt * 128 + 128], ps[bi][0:1, 0:128]))
                bank_guard[bi] = p.ck("vector")
        for (srcrow, off, dstt) in ((rrow, 0, segq_bc), (brow, 0, sing_bc)):
            for qs in range(4):
                bi = 4 + (qs % 2)
                p.wait("tensor", bank_guard.get(bi))
                p.on("tensor", lambda te, srcrow=srcrow, off=off, qs=qs, bi=bi: te.matmul(
                    ps[bi][0:128, 0:512], ones_row[0:1, 0:128],
                    srcrow[0:1, off + qs * 512:off + qs * 512 + 512], start=True, stop=True))
                tmm = p.ck("tensor")
                p.wait("vector", tmm)
                p.on("vector", lambda v, qs=qs, bi=bi, dstt=dstt: v.tensor_copy(
                    dstt[:, qs * 512:qs * 512 + 512], ps[bi][0:128, 0:512]))
                bank_guard[bi] = p.ck("vector")
        p.barrier()

    def layer0():
        emit_ln_and_transpose(store_xn_dram=True)
        l0_seg_and_router()
        emit_mha(d_w["lt_w_in"], lambda h: (64 * h, 1024 + 64 * h, 2048 + 64 * h),
                 din["L0_attn_lt_b_in"], d_w["lt_w_out"], din["L0_attn_lt_b_out"],
                 causal=False, scale=0.125, proj_fm_dst=(d_comb, 0))
        if l0_stop == "long":
            return

        def st_mask(kc, qs, ebuf):
            p.on("vector", lambda v, kc=kc, qs=qs, ebuf=ebuf: v.scalar_tensor_tensor(
                ebuf, segq_bc[:, qs * 512:qs * 512 + 512], seg_tm[:, kc:kc + 1], ebuf,
                OP.is_equal, OP.mult))

        def st_where(oc, qs, tile_ap, tok):
            p.wait("vector", tok)
            p.on("vector", lambda v, oc=oc, qs=qs, tile_ap=tile_ap: v.tensor_sub(
                tD[:, 0:512], xnT[:, oc * S + qs * 512:oc * S + qs * 512 + 512], tile_ap))
            p.on("vector", lambda v, qs=qs: v.tensor_mul(
                tD[:, 0:512], tD[:, 0:512], sing_bc[:, qs * 512:qs * 512 + 512]))
            p.on("vector", lambda v, tile_ap=tile_ap: v.tensor_add(tile_ap, tile_ap, tD[:, 0:512]))
            return p.ck("vector")

        emit_mha(d_w["st_w_in"], lambda h: (64 * h, 1024 + 64 * h, 2048 + 64 * h),
                 din["L0_attn_st_b_in"], d_w["st_w_out"], din["L0_attn_st_b_out"],
                 causal=False, scale=0.125, mask_fm=st_mask,
                 proj_fm_dst=(d_comb, 1024), proj_post_evac=st_where)
        proj_from_fm_dram(d_comb, d_w["fus_w"], din["L0_attn_fus_b"], d_s, nch_in=16)
        l0_ffn_dense_and_combine()

    def l0_ffn_dense_and_combine():
        for e in range(16):
            tokb1 = p.dma("sync", b1col[:, 0:16],
                          din["L0_ffn_b1"][e].rearrange("(c p) -> p c", p=128),
                          allow_slow_non_contiguous=True)
            tokb2 = p.dma("sync", brow[0:1, 0:1024], din["L0_ffn_b2"][e].unsqueeze(0))
            p.wait("scalar", tokb1)
            p.wait("tensor", tokb2)
            for hc in range(16):
                bufw = wa if hc % 2 == 0 else wb
                tokw = p.dma("sync", bufw.ap()[:, 0:1024].rearrange("p (c o) -> p c o", c=8),
                             din["L0_ffn_w1"][e].rearrange("(c r) o -> r c o", r=128)[:, :, ts(hc)])
                p.wait("tensor", tokw)
                for qs in range(4):
                    bi = qs % 4
                    p.wait("tensor", bank_guard.get(bi))
                    for c in range(NCH):
                        p.on("tensor", lambda te, c=c, qs=qs, bi=bi, bufw=bufw: te.matmul(
                            ps[bi][:, 0:512], bufw.ap()[:, c * 128:c * 128 + 128],
                            xnT[:, c * S + qs * 512:c * S + qs * 512 + 512],
                            start=(c == 0), stop=(c == NCH - 1)))
                    tmm = p.ck("tensor")
                    p.wait("scalar", tmm)
                    p.on("scalar", lambda sx, bi=bi, hc=hc: sx.activation(
                        tE[:, 0:512], ps[bi][:, 0:512], AF.Gelu, bias=b1col[:, hc:hc + 1], scale=1.0))
                    tev = p.ck("scalar")
                    bank_guard[bi] = tev
                    p.wait("gpsimd", tev)
                    tok_st = p.dma("gpsimd", d_t1[ts(hc), qs * 512:qs * 512 + 512], tE[:, 0:512])
                    p.wait("scalar", tok_st)
                tte = p.ck("tensor") if p._dirty["tensor"] else tmm
                p.wait("sync", tte)
            for tg in range(4):
                for hc in range(16):
                    bufw = wa if hc % 2 == 0 else wb
                    tokw = p.dma("sync", bufw[0:128, 0:1024], din["L0_ffn_w2"][e][ts(hc), :])
                    tokh = p.dma("sync", tC[:, 0:512], d_t1[ts(hc), tg * 512:tg * 512 + 512])
                    p.wait("tensor", tokw)
                    p.wait("tensor", tokh)
                    for tt4 in range(4):
                        for os_ in range(2):
                            bi = tt4 * 2 + os_
                            if hc == 0:
                                p.wait("tensor", bank_guard.get(bi))
                            p.on("tensor", lambda te, tt4=tt4, os_=os_, bi=bi, hc=hc, bufw=bufw:
                                 te.matmul(ps[bi][:, 0:512], tC[:, tt4 * 128:tt4 * 128 + 128],
                                           bufw.ap()[:, os_ * 512:os_ * 512 + 512],
                                           start=(hc == 0), stop=False))
                    tmm = p.ck("tensor")
                    p.wait("sync", tmm)
                for tt4 in range(4):
                    tt = tg * 4 + tt4
                    for os_ in range(2):
                        bi = tt4 * 2 + os_
                        p.on("tensor", lambda te, bi=bi, os_=os_: te.matmul(
                            ps[bi][:, 0:512], ones_row[0:1, 0:128],
                            brow[0:1, os_ * 512:os_ * 512 + 512], start=False, stop=True))
                        tmm = p.ck("tensor")
                        p.wait("vector", tmm)
                        p.on("vector", lambda v, bi=bi, os_=os_, tt=tt, e=e: v.tensor_scalar(
                            tF[:, bi * 256:bi * 256 + 512] if False else tF[:, (bi % 4) * 512:(bi % 4) * 512 + 512],
                            ps[bi][:, 0:512], ohR_tm[:, tt * 16 + e:tt * 16 + e + 1], None, OP.mult))
                        tev = p.ck("vector")
                        bank_guard[bi] = tev
                        p.wait("gpsimd", tev)
                        tok_st = p.dma("gpsimd", d_t2[ts(tt), os_ * 512:os_ * 512 + 512],
                                       tF[:, (bi % 4) * 512:(bi % 4) * 512 + 512],
                                       accum_op=(OP.add if e > 0 else OP.bypass))
                        p.wait("vector", tok_st)
        p.barrier()
        for tt in range(NT):
            tokg = p.dma("sync", tB[:, 0:1024], d_t2[ts(tt), 0:1024])
            tok_xx = p.dma("sync", tC[:, 0:1024], d_x[ts(tt), :])
            tok_ss = p.dma("sync", tC[:, 1024:2048], d_s[ts(tt), :])
            p.wait("vector", tokg)
            p.on("vector", lambda v, tt=tt: v.tensor_scalar(tB[:, 0:1024], tB[:, 0:1024],
                                                            wtok_tm[:, tt:tt + 1], None, OP.mult))
            tv = p.ck("vector")
            p.wait("scalar", tv)
            p.on("scalar", lambda sx: sx.activation(tB[:, 1024:2048], tB[:, 0:1024], AF.Silu))
            tsl = p.ck("scalar")
            p.wait("vector", tsl)
            p.wait("vector", tok_xx)
            p.wait("vector", tok_ss)
            p.on("vector", lambda v: v.tensor_mul(tB[:, 1024:2048], tB[:, 1024:2048], tC[:, 1024:2048]))
            p.on("vector", lambda v: v.tensor_add(tC[:, 0:1024], tC[:, 0:1024], tB[:, 1024:2048]))
            tcb = p.ck("vector")
            p.wait("gpsimd", tcb)
            tok_st = p.dma("gpsimd", d_x[ts(tt), :], tC[:, 0:1024])
            p.wait("sync", tok_st)
            p.wait("scalar", tcb)
        p.barrier()

    L2_D4 = [0.4829629131445341, 0.8365163037378079, 0.2241438680420134, -0.1294095225512604]
    L2_OFFS = sorted({(1 << j) * tau for j in range(11) for tau in range(4) if (1 << j) * tau < S})
    L2_TERMS = {d_: [(j, tau) for j in range(11) for tau in range(1, 4) if (1 << j) * tau == d_]
                for d_ in L2_OFFS if d_ > 0}

    def proj_fm_xnT(w_inmajor, col0, bias_dram, dst_fm, dout=1024, evac_act=AF.Identity):
        if bias_dram is not None:
            tb = p.dma("sync", sc1[:, 0:dout // 128],
                       bias_dram.rearrange("(c p) -> p c", p=128), allow_slow_non_contiguous=True)
            p.wait("scalar", tb)
        prev_te = None
        for oc in range(dout // 128):
            bufw = wa if oc % 2 == 0 else wb
            if prev_te is not None:
                p.wait("sync", prev_te)
            tokw = p.dma("sync", bufw.ap()[:, 0:1024].rearrange("p (c o) -> p c o", c=8),
                         w_inmajor.rearrange("(c r) o -> r c o", r=128)[:, :, col0 + oc * 128:col0 + oc * 128 + 128])
            p.wait("tensor", tokw)
            for qs in range(4):
                bi = qs % 2
                p.wait("tensor", bank_guard.get(bi))
                for c in range(NCH):
                    p.on("tensor", lambda te, c=c, qs=qs, bi=bi, bufw=bufw:
                         te.matmul(ps[bi][:, 0:512], bufw.ap()[:, c * 128:c * 128 + 128],
                                   xnT[:, c * S + qs * 512:c * S + qs * 512 + 512],
                                   start=(c == 0), stop=(c == NCH - 1)))
                tmm = p.ck("tensor")
                prev_te = tmm
                p.wait("scalar", tmm)
                if bias_dram is not None:
                    p.on("scalar", lambda sx, bi=bi, oc=oc, evac_act=evac_act: sx.activation(
                        tB[:, 0:512], ps[bi][:, 0:512], evac_act, bias=sc1[:, oc:oc + 1], scale=1.0))
                else:
                    p.on("scalar", lambda sx, bi=bi, evac_act=evac_act: sx.activation(
                        tB[:, 0:512], ps[bi][:, 0:512], evac_act))
                tev = p.ck("scalar")
                bank_guard[bi] = tev
                p.wait("gpsimd", tev)
                tok_st = p.dma("gpsimd", dst_fm[oc * 128:oc * 128 + 128, qs * 512:qs * 512 + 512], tB[:, 0:512])
                p.wait("scalar", tok_st)
        p.barrier()

    def proj_tm_xnT(w_inmajor, col0, bias_dram, dst_tm, dout=1024, evac_act=None,
                    tile_hook=None):
        if bias_dram is not None:
            tb = p.dma("sync", brow[0:1, 0:dout], bias_dram.unsqueeze(0)[0:1, col0:col0 + dout]
                       if False else bias_dram.unsqueeze(0)[0:1, 0:dout])
            p.wait("tensor", tb)
        nos = dout // 512
        wtoks = []
        for os_ in range(nos):
            buf = wa if os_ == 0 else wb
            wtoks.append(p.dma("sync", buf.ap()[:, 0:4096].rearrange("p (c o) -> p c o", c=8),
                               w_inmajor.rearrange("(c r) o -> r c o", r=128)[
                                   :, :, col0 + os_ * 512:col0 + os_ * 512 + 512]))
        p.wait("tensor", wtoks)
        for tt in range(NT):
            evs = []
            for os_ in range(nos):
                bi = (tt % 2) * 2 + os_
                buf = wa if os_ == 0 else wb
                p.wait("tensor", bank_guard.get(bi))
                for c in range(NCH):
                    p.on("tensor", lambda te, c=c, tt=tt, bi=bi, buf=buf:
                         te.matmul(ps[bi][:, 0:512], xnT[:, c * S + tt * 128:c * S + tt * 128 + 128],
                                   buf.ap()[:, c * 512:c * 512 + 512], start=(c == 0), stop=False))
                if bias_dram is not None:
                    p.on("tensor", lambda te, bi=bi, os_=os_: te.matmul(
                        ps[bi][:, 0:512], ones_row[0:1, 0:128],
                        brow[0:1, os_ * 512:os_ * 512 + 512], start=False, stop=True))
                else:
                    p.on("tensor", lambda te, bi=bi, os_=os_: te.matmul(
                        ps[bi][:, 0:512], ones_row[0:1, 0:128],
                        brow[0:1, os_ * 512:os_ * 512 + 512], start=False, stop=True))
                tmm = p.ck("tensor")
                eng = "vector" if evac_act is None else "scalar"
                p.wait(eng, tmm)
                if evac_act is None:
                    p.on("vector", lambda v, bi=bi, os_=os_: v.tensor_copy(tB[:, os_ * 512:os_ * 512 + 512], ps[bi][:, 0:512]))
                else:
                    p.on("scalar", lambda sx, bi=bi, os_=os_, evac_act=evac_act: sx.activation(
                        tB[:, os_ * 512:os_ * 512 + 512], ps[bi][:, 0:512], evac_act))
                tev = p.ck(eng)
                bank_guard[bi] = tev
                evs.append(tev)
            if tile_hook is not None:
                tev = tile_hook(tt, tB[:, 0:dout], evs)
                evs = [tev]
            p.wait("gpsimd", evs)
            tok_st = p.dma("gpsimd", dst_tm[ts(tt), 0:dout], tB[:, 0:dout])
            p.wait("scalar", tok_st)
            p.wait("vector", tok_st)
        p.barrier()

    def l2_attn():
        proj_fm_xnT(d_w["l2_qkv"], 0, din["L2_attn_b_qkv"][0:1024], d_q2)
        proj_fm_xnT(d_w["l2_qkv"], 1024, din["L2_attn_b_qkv"][1024:2048], d_k2)
        proj_tm_xnT(d_w["l2_qkv"], 2048, din["L2_attn_b_qkv"][2048:3072], d_v2)
        p.on("gpsimd", lambda g: g.iota(tA[:, 0:16], [[1, 16]], base=0, channel_multiplier=0,
                                        allow_small_or_imprecise_dtypes=True))
        p.on("gpsimd", lambda g: g.iota(sc3[:, 0:1], [[0, 1]], base=0, channel_multiplier=1,
                                        allow_small_or_imprecise_dtypes=True))
        tgp = p.ck("gpsimd")
        p.wait("vector", tgp)
        p.on("vector", lambda v: v.tensor_scalar(sc3[:, 1:2], sc3[:, 0:1], 63.5, None, OP.is_gt))
        p.ckw("vector")
        for c in range(8):
            p.on("vector", lambda v, c=c: v.tensor_scalar(sc3[:, 2:3], sc3[:, 1:2], float(2 * c), None, OP.add))
            p.ckw("vector")
            p.on("vector", lambda v, c=c: v.tensor_scalar(tF[:, c * 16:c * 16 + 16], tA[:, 0:16],
                                                          sc3[:, 2:3], None, OP.is_equal))
            tE16 = p.ckw("vector")
        p.wait("tensor", tE16)
        tpr = p.dma("sync", sc2[0:16, 0:11], din["L2_attn_scale_gain"].rearrange("j h -> h j"),
                    allow_slow_non_contiguous=True)
        tqs = p.dma("sync", tA[0:64, 32:43], d_w["l2_qscale"][0:64, 0:11])
        tqs2 = p.dma("sync", tA[64:128, 32:43], d_w["l2_qscale"][0:64, 0:11])
        p.wait("vector", tqs2)
        p.wait("vector", tpr)
        tib = p.dma("sync", sc2[0:16, 11:12], din["L2_attn_identity_bypass"].unsqueeze(1))
        p.wait("scalar", tib)
        p.on("scalar", lambda sx: sx.activation(sc2[0:16, 12:13], sc2[0:16, 11:12], AF.Exp))
        tso = p.ck("scalar")
        p.wait("vector", tso)
        p.on("vector", lambda v: v.tensor_scalar(sc2[0:16, 12:13], sc2[0:16, 12:13], 1.0, None, OP.add))
        tso2 = p.ck("vector")
        p.wait("scalar", tso2)
        p.on("scalar", lambda sx: sx.activation(sc2[0:16, 12:13], sc2[0:16, 12:13], AF.Ln))
        tso = p.ck("scalar")
        p.wait("vector", tso)
        qkbuf = tE
        for qs in range(4):
            p.wait("sync", ("vector", p.cnt["vector"]))
            tokq = p.dma("sync", wa[0:128, 0:4096],
                         d_q2.rearrange("(c r) n -> r c n", r=128)[:, :, qs * 512:qs * 512 + 512])
            p.wait("vector", tokq)
            for j in range(11):
                bi = 4 + (j % 2)
                p.wait("tensor", bank_guard.get(bi))
                tvp = None
                for c in range(NCH):
                    p.on("vector", lambda v, c=c, j=j: v.tensor_scalar(
                        tB[:, (c % 2) * 512:(c % 2) * 512 + 512], wa[0:128, c * 512:c * 512 + 512],
                        tA[:, 32 + j:33 + j], None, OP.mult))
                    tvp = p.ck("vector")
                    p.wait("tensor", tvp)
                    p.on("tensor", lambda te, c=c, j=j, bi=bi: te.matmul(
                        ps[bi][0:16, 0:512], tF[:, c * 16:c * 16 + 16],
                        tB[:, (c % 2) * 512:(c % 2) * 512 + 512], start=(c == 0), stop=(c == NCH - 1)))
                    if c < NCH - 1:
                        tmm_g = p.ck("tensor")
                        p.wait("vector", tmm_g)
                tmm = p.ck("tensor")
                p.wait("scalar", tmm)
                p.on("scalar", lambda sx, j=j, bi=bi: sx.activation(
                    ti32.ap().bitcast(F32)[0:16, 0:2048][0:16, 0:512] if False else
                    qg[0:16, j * 512:j * 512 + 512], ps[bi][0:16, 0:512], AF.Identity,
                    bias=sc2[0:16, j:j + 1], scale=1.0))
                tev = p.ck("scalar")
                bank_guard[bi] = tev
            p.wait("vector", tev)
            p.wait("scalar", tev)
            p.on("scalar", lambda sx: sx.activation(qg[0:16, 0:5632], qg[0:16, 0:5632], AF.Exp))
            tse = p.ck("scalar")
            p.wait("vector", tse)
            p.on("vector", lambda v: v.tensor_copy(tB[0:16, 0:512], qg[0:16, 0:512]))
            for j in range(1, 11):
                p.on("vector", lambda v, j=j: v.tensor_add(tB[0:16, 0:512], tB[0:16, 0:512],
                                                           qg[0:16, j * 512:j * 512 + 512]))
            p.on("vector", lambda v: v.reciprocal(tB[0:16, 512:1024], tB[0:16, 0:512]))
            for j in range(11):
                p.on("vector", lambda v, j=j: v.tensor_mul(qg[0:16, j * 512:j * 512 + 512],
                                                           qg[0:16, j * 512:j * 512 + 512], tB[0:16, 512:1024]))
            tgn = p.ck("vector")
            p.wait("gpsimd", tgn)
            for j in range(11):
                p.dma("gpsimd", d_gains[0:16, j * S + qs * 512:j * S + qs * 512 + 512],
                      qg[0:16, j * 512:j * 512 + 512])
            for di, d_ in enumerate(L2_OFFS):
                lo = qs * 512 - d_
                tokk = None
                if lo + 512 > 0:
                    a0 = max(0, lo)
                    p.wait("sync", ("vector", p.cnt["vector"]))
                    tokk = p.dma("sync", wb.ap()[:, 0:4096].rearrange("p (c n) -> p c n", c=8)[
                        :, :, (a0 - lo):512],
                        d_k2.rearrange("(c r) n -> r c n", r=128)[:, :, a0:lo + 512])
                if lo < 0:
                    zer = min(-lo, 512)
                    p.on("vector", lambda v, zer=zer: v.memset(
                        wb.ap()[:, 0:4096].rearrange("p (c n) -> p c n", c=8)[:, :, 0:zer], 0.0))
                tz = p.ck("vector") if lo < 0 else None
                p.wait("vector", tokk)
                bi = 4 + (di % 2)
                p.wait("tensor", bank_guard.get(bi))
                p.wait("tensor", tz)
                tvp = None
                for c in range(NCH):
                    p.on("vector", lambda v, c=c: v.tensor_mul(
                        tB[:, (c % 2) * 512:(c % 2) * 512 + 512], wa[0:128, c * 512:c * 512 + 512],
                        wb[0:128, c * 512:c * 512 + 512]))
                    tvp = p.ck("vector")
                    p.wait("tensor", tvp)
                    p.on("tensor", lambda te, c=c, bi=bi: te.matmul(
                        ps[bi][0:16, 0:512], tF[:, c * 16:c * 16 + 16],
                        tB[:, (c % 2) * 512:(c % 2) * 512 + 512], start=(c == 0), stop=(c == NCH - 1)))
                    if c < NCH - 1:
                        tmm_g = p.ck("tensor")
                        p.wait("vector", tmm_g)
                tmm = p.ck("tensor")
                p.wait("vector", tmm)
                p.on("vector", lambda v, bi=bi: v.tensor_copy(tB[0:16, 1024:1536], ps[bi][0:16, 0:512]))
                tqk = p.ck("vector")
                bank_guard[bi] = tqk
                p.wait("vector", ("tensor", p.cnt["tensor"]))
                if d_ == 0:
                    p.on("vector", lambda v: v.tensor_scalar(sc3[0:16, 3:4], sc2[0:16, 12:13],
                                                             L2_D4[0], None, OP.add))
                    p.ckw("vector")
                    p.on("vector", lambda v: v.tensor_scalar(tB[0:16, 1536:2048], tB[0:16, 1024:1536],
                                                             sc3[0:16, 3:4], None, OP.mult))
                else:
                    terms = L2_TERMS[d_]
                    j0, tau0 = terms[0]
                    p.on("vector", lambda v, j0=j0, tau0=tau0: v.tensor_scalar(
                        tB[0:16, 512:1024], qg[0:16, j0 * 512:j0 * 512 + 512], L2_D4[tau0], None, OP.mult))
                    for (j1, tau1) in terms[1:]:
                        p.on("vector", lambda v, j1=j1, tau1=tau1: v.scalar_tensor_tensor(
                            tB[0:16, 512:1024], qg[0:16, j1 * 512:j1 * 512 + 512], L2_D4[tau1],
                            tB[0:16, 512:1024], OP.mult, OP.add))
                    p.on("vector", lambda v: v.tensor_mul(tB[0:16, 1536:2048], tB[0:16, 1024:1536],
                                                          tB[0:16, 512:1024]))
                twd = p.ck("vector")
                p.wait("tensor", twd)
                for t4 in range(4):
                    bi2 = 6 + (t4 % 2)
                    p.wait("tensor", bank_guard.get(bi2))
                    p.on("tensor", lambda te, t4=t4, bi2=bi2: te.transpose(
                        ps[bi2][0:128, 0:16], tB[0:16, 1536 + t4 * 128:1536 + t4 * 128 + 128],
                        ident[0:16, 0:16]))
                    tmm2 = p.ck("tensor")
                    p.wait("vector", tmm2)
                    p.on("vector", lambda v, t4=t4, bi2=bi2, di=di: v.tensor_copy(
                        tC[:, di * 64 + t4 * 16:di * 64 + t4 * 16 + 16], ps[bi2][0:128, 0:16]))
                    bank_guard[bi2] = p.ck("vector")
            p.wait("gpsimd", bank_guard.get(6))
            p.wait("gpsimd", bank_guard.get(7))
            tok_sds = []
            for t4 in range(4):
                tok_sd = p.dma("gpsimd",
                               d_sd[qs * 512 + t4 * 128:qs * 512 + t4 * 128 + 128, 0:352].rearrange(
                                   "p (dd c) -> p dd c", dd=22),
                               tC.ap()[:, 0:22 * 64].rearrange("p (dd tc) -> p dd tc", dd=22)[
                                   :, :, t4 * 16:t4 * 16 + 16])
                tok_sds.append(tok_sd)
            p.wait("vector", tok_sds)
            p.wait("sync", tok_sds)
        p.barrier()
        for tt in range(NT):
            tok_s = p.dma("sync", tC[:, 0:352], d_sd[ts(tt), 0:352])
            p.wait("vector", tok_s)
            first = True
            for di, d_ in enumerate(L2_OFFS):
                lo = tt * 128 - d_
                if lo + 128 <= 0:
                    continue
                a0 = max(0, lo)
                tok_v = p.dma("sync", tB[(a0 - lo):128, 0:1024], d_v2[a0:lo + 128, 0:1024])
                p.wait("vector", tok_v)
                if lo < 0:
                    p.on("vector", lambda v, lo=lo: v.memset(tB[0:-lo, 0:1024], 0.0))
                sbc = tC.ap()[:, di * 16:di * 16 + 16] if False else None
                if first:
                    p.on("vector", lambda v, di=di: v.tensor_tensor(
                        tD.ap()[:, 0:1024].rearrange("p (h e) -> p h e", h=16),
                        tB.ap()[:, 0:1024].rearrange("p (h e) -> p h e", h=16),
                        tC.ap()[:, di * 16:di * 16 + 16].unsqueeze(2).to_broadcast([128, 16, 64]),
                        OP.mult))
                else:
                    p.on("vector", lambda v, di=di: v.tensor_mul(
                        tE.ap()[:, 0:1024].rearrange("p (h e) -> p h e", h=16),
                        tB.ap()[:, 0:1024].rearrange("p (h e) -> p h e", h=16),
                        tC.ap()[:, di * 16:di * 16 + 16].unsqueeze(2).to_broadcast([128, 16, 64])))
                    p.on("vector", lambda v: v.tensor_add(tD[:, 0:1024], tD[:, 0:1024], tE[:, 0:1024]))
                first = False
                tacc = p.ck("vector")
                p.wait("sync", tacc)
            p.wait("gpsimd", tacc)
            tok_g = p.dma("gpsimd", d_t1[ts(tt), 0:1024], tD[:, 0:1024])
            p.wait("vector", tok_g)
        p.barrier()

        def l2_gate_hook(tt, tile_ap, evs):
            p.wait("sync", ("tensor", p.cnt["tensor"]))
            tokg = p.dma("sync", tD[:, 0:1024], d_t1[ts(tt), 0:1024])
            p.wait("vector", tokg)
            p.wait("vector", evs)
            p.on("vector", lambda v, tile_ap=tile_ap: v.tensor_mul(tD[:, 0:1024], tD[:, 0:1024], tile_ap))
            tu = p.ck("vector")
            p.wait("tensor", tu)
            for c in range(NCH):
                bi = 6 + (c % 2)
                p.wait("tensor", bank_guard.get(bi))
                p.on("tensor", lambda te, c=c, bi=bi: te.transpose(
                    ps[bi][0:128, 0:128], tD[:, ts(c)], ident[:, :]))
                tmm = p.ck("tensor")
                p.wait("vector", tmm)
                p.on("vector", lambda v, c=c, bi=bi: v.tensor_copy(tE[:, c * 128:c * 128 + 128], ps[bi][0:128, 0:128]))
                bank_guard[bi] = p.ck("vector")
            tev = bank_guard[6 + ((NCH - 1) % 2)]
            p.wait("gpsimd", tev)
            tok_u = p.dma("gpsimd", d_comb[0:1024, ts(tt)].rearrange("(c q) t -> q c t", q=128),
                          tE.ap()[:, 0:1024].rearrange("q (c t) -> q c t", c=8))
            p.wait("vector", tok_u)
            return tev

        proj_tm_xnT(d_w["l2_gate"], 0, din["L2_attn_b_gate"], d_t2, dout=1024,
                    evac_act=AF.Sigmoid, tile_hook=l2_gate_hook)
        if l0_stop == "l2gate":
            return True
        proj_from_fm_dram(d_comb, d_w["l2_out"], din["L2_attn_b_out"], d_s, nch_in=8)

    def l2_ffn_and_combine():
        _stop = l0_stop
        tokw = p.dma("sync", wa[0:128, 0:64].rearrange("p (c o) -> p c o", c=8),
                     d_w["l2f_gw"].rearrange("(c r) o -> r c o", r=128))
        p.wait("tensor", tokw)
        for tt in range(NT):
            bi = 4 + (tt % 2)
            p.wait("tensor", bank_guard.get(bi))
            for c in range(NCH):
                p.on("tensor", lambda te, c=c, tt=tt, bi=bi: te.matmul(
                    ps[bi][0:128, 0:8], xnT[:, c * S + tt * 128:c * S + tt * 128 + 128],
                    wa.ap()[:, c * 8:c * 8 + 8], start=(c == 0), stop=(c == NCH - 1)))
            tmm = p.ck("tensor")
            p.wait("vector", tmm)
            p.on("vector", lambda v, bi=bi: v.tensor_copy(tA[:, 0:8], ps[bi][0:128, 0:8]))
            p.ckw("vector")
            p.on("vector", lambda v: v.reduce_max(sc3[:, 0:1], tA[:, 0:8], axis=AX.X))
            p.ckw("vector")
            p.on("vector", lambda v: v.tensor_scalar(sc3[:, 1:2], sc3[:, 0:1], -1.0, None, OP.mult))
            tvx = p.ck("vector")
            bank_guard[bi] = tvx
            p.wait("scalar", tvx)
            p.wait("scalar", ("tensor", p.cnt["tensor"]))
            p.on("scalar", lambda sx: sx.activation(tA[:, 8:16], tA[:, 0:8], AF.Exp,
                                                    bias=sc3[:, 1:2], scale=1.0, accum_out=accA[:, 0:1]))
            tse = p.ck("scalar")
            p.wait("vector", tse)
            p.on("vector", lambda v: v.reciprocal(sc3[:, 3:4], accA[:, 0:1]))
            p.ckw("vector")
            p.on("vector", lambda v: v.tensor_scalar(tA[:, 8:16], tA[:, 8:16], sc3[:, 3:4], None, OP.mult))
            tg8 = p.ckw("vector")
            p.wait("tensor", tg8)
            bi2 = 6 + (tt % 2)
            p.wait("tensor", bank_guard.get(bi2))
            p.on("tensor", lambda te, bi2=bi2: te.transpose(ps[bi2][0:8, 0:128], tA[:, 8:16], ident[:, :]))
            tmm2 = p.ck("tensor")
            p.wait("vector", tmm2)
            p.on("vector", lambda v, tt=tt, bi2=bi2: v.tensor_copy(qg[0:8, tt * 128:tt * 128 + 128],
                                                                   ps[bi2][0:8, 0:128]))
            bank_guard[bi2] = p.ck("vector")
            if l0_stop == "l2g8" and tt == 0:
                return True
        tglast = bank_guard[6 + ((NT - 1) % 2)]
        p.on("gpsimd", lambda g: g.iota(tB[0:8, 0:128], [[1, 128]], base=0, channel_multiplier=0,
                                        allow_small_or_imprecise_dtypes=True))
        p.on("gpsimd", lambda g: g.iota(tB[0:8, 128:129], [[0, 1]], base=0, channel_multiplier=16,
                                        allow_small_or_imprecise_dtypes=True))
        tgp = p.ck("gpsimd")
        p.wait("vector", tgp)
        p.on("vector", lambda v: v.tensor_scalar(tB[0:8, 256:384], tB[0:8, 0:128], tB[0:8, 128:129], None, OP.is_ge))
        p.ckw("vector")
        p.on("vector", lambda v: v.tensor_scalar(tB[0:8, 128:129], tB[0:8, 128:129], 16.0, None, OP.add))
        p.ckw("vector")
        p.on("vector", lambda v: v.tensor_scalar(tB[0:8, 384:512], tB[0:8, 0:128], tB[0:8, 128:129], None, OP.is_lt))
        p.ckw("vector")
        p.on("vector", lambda v: v.tensor_mul(tB[0:8, 0:128], tB[0:8, 256:384], tB[0:8, 384:512]))
        p.ckw("vector")
        p.on("vector", lambda v: v.tensor_scalar(tB[0:8, 0:128], tB[0:8, 0:128], 1.0 / 16.0, None, OP.mult))
        tE8 = p.ckw("vector")
        p.wait("tensor", tE8)
        p.wait("tensor", tglast)
        for qs in range(4):
            bi = 4 + (qs % 2)
            p.wait("tensor", bank_guard.get(bi))
            p.on("tensor", lambda te, qs=qs, bi=bi: te.matmul(
                ps[bi][0:128, 0:512], tB[0:8, 0:128], qg[0:8, qs * 512:qs * 512 + 512],
                start=True, stop=True))
            tmm = p.ck("tensor")
            p.wait("vector", tmm)
            p.on("vector", lambda v, qs=qs, bi=bi: v.tensor_copy(tD[:, qs * 512:qs * 512 + 512],
                                                                 ps[bi][0:128, 0:512]))
            bank_guard[bi] = p.ck("vector")
        p.wait("sync", ("tensor", p.cnt["tensor"]))
        tokU = p.dma("sync", wa[0:128, 0:1024].rearrange("p (c o) -> p c o", c=8),
                     d_w["l2f_U"].rearrange("(c r) o -> r c o", r=128))
        p.wait("tensor", tokU)
        for qs in range(4):
            bi = qs % 2
            p.wait("tensor", bank_guard.get(bi))
            for c in range(NCH):
                p.on("tensor", lambda te, c=c, qs=qs, bi=bi: te.matmul(
                    ps[bi][:, 0:512], wa.ap()[:, c * 128:c * 128 + 128],
                    xnT[:, c * S + qs * 512:c * S + qs * 512 + 512], start=(c == 0), stop=(c == NCH - 1)))
            tmm = p.ck("tensor")
            p.wait("vector", tmm)
            p.on("vector", lambda v, qs=qs, bi=bi: v.tensor_mul(
                tE[:, qs * 512:qs * 512 + 512], ps[bi][:, 0:512], tD[:, qs * 512:qs * 512 + 512]))
            bank_guard[bi] = p.ck("vector")
        tlast = bank_guard[1]
        tokV = p.dma("sync", wb[0:128, 0:1024], d_w["l2f_V"][0:128, 0:1024])
        p.wait("tensor", tokV)
        p.wait("tensor", tlast)
        p.wait("sync", tlast)
        for os_ in range(2):
            tokw = p.dma("sync", wa.ap()[:, 0:4096].rearrange("p (c o) -> p c o", c=8),
                         d_w["l2f_w"].rearrange("(c r) o -> r c o", r=128)[:, :, os_ * 512:os_ * 512 + 512])
            p.wait("tensor", tokw)
            last_tmm = None
            for tt in range(NT):
                bi = 4 + (tt % 2) * 2 + os_ % 2
                p.wait("tensor", bank_guard.get(bi))
                for c in range(NCH):
                    p.on("tensor", lambda te, c=c, tt=tt, bi=bi: te.matmul(
                        ps[bi][:, 0:512], xnT[:, c * S + tt * 128:c * S + tt * 128 + 128],
                        wa.ap()[:, c * 512:c * 512 + 512], start=(c == 0), stop=False))
                p.on("tensor", lambda te, tt=tt, bi=bi, os_=os_: te.matmul(
                    ps[bi][:, 0:512], tE[:, tt * 128:tt * 128 + 128],
                    wb.ap()[:, os_ * 512:os_ * 512 + 512], start=False, stop=True))
                tmm = p.ck("tensor")
                last_tmm = tmm
                p.wait("scalar", tmm)
                p.on("scalar", lambda sx, bi=bi: sx.activation(tC[:, 0:512], ps[bi][:, 0:512], AF.Silu))
                tev = p.ck("scalar")
                bank_guard[bi] = tev
                p.wait("gpsimd", tev)
                tok_st = p.dma("gpsimd", d_t2[ts(tt), os_ * 512:os_ * 512 + 512], tC[:, 0:512])
                p.wait("scalar", tok_st)
            p.wait("sync", last_tmm)
        p.barrier()
        for tt in range(NT):
            tokg = p.dma("sync", tB[:, 0:1024], d_t2[ts(tt), 0:1024])
            tok_xx = p.dma("sync", tC[:, 0:1024], d_x[ts(tt), :])
            tok_ss = p.dma("sync", tC[:, 1024:2048], d_s[ts(tt), :])
            p.wait("vector", tokg)
            p.wait("vector", tok_xx)
            p.wait("vector", tok_ss)
            p.on("vector", lambda v: v.tensor_mul(tB[:, 1024:2048], tB[:, 0:1024], tC[:, 1024:2048]))
            p.on("vector", lambda v: v.tensor_add(tC[:, 0:1024], tC[:, 0:1024], tB[:, 1024:2048]))
            tcb = p.ck("vector")
            p.wait("gpsimd", tcb)
            tok_st = p.dma("gpsimd", d_x[ts(tt), :], tC[:, 0:1024])
            p.wait("sync", tok_st)
        p.barrier()

    def layer2():
        emit_ln_and_transpose(store_xn_dram=True)
        if l2_attn():
            return
        if l2_ffn_and_combine():
            return

    FQ_LN2_HI = LN2_HI
    FQ_LN2_LO = LN2_LO
    INV_LN2 = 1.4426950408889634

    def emit_float_quantize(dst, src, n, exp_bits, man, rows=128):
        bias = 2 ** (exp_bits - 1) - 1
        min_exp = 1 - bias
        maxv = (2.0 - 2.0 ** (-man)) * 2.0 ** bias
        r_ = slice(0, rows)
        p.on("scalar", lambda sx, src=src: sx.activation(tB[r_, 0:n], src, AF.Abs))
        ts1 = p.ck("scalar")
        p.wait("vector", ts1)
        p.on("vector", lambda v: v.tensor_scalar(tB[r_, 0:n], tB[r_, 0:n], 1e-30, None, OP.max))
        tv1 = p.ck("vector")
        p.wait("scalar", tv1)
        p.on("scalar", lambda sx: sx.activation(tB[r_, 0:n], tB[r_, 0:n], AF.Ln))
        ts2 = p.ck("scalar")
        p.wait("vector", ts2)
        p.on("vector", lambda v: v.tensor_scalar(tB[r_, 0:n], tB[r_, 0:n], INV_LN2, None, OP.mult))
        p.on("vector", lambda v: v.tensor_copy(ti32[r_, 0:n], tB[r_, 0:n]))
        p.on("vector", lambda v: v.tensor_copy(tD[r_, 0:n], ti32[r_, 0:n]))
        p.on("vector", lambda v: v.tensor_sub(tB[r_, 0:n], tB[r_, 0:n], tD[r_, 0:n]))
        p.on("vector", lambda v: v.tensor_scalar(ti32[r_, 0:n].bitcast(F32), tB[r_, 0:n], 0.0, None, OP.is_lt))
        p.on("vector", lambda v: v.tensor_scalar(tB[r_, 0:n], tB[r_, 0:n], 1.0, None, OP.is_ge))
        p.on("vector", lambda v: v.tensor_sub(tD[r_, 0:n], tD[r_, 0:n], ti32[r_, 0:n].bitcast(F32)))
        p.on("vector", lambda v: v.tensor_add(tD[r_, 0:n], tD[r_, 0:n], tB[r_, 0:n]))
        p.on("vector", lambda v: v.tensor_scalar(tD[r_, 0:n], tD[r_, 0:n], float(min_exp), None, OP.max))
        p.on("vector", lambda v: v.tensor_scalar(tD[r_, 0:n], tD[r_, 0:n], -1.0, float(man), OP.mult, OP.add))
        tu = p.ck("vector")
        p.wait("scalar", tu)
        p.on("scalar", lambda sx: sx.activation(tB[r_, 0:n], tD[r_, 0:n], AF.Exp, scale=FQ_LN2_HI))
        ts3 = p.ck("scalar")
        p.wait("vector", ts3)
        p.on("vector", lambda v, src=src: v.tensor_mul(tB[r_, 0:n], tB[r_, 0:n], src))
        tv2 = p.ck("vector")
        p.wait("scalar", tv2)
        p.on("scalar", lambda sx: sx.activation(tE[r_, 0:n], tD[r_, 0:n], AF.Exp, scale=FQ_LN2_LO))
        ts4 = p.ck("scalar")
        p.wait("vector", ts4)
        p.on("vector", lambda v: v.tensor_mul(tB[r_, 0:n], tB[r_, 0:n], tE[r_, 0:n]))
        p.on("vector", lambda v: v.tensor_copy(ti32[r_, 0:n], tB[r_, 0:n]))
        p.on("vector", lambda v: v.tensor_copy(tE[r_, 0:n], ti32[r_, 0:n]))
        p.on("vector", lambda v: v.tensor_sub(tB[r_, 0:n], tB[r_, 0:n], tE[r_, 0:n]))
        p.on("vector", lambda v: v.tensor_scalar(ti32[r_, 0:n].bitcast(F32), tB[r_, 0:n], 0.5, None, OP.is_ge))
        p.on("vector", lambda v: v.tensor_scalar(tB[r_, 0:n], tB[r_, 0:n], -0.5, None, OP.is_lt))
        p.on("vector", lambda v: v.tensor_sub(tE[r_, 0:n], tE[r_, 0:n], tB[r_, 0:n]))
        p.on("vector", lambda v: v.tensor_add(tB[r_, 0:n], tE[r_, 0:n], ti32[r_, 0:n].bitcast(F32)))
        tv3 = p.ck("vector")
        p.wait("scalar", tv3)
        p.on("scalar", lambda sx: sx.activation(tE[r_, 0:n], tD[r_, 0:n], AF.Exp, scale=-FQ_LN2_HI))
        ts5 = p.ck("scalar")
        p.wait("vector", ts5)
        p.on("vector", lambda v: v.tensor_mul(tB[r_, 0:n], tB[r_, 0:n], tE[r_, 0:n]))
        tv4 = p.ck("vector")
        p.wait("scalar", tv4)
        p.on("scalar", lambda sx: sx.activation(tE[r_, 0:n], tD[r_, 0:n], AF.Exp, scale=-FQ_LN2_LO))
        ts6 = p.ck("scalar")
        p.wait("vector", ts6)
        p.on("vector", lambda v: v.tensor_mul(tB[r_, 0:n], tB[r_, 0:n], tE[r_, 0:n]))
        tvq = p.ck("vector")
        p.wait("scalar", tvq)
        p.on("scalar", lambda sx: sx.activation(tE[r_, 0:n], tB[r_, 0:n], AF.Abs))
        tab = p.ck("scalar")
        p.wait("vector", tab)
        p.on("vector", lambda v, maxv=maxv, n=n: v.tensor_scalar(tE[r_, 0:n], tE[r_, 0:n], maxv, None, OP.is_gt))
        tov = p.ck("vector")
        p.wait("scalar", tov)
        p.on("scalar", lambda sx: sx.activation(tE[r_, 0:n], tE[r_, 0:n], AF.Exp, scale=1e5))
        tex = p.ck("scalar")
        p.wait("vector", tex)
        p.on("vector", lambda v, dst=dst, n=n: v.tensor_mul(dst, tB[r_, 0:n], tE[r_, 0:n]))
        return p.ck("vector")

    def l3_attn():
        proj_tm_xnT(din["L3_attn_Wq"], 0, din["L3_attn_bq"], d_q2_tm)
        proj_tm_xnT(din["L3_attn_Wk"], 0, din["L3_attn_bk"], d_k2_tm)
        proj_tm_xnT(din["L3_attn_Wv"], 0, din["L3_attn_bv"], d_v2)
        V2 = tF
        for gp in range(8):
            g0 = 2 * gp
            toks = []
            for (dstT, srcD) in ((tD, d_q2_tm), (tC, d_k2_tm)):
                for hh in range(2):
                    g = g0 + hh
                    src = srcD[128 * g:128 * g + 128, :].rearrange("(c two) d -> c two d", two=2)
                    toks.append(p.dma("sync", dstT[64 * hh:64 * hh + 64, 0:1024], src[:, 0, :]))
                    toks.append(p.dma("sync", dstT[64 * hh:64 * hh + 64, 1024:2048], src[:, 1, :]))
            p.wait("tensor", toks)
            for hh in range(2):
                g = g0 + hh
                srcv = d_v2[128 * g:128 * g + 128, :].rearrange("(c two) d -> c two d", two=2)
                tv1 = p.dma("sync", tE[0:64, 0:1024], srcv[:, 0, :])
                tv2 = p.dma("sync", tE[0:64, 1024:2048], srcv[:, 1, :])
                p.wait("tensor", [tv1, tv2])
                voff = 66 * hh
                for kc in range(NT):
                    bi = 6 + (kc % 2)
                    p.wait("tensor", bank_guard.get(bi))
                    p.on("tensor", lambda te, kc=kc, bi=bi: te.transpose(
                        ps[bi][0:128, 0:64], tE[0:64, kc * 128:kc * 128 + 128], ident[0:64, 0:64]))
                    tmm = p.ck("tensor")
                    p.wait("vector", tmm)
                    p.on("vector", lambda v, kc=kc, bi=bi, voff=voff: v.tensor_copy(
                        V2[:, kc * 132 + voff:kc * 132 + voff + 64], ps[bi][0:128, 0:64]))
                    bank_guard[bi] = p.ck("vector")
                p.wait("sync", bank_guard[6 + ((NT - 1) % 2)])
            V2r = V2.ap().rearrange("p (t o) -> p t o", t=NT)
            p.on("vector", lambda v: v.memset(V2r[:, :, 64:65], 1.0))
            p.on("vector", lambda v: v.memset(V2r[:, :, 130:131], 1.0))
            tones = p.ck("vector")
            p.wait("tensor", tones)
            prev_exp_last = None
            prev_av_last = None
            for hh in range(2):
                qoff = 64 * hh
                voff = 66 * hh
                for qs in range(4):
                    nkc = NT
                    av_bank = ps[2 + hh]
                    exp_ready = {}
                    av_done = {}
                    for kc in range(nkc):
                        sb_bank = ps[kc % 2]
                        if kc >= 2:
                            p.wait("tensor", exp_ready[kc - 2])
                        elif kc == 0 and prev_exp_last is not None:
                            p.wait("tensor", prev_exp_last)
                        p.on("tensor", lambda te, kc=kc, qs=qs, qoff=qoff, sb_bank=sb_bank:
                             te.matmul(sb_bank[:, 0:512],
                                       tC[qoff:qoff + 64, kc * 128:kc * 128 + 128],
                                       tD[qoff:qoff + 64, qs * 512:qs * 512 + 512],
                                       start=True, stop=True))
                        tsc = p.ck("tensor")
                        p.wait("scalar", tsc)
                        if kc >= 4:
                            p.wait("scalar", av_done[kc - 4])
                        elif kc == 0 and prev_av_last is not None:
                            p.wait("scalar", prev_av_last)
                        ebuf = expb[:, ts(kc % 4, 512)]
                        p.on("scalar", lambda sx, ebuf=ebuf, sb_bank=sb_bank:
                             sx.activation(ebuf, sb_bank[:, 0:512], AF.Exp, scale=1.0 / 1024.0))
                        texp = p.ck("scalar")
                        exp_ready[kc] = texp
                        p.wait("tensor", texp)
                        p.on("tensor", lambda te, kc=kc, voff=voff, av_bank=av_bank, ebuf=ebuf, nkc=nkc:
                             te.matmul(av_bank[0:65, 0:512], V2[:, kc * 132 + voff:kc * 132 + voff + 65],
                                       ebuf, start=(kc == 0), stop=(kc == nkc - 1)))
                        av_done[kc] = p.ck("tensor")
                    p.wait("vector", av_done[nkc - 1])
                    p.on("vector", lambda v, av_bank=av_bank: v.reciprocal(
                        rrow[0:1, 1024:1536], av_bank[64:65, 0:512]))
                    trc = p.ck("vector")
                    p.wait("tensor", trc)
                    p.on("tensor", lambda te: te.matmul(ps[4][0:64, 0:512], ones_row[0:1, 0:64],
                                                        rrow[0:1, 1024:1536], start=True, stop=True))
                    tbc = p.ck("tensor")
                    p.wait("vector", tbc)
                    p.on("vector", lambda v: v.tensor_copy(tE[0:64, 512:1024], ps[4][0:64, 0:512]))
                    p.on("vector", lambda v, av_bank=av_bank: v.tensor_mul(
                        tE[0:64, 0:512], av_bank[0:64, 0:512], tE[0:64, 512:1024]))
                    tn = p.ck("vector")
                    p.wait("gpsimd", tn)
                    tok_park = p.dma("gpsimd", d_av[64 * (g0 + hh):64 * (g0 + hh) + 64, qs * 512:qs * 512 + 512],
                                     tE[0:64, 0:512])
                    p.wait("vector", tok_park)
                    p.wait("tensor", tok_park)
                    prev_exp_last = exp_ready[nkc - 1]
                    prev_av_last = av_done[nkc - 1]
            p.barrier()
        for g in range(16):
            tok_a = p.dma("sync", tC[0:64, 0:2048], d_av[64 * g:64 * g + 64, :])
            p.wait("tensor", tok_a)
            for m in range(16):
                bi = 6 + (m % 2)
                p.wait("tensor", bank_guard.get(bi))
                src = tC.ap()[0:64, 0:2048].rearrange("c (j m) -> c m j", m=16)
                p.on("tensor", lambda te, m=m, bi=bi, src=src: te.transpose(
                    ps[bi][0:128, 0:64], src[:, m, :], ident[0:64, 0:64]))
                tmm = p.ck("tensor")
                p.wait("vector", tmm)
                p.on("vector", lambda v, m=m, bi=bi: v.tensor_copy(tD[:, 64 * m:64 * m + 64], ps[bi][0:128, 0:64]))
                bank_guard[bi] = p.ck("vector")
            p.wait("tensor", bank_guard[6])
            p.wait("tensor", bank_guard[7])
            p.wait("sync", bank_guard[7])
            for c in range(NCH):
                bi = 6 + (c % 2)
                p.wait("tensor", bank_guard.get(bi))
                p.on("tensor", lambda te, c=c, bi=bi: te.transpose(
                    ps[bi][0:128, 0:128], tD[:, ts(c)], ident[:, :]))
                tmm = p.ck("tensor")
                p.wait("vector", tmm)
                p.on("vector", lambda v, c=c, bi=bi: v.tensor_copy(tE[:, c * 128:c * 128 + 128], ps[bi][0:128, 0:128]))
                bank_guard[bi] = p.ck("vector")
            p.wait("gpsimd", bank_guard[6])
            p.wait("gpsimd", bank_guard[7])
            tok_u = p.dma("gpsimd", d_comb[0:1024, g * 128:g * 128 + 128].rearrange("(c q) t -> q c t", q=128),
                          tE.ap()[:, 0:1024].rearrange("q (c t) -> q c t", c=8))
            p.wait("vector", tok_u)
        p.barrier()
        proj_from_fm_dram(d_comb, din["L3_attn_Wo"], din["L3_attn_bo"], d_s, nch_in=8)

    def l3_ffn_and_combine():
        for ri in range(32):
            tok = p.dma("sync", wa[0:128, 0:1024], d_w["l3f_w2"][ts(ri), 0:1024])
            p.wait("vector", tok)
            p.wait("scalar", tok)
            tq = emit_float_quantize(tA[:, 0:1024], wa[0:128, 0:1024], 1024, 4, 3)
            p.wait("gpsimd", tq)
            tok_s = p.dma("gpsimd", d_wq2[ts(ri), 0:1024], tA[0:128, 0:1024])
            p.wait("vector", tok_s)
            p.wait("scalar", tok_s)
            p.wait("sync", tq)
        tokb = p.dma("sync", qg[0:1, 0:1024], din["L3_ffn_b2"].unsqueeze(0))
        p.wait("vector", tokb)
        p.wait("scalar", tokb)
        tq = emit_float_quantize(qg[0:1, 0:1024], qg[0:1, 0:1024], 1024, 5, 10, rows=1)
        p.wait("tensor", tq)
        tokb1 = p.dma("sync", b1col[:, 0:16], din["L3_ffn_b1"].rearrange("(c p) -> p c", p=128)[:, 0:16],
                      allow_slow_non_contiguous=True)
        tokb1b = p.dma("sync", convb[:, 0:16], din["L3_ffn_b1"].rearrange("(c p) -> p c", p=128)[:, 16:32],
                       allow_slow_non_contiguous=True)
        p.wait("scalar", [tokb1, tokb1b])
        for hc in range(32):
            bufw = wa if hc % 2 == 0 else wb
            p.wait("sync", ("tensor", p.cnt["tensor"]))
            tokw = p.dma("sync", bufw.ap()[:, 0:1024].rearrange("p (c o) -> p c o", c=8),
                         d_w["l3f_w1"].rearrange("(c r) o -> r c o", r=128)[:, :, ts(hc)])
            p.wait("tensor", tokw)
            tev = None
            for qs in range(4):
                bi = qs % 4
                p.wait("tensor", bank_guard.get(bi))
                for c in range(NCH):
                    p.on("tensor", lambda te, c=c, qs=qs, bi=bi, bufw=bufw: te.matmul(
                        ps[bi][:, 0:512], bufw.ap()[:, c * 128:c * 128 + 128],
                        xnT[:, c * S + qs * 512:c * S + qs * 512 + 512],
                        start=(c == 0), stop=(c == NCH - 1)))
                tmm = p.ck("tensor")
                p.wait("scalar", tmm)
                bcol = b1col if hc < 16 else convb
                p.on("scalar", lambda sx, bi=bi, hc=hc, qs=qs, bcol=bcol: sx.activation(
                    tC[:, qs * 512:qs * 512 + 512], ps[bi][:, 0:512], AF.Relu,
                    bias=bcol[:, (hc % 16):(hc % 16) + 1], scale=1.0))
                tev = p.ck("scalar")
                bank_guard[bi] = tev
            p.wait("vector", tev)
            tq = emit_float_quantize(tC[:, 0:2048], tC[:, 0:2048], 2048, 4, 3)
            p.wait("gpsimd", tq)
            tok_st = p.dma("gpsimd", d_t1[ts(hc), 0:2048], tC[0:128, 0:2048])
            p.wait("scalar", tok_st)
            p.wait("vector", tok_st)
        p.barrier()
        for tg in range(4):
            for hc in range(32):
                bufw = wa if hc % 2 == 0 else wb
                tokw = p.dma("sync", bufw[0:128, 0:1024], d_wq2[ts(hc), 0:1024])
                tokh = p.dma("sync", tC[:, 0:512], d_t1[ts(hc), tg * 512:tg * 512 + 512])
                p.wait("tensor", [tokw, tokh])
                for tt4 in range(4):
                    for os_ in range(2):
                        bi = tt4 * 2 + os_
                        if hc == 0:
                            p.wait("tensor", bank_guard.get(bi))
                        p.on("tensor", lambda te, tt4=tt4, os_=os_, bi=bi, hc=hc, bufw=bufw:
                             te.matmul(ps[bi][:, 0:512], tC[:, tt4 * 128:tt4 * 128 + 128],
                                       bufw.ap()[:, os_ * 512:os_ * 512 + 512],
                                       start=(hc == 0), stop=False))
                tmm = p.ck("tensor")
                p.wait("sync", tmm)
            for tt4 in range(4):
                tt = tg * 4 + tt4
                for os_ in range(2):
                    bi = tt4 * 2 + os_
                    p.on("tensor", lambda te, bi=bi, os_=os_: te.matmul(
                        ps[bi][:, 0:512], ones_row[0:1, 0:128],
                        qg[0:1, os_ * 512:os_ * 512 + 512], start=False, stop=True))
                    tmm = p.ck("tensor")
                    p.wait("vector", tmm)
                    p.on("vector", lambda v, bi=bi, os_=os_: v.tensor_copy(
                        tC[:, os_ * 512:os_ * 512 + 512], ps[bi][:, 0:512]))
                    tev = p.ck("vector")
                    bank_guard[bi] = tev
                p.wait("scalar", tev)
                tq = emit_float_quantize(tC[:, 0:1024], tC[:, 0:1024], 1024, 5, 10)
                p.wait("scalar", tq)
                p.on("scalar", lambda sx: sx.activation(tC[:, 1024:2048], tC[:, 0:1024], AF.Silu))
                tsl = p.ck("scalar")
                p.wait("sync", tq)
                tok_xx = p.dma("sync", tD[:, 0:1024], d_x[ts(tt), :])
                tok_ss = p.dma("sync", tD[:, 1024:2048], d_s[ts(tt), :])
                p.wait("vector", [tsl, tok_xx, tok_ss])
                p.on("vector", lambda v: v.tensor_mul(tC[:, 1024:2048], tC[:, 1024:2048], tD[:, 1024:2048]))
                p.on("vector", lambda v: v.tensor_add(tD[:, 0:1024], tD[:, 0:1024], tC[:, 1024:2048]))
                tcb = p.ck("vector")
                p.wait("gpsimd", tcb)
                dst = dout if final_store_combine else d_x
                tok_st = p.dma("gpsimd", dst[ts(tt), :], tD[:, 0:1024])
                p.wait("sync", tok_st)
                p.wait("scalar", tok_st)
        p.barrier()

    def layer3():
        emit_ln_and_transpose(store_xn_dram=False)
        l3_attn()
        l3_ffn_and_combine()

    tok_x = p.dma("sync", d_x[:, :], din["L0_attn_x"][:, :])
    p.wait("sync", tok_x)

    def emit_round(eng, dst, src, n, tmpf, tmpi):
        p.on(eng, lambda v: v.tensor_copy(tmpi[:, 0:n], src))
        p.on(eng, lambda v: v.tensor_copy(tmpf[:, 0:n], tmpi[:, 0:n]))
        p.on(eng, lambda v: v.tensor_sub(dst, src, tmpf[:, 0:n]))
        p.on(eng, lambda v: v.tensor_scalar(tmpi[:, 0:n].bitcast(F32), dst, 0.5, None, OP.is_ge))
        p.on(eng, lambda v: v.tensor_scalar(dst, dst, -0.5, None, OP.is_lt))
        p.on(eng, lambda v: v.tensor_sub(tmpf[:, 0:n], tmpf[:, 0:n], dst))
        p.on(eng, lambda v: v.tensor_add(dst, tmpf[:, 0:n], tmpi[:, 0:n].bitcast(F32)))

    def emit_weight_quant(dw_in, rows, cols, d_out):
        nslab = rows // 128
        p.on("vector", lambda v: v.memset(sc1[:, 0:1], 0.0))
        tv = None
        for ri in range(nslab):
            buf = wa if ri % 2 == 0 else wb
            tok = p.dma("sync", buf[0:128, 0:cols], dw_in[ts(ri), 0:cols])
            p.wait("vector", tok)
            p.on("vector", lambda v, buf=buf, cols=cols: v.reduce_sum(
                sc2[:, 0:1], buf[0:128, 0:cols], axis=AX.X, apply_absolute_value=True))
            p.ckw("vector")
            p.on("vector", lambda v: v.tensor_add(sc1[:, 0:1], sc1[:, 0:1], sc2[:, 0:1]))
            tv = p.ckw("vector")
            p.wait("sync", tv)
        p.wait("tensor", tv)
        p.wait("tensor", bank_guard.get(5))
        p.on("tensor", lambda te: te.transpose(ps[5][0:1, 0:128], sc1[:, 0:1], ident[:, :]))
        tmm = p.ck("tensor")
        p.wait("vector", tmm)
        p.on("vector", lambda v: v.reduce_sum(rrow[0:1, 1536:1537], ps[5][0:1, 0:128], axis=AX.X))
        p.ckw("vector")
        p.on("vector", lambda v, rows=rows, cols=cols: v.tensor_scalar(
            rrow[0:1, 1537:1538], rrow[0:1, 1536:1537], 1.0 / (rows * cols), None, OP.mult))
        p.ckw("vector")
        p.on("vector", lambda v: v.tensor_scalar(
            rrow[0:1, 1537:1538], rrow[0:1, 1537:1538], 1e-5, None, OP.max))
        p.ckw("vector")
        p.on("vector", lambda v: v.reciprocal(rrow[0:1, 1538:1539], rrow[0:1, 1537:1538]))
        trc = p.ckw("vector")
        bank_guard[5] = trc
        p.wait("tensor", trc)
        p.wait("tensor", bank_guard.get(4))
        p.on("tensor", lambda te: te.matmul(ps[4][0:128, 0:2], ones_row[0:1, 0:128],
                                            rrow[0:1, 1537:1539], start=True, stop=True))
        tmm = p.ck("tensor")
        p.wait("vector", tmm)
        p.on("vector", lambda v: v.tensor_copy(sc3[:, 0:2], ps[4][0:128, 0:2]))
        tq = p.ck("vector")
        bank_guard[4] = tq
        p.wait("tensor", tq)
        for ri in range(nslab):
            buf = wa if ri % 2 == 0 else wb
            tok = p.dma("sync", buf[0:128, 0:cols], dw_in[ts(ri), 0:cols])
            p.wait("vector", tok)
            for cg in range(0, cols, 2048):
                w_ = min(2048, cols - cg)
                p.on("vector", lambda v, buf=buf, cg=cg, w_=w_: v.tensor_scalar(
                    tB[:, 0:w_], buf[0:128, cg:cg + w_], sc3[:, 1:2], None, OP.mult))
                emit_round("vector", tA[:, 0:w_], tB[:, 0:w_], w_, tE, ti32)
                p.on("vector", lambda v, w_=w_: v.tensor_scalar(
                    tA[:, 0:w_], tA[:, 0:w_], -1.0, 1.0, OP.max, OP.min))
                p.on("vector", lambda v, w_=w_: v.tensor_scalar(
                    tA[:, 0:w_], tA[:, 0:w_], sc3[:, 0:1], None, OP.mult))
                tq = p.ck("vector")
                p.wait("gpsimd", tq)
                tok_s = p.dma("gpsimd", d_out[ts(ri), cg:cg + w_], tA[0:128, 0:w_])
                p.wait("vector", tok_s)
            p.wait("sync", tq)
        p.barrier()

    def l1_ffn_and_combine():
        emit_weight_quant(d_w["l1f_w1"], D, 4096, d_wq1)
        emit_weight_quant(d_w["l1f_w2"], 4096, D, d_wq2)
        tok1 = p.dma("sync", convw.ap().rearrange("p (c k) -> p c k", c=32)[:, :, 0:3],
                     din["L1_ffn_conv_w"].squeeze(1).rearrange("(c p) k -> p c k", p=128))
        tok2 = p.dma("sync", convb[:, 0:32],
                     din["L1_ffn_conv_b"].rearrange("(c p) -> p c", p=128),
                     allow_slow_non_contiguous=True)
        p.wait("vector", tok1)
        p.wait("scalar", tok2)
        for tt in range(NT):
            tok = p.dma("sync", tB[:, 0:1024], d_xn[ts(tt), :])
            p.wait("vector", tok)
            p.on("vector", lambda v: v.reduce_max(sc1[:, 0:1], tB[:, 0:1024], axis=AX.X,
                                                  apply_absolute_value=True))
            p.ckw("vector")
            p.on("vector", lambda v: v.tensor_scalar(sc1[:, 0:1], sc1[:, 0:1], 1e-5, None, OP.max))
            p.ckw("vector")
            p.on("vector", lambda v: v.reciprocal(sc2[:, 0:1], sc1[:, 0:1]))
            p.ckw("vector")
            p.on("vector", lambda v: v.tensor_scalar(sc2[:, 0:1], sc2[:, 0:1], 127.0, None, OP.mult))
            p.ckw("vector")
            p.on("vector", lambda v: v.reciprocal(sc3[:, 0:1], sc2[:, 0:1]))
            p.ckw("vector")
            p.on("vector", lambda v: v.tensor_scalar(tC[:, 0:1024], tB[:, 0:1024], sc2[:, 0:1], None, OP.mult))
            emit_round("vector", tD[:, 0:1024], tC[:, 0:1024], 1024, tE, ti32)
            p.on("vector", lambda v: v.tensor_scalar(tD[:, 0:1024], tD[:, 0:1024], -128.0, 127.0, OP.max, OP.min))
            p.on("vector", lambda v: v.tensor_scalar(tD[:, 0:1024], tD[:, 0:1024], sc3[:, 0:1], None, OP.mult))
            tq = p.ck("vector")
            p.wait("tensor", tq)
            for c in range(NCH):
                bi = 6 + (c % 2)
                p.wait("tensor", bank_guard.get(bi))
                p.on("tensor", lambda te, c=c, bi=bi: te.transpose(ps[bi][0:128, 0:128], tD[:, ts(c)], ident[:, :]))
                tmm = p.ck("tensor")
                p.wait("vector", tmm)
                p.on("vector", lambda v, c=c, tt=tt, bi=bi: v.tensor_copy(
                    xnT[:, c * S + tt * 128:c * S + tt * 128 + 128], ps[bi][0:128, 0:128]))
                bank_guard[bi] = p.ck("vector")
            p.wait("sync", bank_guard[7])
        p.barrier()
        p.on("vector", lambda v: v.memset(tA[:, 0:2048], 0.0))
        tamax = p.ck("vector")
        for hc in range(32):
            tok_w = p.dma("sync", (wa if hc % 2 == 0 else wb).ap()[:, 0:1024].rearrange(
                "p (c o) -> p c o", c=8), d_wq1.rearrange("(c r) o -> r c o", r=128)[:, :, ts(hc)])
            p.wait("tensor", tok_w)
            buf = wa if hc % 2 == 0 else wb
            for qs in range(4):
                bi = qs % 4
                p.wait("tensor", bank_guard.get(bi))
                for c in range(NCH):
                    p.on("tensor", lambda te, c=c, qs=qs, bi=bi, buf=buf:
                         te.matmul(ps[bi][:, 0:512], buf.ap()[:, c * 128:c * 128 + 128],
                                   xnT[:, c * S + qs * 512:c * S + qs * 512 + 512],
                                   start=(c == 0), stop=(c == NCH - 1)))
                tmm = p.ck("tensor")
                p.wait("vector", tmm)
                p.on("vector", lambda v, qs=qs, bi=bi: v.tensor_copy(tC[:, 1 + qs * 512:1 + qs * 512 + 512], ps[bi][:, 0:512]))
                bank_guard[bi] = p.ck("vector")
            p.on("vector", lambda v: v.memset(tC[:, 0:1], 0.0))
            p.on("vector", lambda v: v.memset(tC[:, 2049:2050], 0.0))
            p.on("vector", lambda v, hc=hc: v.tensor_scalar(tD[:, 0:2048], tC[:, 0:2048], convw[:, 4 * hc:4 * hc + 1], None, OP.mult))
            p.on("vector", lambda v, hc=hc: v.scalar_tensor_tensor(tD[:, 0:2048], tC[:, 1:2049], convw[:, 4 * hc + 1:4 * hc + 2], tD[:, 0:2048], OP.mult, OP.add))
            p.on("vector", lambda v, hc=hc: v.scalar_tensor_tensor(tD[:, 0:2048], tC[:, 2:2050], convw[:, 4 * hc + 2:4 * hc + 3], tD[:, 0:2048], OP.mult, OP.add))
            tcv = p.ck("vector")
            p.wait("scalar", tcv)
            p.on("scalar", lambda sx, hc=hc: sx.activation(tE[:, 0:2048], tD[:, 0:2048], AF.Silu,
                                                           bias=convb[:, hc:hc + 1], scale=1.0))
            tsl = p.ck("scalar")
            p.wait("vector", tsl)
            p.on("vector", lambda v: v.tensor_scalar(tB[:, 0:2048], tE[:, 0:2048], -1.0, None, OP.mult))
            p.on("vector", lambda v: v.tensor_max(tB[:, 0:2048], tB[:, 0:2048], tE[:, 0:2048]))
            p.on("vector", lambda v: v.tensor_max(tA[:, 0:2048], tA[:, 0:2048], tB[:, 0:2048]))
            tmx = p.ck("vector")
            p.wait("gpsimd", tsl)
            tok_st = p.dma("gpsimd", d_t1[ts(hc), 0:2048], tE[:, 0:2048])
            p.wait("scalar", tok_st)
            p.wait("sync", tmx)
        p.barrier()
        for tt in range(NT):
            bi = 6 + (tt % 2)
            p.wait("tensor", bank_guard.get(bi))
            p.on("tensor", lambda te, tt=tt, bi=bi: te.transpose(ps[bi][0:128, 0:128], tA[:, ts(tt)], ident[:, :]))
            tmm = p.ck("tensor")
            p.wait("vector", tmm)
            p.on("vector", lambda v, tt=tt, bi=bi: v.reduce_max(sc1[:, tt:tt + 1], ps[bi][0:128, 0:128], axis=AX.X))
            bank_guard[bi] = p.ckw("vector")
        for tt in range(NT):
            p.on("vector", lambda v, tt=tt: v.tensor_scalar(sc1[:, tt:tt + 1], sc1[:, tt:tt + 1], 1e-5, None, OP.max))
        p.ckw("vector")
        p.on("vector", lambda v: v.reciprocal(sc2[:, 0:NT], sc1[:, 0:NT]))
        p.ckw("vector")
        p.on("vector", lambda v: v.tensor_scalar(sc2[:, 0:NT], sc2[:, 0:NT], 127.0, None, OP.mult))
        p.ckw("vector")
        p.on("vector", lambda v: v.reciprocal(sc3[:, 0:NT], sc2[:, 0:NT]))
        tsc_ = p.ckw("vector")
        p.wait("tensor", tsc_)
        for (srccol, dstbase) in ((sc2, 0), (sc3, 0)):
            for tt in range(NT):
                bi = 6 + (tt % 2)
                p.wait("tensor", bank_guard.get(bi))
                p.on("tensor", lambda te, srccol=srccol, tt=tt, bi=bi: te.transpose(
                    ps[bi][0:1, 0:128], srccol[:, tt:tt + 1], ident[:, :]))
                tmm = p.ck("tensor")
                p.wait("vector", tmm)
                if srccol is sc2:
                    p.on("vector", lambda v, tt=tt, bi=bi: v.tensor_copy(
                        rrow[0:1, tt * 128:tt * 128 + 128], ps[bi][0:1, 0:128]))
                else:
                    p.on("vector", lambda v, tt=tt, bi=bi: v.tensor_copy(
                        brow[0:1, tt * 128:tt * 128 + 128], ps[bi][0:1, 0:128]))
                bank_guard[bi] = p.ck("vector")
        tsc2 = p.ck("vector") if p._dirty["vector"] else None
        for (srcrow, dstt) in ((rrow, tA), (brow, tB)):
            for qs in range(4):
                bi = 4 + (qs % 2)
                p.wait("tensor", bank_guard.get(bi))
                p.on("tensor", lambda te, srcrow=srcrow, qs=qs, bi=bi: te.matmul(
                    ps[bi][0:128, 0:512], ones_row[0:1, 0:128],
                    srcrow[0:1, qs * 512:qs * 512 + 512], start=True, stop=True))
                tmm = p.ck("tensor")
                p.wait("vector", tmm)
                p.on("vector", lambda v, qs=qs, bi=bi, dstt=dstt: v.tensor_copy(
                    dstt[:, qs * 512:qs * 512 + 512], ps[bi][0:128, 0:512]))
                bank_guard[bi] = p.ck("vector")
        p.barrier()
        for tg in range(4):
            for hc in range(32):
                buf = wa if hc % 2 == 0 else wb
                tokw = p.dma("sync", buf[0:128, 0:1024], d_wq2[ts(hc), 0:1024])
                tokh = p.dma("sync", tC[:, 0:512], d_t1[ts(hc), tg * 512:tg * 512 + 512])
                p.wait("vector", tokh)
                p.on("vector", lambda v, tg=tg: v.tensor_mul(tD[:, 0:512], tC[:, 0:512],
                                                             tA[:, tg * 512:tg * 512 + 512]))
                emit_round("vector", tD[:, 0:512], tD[:, 0:512], 512, tE, ti32)
                p.on("vector", lambda v: v.tensor_scalar(tD[:, 0:512], tD[:, 0:512], -128.0, 127.0, OP.max, OP.min))
                p.on("vector", lambda v, tg=tg: v.tensor_mul(tD[:, 0:512], tD[:, 0:512],
                                                             tB[:, tg * 512:tg * 512 + 512]))
                tq = p.ck("vector")
                p.wait("tensor", tq)
                p.wait("tensor", tokw)
                for tt4 in range(4):
                    for os_ in range(2):
                        bi = tt4 * 2 + os_
                        if hc == 0:
                            p.wait("tensor", bank_guard.get(bi))
                        p.on("tensor", lambda te, tt4=tt4, os_=os_, bi=bi, hc=hc, buf=buf:
                             te.matmul(ps[bi][:, 0:512], tD[:, tt4 * 128:tt4 * 128 + 128],
                                       buf.ap()[:, os_ * 512:os_ * 512 + 512],
                                       start=(hc == 0), stop=(hc == 31)))
                tmm = p.ck("tensor")
                p.wait("sync", tmm)
                p.wait("vector", tmm)
            p.wait("scalar", tmm)
            for tt4 in range(4):
                tt = tg * 4 + tt4
                for os_ in range(2):
                    bi = tt4 * 2 + os_
                    p.on("scalar", lambda sx, bi=bi, os_=os_: sx.activation(
                        tF[:, os_ * 512:os_ * 512 + 512], ps[bi][:, 0:512], AF.Silu))
                tsl = p.ck("scalar")
                for bi in (tt4 * 2, tt4 * 2 + 1):
                    bank_guard[bi] = tsl
                tok_xx = p.dma("sync", tC[:, 0:1024], d_x[ts(tt), :])
                tok_ss = p.dma("sync", tC[:, 1024:2048], d_s[ts(tt), :])
                p.wait("vector", tsl)
                p.wait("vector", tok_xx)
                p.wait("vector", tok_ss)
                p.on("vector", lambda v: v.tensor_mul(tF[:, 1024:2048], tC[:, 1024:2048], tF[:, 0:1024]))
                p.on("vector", lambda v: v.tensor_add(tC[:, 0:1024], tC[:, 0:1024], tF[:, 1024:2048]))
                tcb = p.ck("vector")
                p.wait("gpsimd", tcb)
                tok_st = p.dma("gpsimd", d_x[ts(tt), :], tC[:, 0:1024])
                p.wait("sync", tok_st)
                p.wait("scalar", tcb)
        p.barrier()

    def l1_qkv_cols(h):
        return (192 * h, 192 * h + 64, 192 * h + 128)

    def layer1():
        emit_ln_and_transpose(store_xn_dram=True)
        emit_mha(din["L1_attn_W_qkv"], l1_qkv_cols, din["L1_attn_b_qkv"],
                 din["L1_attn_W_out"], din["L1_attn_b_out"],
                 causal=True, dst_s=d_s, scale=0.125)
        l1_ffn_and_combine()

    if 0 in layers:
        layer0()
    if 1 in layers:
        layer1()
    if 2 in layers:
        layer2()
    if 3 in layers:
        layer3()

    p.barrier()
    if dbg == "s":
        tok2 = None
        for tt in range(NT):
            tok = p.dma("sync", tA[:, 0:1024], d_s[ts(tt), :])
            p.wait("gpsimd", tok)
            tok2 = p.dma("gpsimd", dout[ts(tt), :], tA[:, 0:1024])
            p.wait("sync", tok2)
        last = tok2
    elif dbg == "xn":
        tok2 = None
        for tt in range(NT):
            tok = p.dma("sync", tA[:, 0:1024], d_xn[ts(tt), :])
            p.wait("gpsimd", tok)
            tok2 = p.dma("gpsimd", dout[ts(tt), :], tA[:, 0:1024])
            p.wait("sync", tok2)
        last = tok2
    elif final_store_combine:
        last = None
    else:
        last = p.dma("gpsimd", dout[:, :], d_x[:, :])
    if last is not None:
        p.wait("gpsimd", last)

    with nc.Block() as block:
        @block.tensor
        def _(e):
            p.emit("tensor", e)

        @block.vector
        def _(e):
            p.emit("vector", e)

        @block.scalar
        def _(e):
            p.emit("scalar", e)

        @block.gpsimd
        def _(e):
            p.emit("gpsimd", e)

        @block.sync
        def _(e):
            p.emit("sync", e)

    es.close()
    return nc


def kernel(**inputs):
    x = np.ascontiguousarray(inputs["L0_attn_x"])
    nc = bass.Bass("TRN2", target_bir_lowering=False, detect_race_conditions=False)
    build_core_program(nc)
    in_maps = []
    for b in range(8):
        m = {k: np.ascontiguousarray(v) for k, v in inputs.items() if k != "L0_attn_x"}
        m["L0_attn_x"] = x[b]
        in_maps.append(m)
    res = run_bass_kernel_spmd(nc, in_maps, core_ids=list(range(8)))
    return np.stack([r["out"] for r in res.results], axis=0)


# version 06 (final)
# speedup vs baseline: 1.0250x; 1.0061x over previous
import contextlib
import numpy as np
import concourse.bass as bass
import concourse.mybir as mybir
from concourse.bass_utils import run_bass_kernel_spmd

F32 = mybir.dt.float32
I32 = mybir.dt.int32
U8 = mybir.dt.uint8
AF = mybir.ActivationFunctionType
OP = mybir.AluOpType
AX = mybir.AxisListType

S = 2048
D = 1024
NT = S // 128
NCH = D // 128
H = 16
DH = 64
LN2_HI = float(np.float32(0.6931471805599453))
LN2_LO = float(np.float64(0.6931471805599453094172321214581765680755)
               - np.float64(np.float32(0.6931471805599453)))


def ts(i, n=128):
    return slice(i * n, (i + 1) * n)


class P:
    ENGS = ("tensor", "vector", "scalar", "gpsimd", "sync")

    NDS = 8

    def __init__(self, nc, es):
        self.nc = nc
        self.steps = []
        self.cnt = {e: 0 for e in self.ENGS}
        self.dma_cnt = {"sync": 0, "gpsimd": 0, "scalar": 0}
        self.sem = {e: es.enter_context(nc.semaphore("sp_" + e)) for e in self.ENGS}
        self.dsem = {q: [es.enter_context(nc.semaphore("sd_%s%d" % (q, i)))
                         for i in range(self.NDS)] for q in self.dma_cnt}
        self._last = {}
        self._dirty = {e: False for e in self.ENGS}

    def on(self, eng, fn):
        self._dirty[eng] = True

        def run(engine, eng=eng, fn=fn):
            r = fn(engine)
            if r is not None:
                self._last[eng] = r
        self.steps.append((eng, run))

    def ck(self, eng):
        assert self._dirty[eng], "checkpoint on engine with no new ops: " + eng
        self._dirty[eng] = False
        self.cnt[eng] += 1
        sem = self.sem[eng]

        def run(engine, eng=eng, sem=sem):
            self._last[eng].then_inc(sem, 1)
        self.steps.append((eng, run))
        return (eng, self.cnt[eng])

    def ckw(self, eng):
        t = self.ck(eng) if self._dirty[eng] else (eng, self.cnt[eng])
        self.wait(eng, t)
        return t

    def wait(self, eng, token):
        if token is None:
            return
        if isinstance(token, list):
            best = {}
            for t in token:
                if t is None:
                    continue
                src, val = t
                key = src if not src.startswith("d") else (src, (val - 1) % self.NDS)
                if key not in best or best[key][1] < val:
                    best[key] = t
            for t in best.values():
                self.wait(eng, t)
            return
        src, val = token
        if src.startswith("d") and src[1:] in self.dsem:
            idx = val - 1
            sem = self.dsem[src[1:]][idx % self.NDS]
            v = 16 * (idx // self.NDS + 1)
        else:
            sem, v = self.sem[src], val

        def run(engine, sem=sem, v=v):
            engine.wait_ge(sem, v)
        self.steps.append((eng, run))

    def dma(self, q, out, in_, **kw):
        idx = self.dma_cnt[q]
        self.dma_cnt[q] += 1
        sem = self.dsem[q][idx % self.NDS]
        prev_round = idx // self.NDS

        def run(engine, out=out, in_=in_, kw=kw, sem=sem, prev_round=prev_round):
            if prev_round > 0:
                engine.wait_ge(sem, 16 * prev_round)
            engine.dma_start(out=out, in_=in_, **kw).then_inc(sem, 16)
        self.steps.append((q, run))
        return ("d" + q, self.dma_cnt[q])

    def barrier(self):
        for e in self.ENGS:
            if self._dirty[e]:
                self.ck(e)
        toks = [(e, self.cnt[e]) for e in self.ENGS if self.cnt[e] > 0]
        dtoks = []
        for q, c in self.dma_cnt.items():
            for i in range(max(0, c - self.NDS), c):
                dtoks.append(("d" + q, i + 1))
        for e in self.ENGS:
            for t in toks:
                if t[0] != e:
                    self.wait(e, t)
            for t in dtoks:
                self.wait(e, t)

    def emit(self, engine_name, engine):
        for eng, fn in self.steps:
            if eng == engine_name:
                fn(engine)


def build_core_program(nc, dbg=None, layers=(0, 1, 2, 3), l0_stop=None):
    final_store_combine = (3 in layers) and (dbg is None)
    es = contextlib.ExitStack()
    din = {}
    spec = {
        "L0_attn_x": (S, D),
        "L0_attn_lt_w_in": (3 * D, D), "L0_attn_lt_b_in": (3 * D,),
        "L0_attn_lt_w_out": (D, D), "L0_attn_lt_b_out": (D,),
        "L0_attn_st_w_in": (3 * D, D), "L0_attn_st_b_in": (3 * D,),
        "L0_attn_st_w_out": (D, D), "L0_attn_st_b_out": (D,),
        "L0_attn_protos": (8, D), "L0_attn_fus_w": (D, 2 * D), "L0_attn_fus_b": (D,),
        "L0_ffn_router_w": (D, 16),
        "L0_ffn_w1": (16, D, 2048), "L0_ffn_b1": (16, 2048),
        "L0_ffn_w2": (16, 2048, D), "L0_ffn_b2": (16, D),
        "L1_attn_W_qkv": (D, 3 * D), "L1_attn_b_qkv": (3 * D,),
        "L1_attn_W_out": (D, D), "L1_attn_b_out": (D,),
        "L1_ffn_w1": (4096, D), "L1_ffn_conv_w": (4096, 1, 3), "L1_ffn_conv_b": (4096,),
        "L1_ffn_w2": (D, 4096),
        "L2_attn_W_qkv": (3 * D, D), "L2_attn_b_qkv": (3 * D,),
        "L2_attn_W_out": (D, D), "L2_attn_b_out": (D,),
        "L2_attn_W_gate": (D, D), "L2_attn_b_gate": (D,),
        "L2_attn_scale_gain": (11, 16), "L2_attn_W_qscale": (11, 64),
        "L2_attn_identity_bypass": (16,),
        "L2_ffn_weight": (D, D), "L2_ffn_gate_w": (8, D),
        "L2_ffn_lora_U": (8, 16, D), "L2_ffn_lora_V": (8, D, 16),
        "L3_attn_Wq": (D, D), "L3_attn_bq": (D,),
        "L3_attn_Wk": (D, D), "L3_attn_bk": (D,),
        "L3_attn_Wv": (D, D), "L3_attn_bv": (D,),
        "L3_attn_Wo": (D, D), "L3_attn_bo": (D,),
        "L3_ffn_w1": (4096, D), "L3_ffn_b1": (4096,),
        "L3_ffn_w2": (D, 4096), "L3_ffn_b2": (D,),
    }
    for name, shp in spec.items():
        din[name] = nc.dram_tensor(name, list(shp), F32, kind="ExternalInput").ap()
    dout = nc.dram_tensor("out", [S, D], F32, kind="ExternalOutput").ap()

    dbg_outs = set() if dbg != "dump" else {
        "d_s", "d_xn", "d_q2", "d_k2", "d_v2", "d_gains", "d_sd", "d_t1", "d_t2", "d_comb", "d_av"}
    dram = lambda n, shp: nc.dram_tensor(
        n, list(shp), F32, kind=("ExternalOutput" if n in dbg_outs else "Internal")).ap()
    d_s = dram("d_s", (S, D))
    d_x = dram("d_x", (S, D))
    d_xn = dram("d_xn", (S, D))
    d_av = dram("d_av", (D, S))
    d_t1 = dram("d_t1", (4096, 2048))
    d_t2 = dram("d_t2", (4096, 2048))
    d_wq1 = dram("d_wq1", (D, 4096))
    d_wq2 = dram("d_wq2", (4096, D))
    d_comb = dram("d_comb", (2 * D, S))
    d_q2 = dram("d_q2", (D, S))
    d_k2 = dram("d_k2", (D, S))
    d_q2_tm = dram("d_q2_tm", (S, D))
    d_k2_tm = dram("d_k2_tm", (S, D))
    d_v2 = dram("d_v2", (S, D))
    d_gains = dram("d_gains", (16, 11 * S))
    d_sd = dram("d_sd", (S, 358))

    tw_specs = {
        "lt_w_in": ("L0_attn_lt_w_in", 3 * D, D), "lt_w_out": ("L0_attn_lt_w_out", D, D),
        "st_w_in": ("L0_attn_st_w_in", 3 * D, D), "st_w_out": ("L0_attn_st_w_out", D, D),
        "fus_w": ("L0_attn_fus_w", D, 2 * D), "protos": ("L0_attn_protos", 8, D),
        "l1f_w1": ("L1_ffn_w1", 4096, D), "l1f_w2": ("L1_ffn_w2", D, 4096),
        "l2_qkv": ("L2_attn_W_qkv", 3 * D, D), "l2_out": ("L2_attn_W_out", D, D),
        "l2_gate": ("L2_attn_W_gate", D, D), "l2_qscale": ("L2_attn_W_qscale", 11, 64),
        "l2f_w": ("L2_ffn_weight", D, D), "l2f_gw": ("L2_ffn_gate_w", 8, D),
        "l3f_w1": ("L3_ffn_w1", 4096, D), "l3f_w2": ("L3_ffn_w2", D, 4096),
        "l2f_U": (None, 128, D), "l2f_V": (None, 128, D),
    }
    d_w = {}
    for k, (src, r, c) in tw_specs.items():
        if k == "l2f_U":
            d_w[k] = dram("d_w_" + k, (D, 128))
        elif k == "l2f_V":
            d_w[k] = dram("d_w_" + k, (128, D))
        else:
            d_w[k] = dram("d_w_" + k, (c, r))

    sbuf = {}

    def SB(name, shape, dt=F32):
        sbuf[name] = nc.alloc_sbuf_tensor(name, shape, dt)
        return sbuf[name]

    xnT = SB("xnT", [128, NCH * S])
    ident = SB("ident", [128, 128])
    ones_row = SB("ones_row", [1, 128])
    ln_nmean = SB("ln_nmean", [128, NT])
    ln_rstd = SB("ln_rstd", [128, NT])
    wa = SB("wa", [128, 4096])
    wb = SB("wb", [128, 4096])
    tA = SB("tA", [128, 2048])
    tB = SB("tB", [128, 2048])
    tC = SB("tC", [128, 2056])
    tD = SB("tD", [128, 2048])
    tE = SB("tE", [128, 2048])
    tF = SB("tF", [128, 2112])
    expb = SB("expb", [128, 4 * 512])
    sc1 = SB("sc1", [128, 16])
    sc2 = SB("sc2", [128, 16])
    sc3 = SB("sc3", [128, 16])
    rrow = SB("rrow", [1, 2048])
    brow = SB("brow", [1, 2048])
    xtile = SB("xtile", [128, 1024])
    big = SB("big", [128, 6144])
    segq_bc = big.ap()[:, 0:2048]
    sing_bc = big.ap()[:, 2048:4096]
    masks = big.ap()[:, 4096:6144]
    qg = big.ap()
    ti32 = SB("ti32", [128, 2048], I32)
    ones_col = SB("ones_col", [128, 1])
    seg_tm = SB("seg_tm", [128, 16])
    ohR_tm = SB("ohR_tm", [128, 256])
    wtok_tm = SB("wtok_tm", [128, 16])
    b1col = SB("b1col", [128, 16])
    accA = SB("accA", [128, 1])
    convw = SB("convw", [128, 128])
    convb = SB("convb", [128, 32])

    ps = [nc.alloc_psum_tensor("psb%d" % i, [128, 512], F32) for i in range(8)]

    p = P(nc, es)
    bank_guard = {}

    def _gp_init(g):
        g.memset(ones_row[:, :], 1.0)
        g.memset(ones_col[:, :], 1.0)
        return g.iota(tA[:, 0:512], [[1, 512]], base=0, channel_multiplier=-1,
                      allow_small_or_imprecise_dtypes=True)
    p.on("gpsimd", _gp_init)
    t = p.ck("gpsimd")
    p.wait("vector", t)
    p.on("vector", lambda v: v.tensor_scalar(ident[:, :], tA[:, 0:128], 0.0, None, OP.is_equal))
    for r in range(4):
        p.on("vector", lambda v, r=r: v.tensor_scalar(
            masks[:, ts(r, 512)], tA[:, 0:512], float(128 * r), None, OP.is_ge))
    t_const = p.ck("vector")
    p.wait("tensor", t_const)
    p.wait("gpsimd", t_const)
    p.wait("scalar", t_const)

    pret_guard = {}

    def pretranspose(src, rows, cols, dst):
        if cols > 2048:
            assert cols % 2048 == 0
            for cg in range(cols // 2048):
                pretranspose(src[:, cg * 2048:(cg + 1) * 2048], rows, 2048,
                             dst[cg * 2048:(cg + 1) * 2048, :])
            return
        ncol = (cols + 127) // 128
        nslab = (rows + 127) // 128
        for ri in range(nslab):
            r0, r1 = ri * 128, min(rows, ri * 128 + 128)
            rl = r1 - r0
            bname = "wa" if ri % 2 == 0 else "wb"
            ename = "tA" if ri % 2 == 0 else "tB"
            buf = wa if ri % 2 == 0 else wb
            ebuf = tA if ri % 2 == 0 else tB
            p.wait("sync", pret_guard.get(bname))
            tok_l = p.dma("sync", buf[0:rl, 0:cols], src[r0:r1, 0:cols])
            p.wait("tensor", tok_l)
            p.wait("vector", pret_guard.get(ename))
            prev_ev = {}
            for ci in range(ncol):
                c0, c1 = ci * 128, min(cols, ci * 128 + 128)
                bi = 6 + (ci % 2)
                bank = ps[bi]
                p.wait("tensor", bank_guard.get(bi))
                p.on("tensor", lambda te, rl=rl, c0=c0, c1=c1, bank=bank, buf=buf:
                     te.transpose(bank[0:c1 - c0, 0:rl], buf[0:rl, c0:c1], ident[0:rl, 0:rl]))
                tmm = p.ck("tensor")
                p.wait("vector", tmm)
                p.on("vector", lambda v, rl=rl, c0=c0, c1=c1, bank=bank, ci=ci, ebuf=ebuf:
                     v.tensor_copy(ebuf[0:c1 - c0, ci * 128:ci * 128 + rl], bank[0:c1 - c0, 0:rl]))
                prev_ev[ci] = p.ck("vector")
                bank_guard[bi] = prev_ev[ci]
                pret_guard[bname] = tmm
            p.wait("gpsimd", prev_ev[ncol - 1])
            if cols >= 128:
                out_ap = dst[0:cols, r0:r1].rearrange("(c q) r -> q c r", q=128)
                in_ap = ebuf.ap().rearrange("q (c r) -> q c r", r=128)[0:128, 0:ncol, 0:rl]
            else:
                out_ap = dst[0:cols, r0:r1]
                in_ap = ebuf.ap()[0:cols, 0:rl]
            pret_guard[ename] = p.dma("gpsimd", out_ap, in_ap)

    need_prep = []
    if 0 in layers:
        need_prep += ["lt_w_in", "lt_w_out", "st_w_in", "st_w_out", "fus_w", "protos"]
    if 1 in layers:
        need_prep += ["l1f_w1", "l1f_w2"]
    if 2 in layers:
        need_prep += ["l2_qkv", "l2_out", "l2_gate", "l2_qscale", "l2f_w", "l2f_gw"]
    if 3 in layers:
        need_prep += ["l3f_w1", "l3f_w2"]
    for k in need_prep:
        srcname, rows, cols = tw_specs[k]
        pretranspose(din[srcname], rows, cols, d_w[k])
    if 2 in layers:
        pretranspose(din["L2_ffn_lora_U"].rearrange("e r d -> (e r) d"), 128, D, d_w["l2f_U"])
        for e in range(8):
            pretranspose(din["L2_ffn_lora_V"][e], D, 16, d_w["l2f_V"][16 * e:16 * e + 16, :])
    if need_prep or 2 in layers:
        p.barrier()

    ln_eps = 1e-5

    def emit_ln_and_transpose(store_xn_dram=True, src=None):
        src = src if src is not None else d_x
        tok_dxn = None
        tok_xt_prev = None
        for tt in range(NT):
            xt = xtile[:, 0:1024]
            tok_xt = p.dma("sync", xtile[:, 0:1024], src[ts(tt), :])
            p.wait("vector", tok_xt)
            p.wait("scalar", tok_xt)
            p.on("vector", lambda v, xt=xt, tt=tt: v.reduce_sum(sc1[:, tt:tt + 1], xt, axis=AX.X))
            p.ckw("vector")
            p.on("vector", lambda v, tt=tt: v.tensor_scalar(
                ln_nmean[:, tt:tt + 1], sc1[:, tt:tt + 1], -1.0 / D, None, OP.mult))
            tv = p.ck("vector")
            p.wait("scalar", tv)
            if tok_dxn is not None:
                p.wait("scalar", tok_dxn)
            p.on("scalar", lambda s, xt=xt, tt=tt: s.activation(
                tB[:, 0:1024], xt, AF.Square, bias=ln_nmean[:, tt:tt + 1], scale=1.0,
                accum_out=accA[:, 0:1]))
            tsq = p.ck("scalar")
            p.wait("vector", tsq)
            p.on("vector", lambda v, tt=tt: v.tensor_scalar(
                sc2[:, tt:tt + 1], accA[:, 0:1], 1.0 / D, ln_eps, OP.mult, OP.add))
            tv2 = p.ck("vector")
            p.wait("scalar", tv2)
            p.on("scalar", lambda s, tt=tt: s.activation(sc3[:, tt:tt + 1], sc2[:, tt:tt + 1], AF.Sqrt))
            tsq2 = p.ck("scalar")
            p.wait("vector", tsq2)
            p.on("vector", lambda v, tt=tt: v.reciprocal(sc1[:, tt:tt + 1], sc3[:, tt:tt + 1]))
            p.ckw("vector")
            p.on("vector", lambda v, tt=tt: v.tensor_mul(sc3[:, tt:tt + 1], sc1[:, tt:tt + 1], sc1[:, tt:tt + 1]))
            p.ckw("vector")
            p.on("vector", lambda v, tt=tt: v.tensor_mul(sc3[:, tt:tt + 1], sc3[:, tt:tt + 1], sc2[:, tt:tt + 1]))
            p.ckw("vector")
            p.on("vector", lambda v, tt=tt: v.tensor_scalar(
                sc3[:, tt:tt + 1], sc3[:, tt:tt + 1], -0.5, 1.5, OP.mult, OP.add))
            p.ckw("vector")
            p.on("vector", lambda v, tt=tt: v.tensor_mul(ln_rstd[:, tt:tt + 1], sc1[:, tt:tt + 1], sc3[:, tt:tt + 1]))
            p.ckw("vector")
            p.on("vector", lambda v, tt=tt: v.tensor_mul(sc2[:, tt:tt + 1], ln_nmean[:, tt:tt + 1], ln_rstd[:, tt:tt + 1]))
            tvr = p.ck("vector")
            p.wait("scalar", tvr)
            p.on("scalar", lambda s, xt=xt, tt=tt: s.activation(
                tB[:, 0:1024], xt, AF.Identity, bias=sc2[:, tt:tt + 1], scale=ln_rstd[:, tt:tt + 1]))
            tap = p.ck("scalar")
            if store_xn_dram:
                p.wait("gpsimd", tap)
                tok_dxn = p.dma("gpsimd", d_xn[ts(tt), :], tB[:, 0:1024])
            p.wait("tensor", tap)
            evs = {}
            for c in range(NCH):
                bi = 6 + (c % 2)
                bank = ps[bi]
                p.wait("tensor", bank_guard.get(bi))
                p.on("tensor", lambda te, c=c, bank=bank: te.transpose(
                    bank[0:128, 0:128], tB[:, ts(c)], ident[:, :]))
                tmm = p.ck("tensor")
                p.wait("vector", tmm)
                p.on("vector", lambda v, c=c, tt=tt, bank=bank: v.tensor_copy(
                    xnT[:, c * S + tt * 128: c * S + tt * 128 + 128], bank[0:128, 0:128]))
                evs[c] = p.ck("vector")
                bank_guard[bi] = evs[c]
            p.wait("scalar", evs[NCH - 1])
            p.wait("sync", tap)
        p.barrier()

    def load_w_slices(dw, os0, width, buf, nch=NCH, bufoff=0):
        src = dw.rearrange("(c r) o -> r c o", r=128)[:, :, os0:os0 + width]
        dst = buf.ap()[:, bufoff:bufoff + nch * width].rearrange("p (c o) -> p c o", c=nch)
        return p.dma("sync", dst, src)

    def bias_row(dbias, n, off=0, buf=None):
        buf = buf or brow
        return p.dma("sync", buf[0:1, off:off + n], dbias.unsqueeze(0)[0:1, 0:n])

    def emit_mha(w_qkv_inmajor, qkv_perhead_cols, b_qkv, w_out_inmajor, b_out,
                 causal=False, stabilize=False, mask_fm=None, dst_s=None,
                 scale=0.125, proj_fm_dst=None, proj_post_evac=None):
        V2 = tF
        for hp in range(8):
            h0 = 2 * hp
            wtoks = []
            for j, kind in enumerate((0, 1, 2)):
                for hh in range(2):
                    qc, kc_, vc = qkv_perhead_cols(h0 + hh)
                    col0 = (qc, kc_, vc)[kind]
                    src = w_qkv_inmajor.rearrange("(c r) o -> r c o", r=128)[:, :, col0:col0 + 64]
                    dstap = wa.ap()[:, kind * 1024:kind * 1024 + 8 * 128].rearrange(
                        "p (c o) -> p c o", c=8)[:, :, hh * 64:hh * 64 + 64]
                    wtoks.append(p.dma("sync", dstap, src))
            btoks = []
            for hh in range(2):
                qc, kc_, vc = qkv_perhead_cols(h0 + hh)
                btoks.append(p.dma("sync", sc1[hh * 64:hh * 64 + 64, 8:9], b_qkv[qc:qc + 64].unsqueeze(1)))
                btoks.append(p.dma("sync", sc2[hh * 64:hh * 64 + 64, 8:9], b_qkv[kc_:kc_ + 64].unsqueeze(1)))
                btoks.append(p.dma("sync", rrow[0:1, hh * 64:hh * 64 + 64], b_qkv[vc:vc + 64].unsqueeze(0)))
            p.wait("tensor", wtoks)
            p.wait("tensor", btoks)
            p.wait("scalar", btoks)
            for dst, woff, bcol in ((tD, 0, sc1), (tC, 1024, sc2)):
                evtoks = {}
                for qs in range(4):
                    bank = ps[4 + (qs % 2)]
                    if qs >= 2:
                        p.wait("tensor", evtoks[qs - 2])
                    for c in range(NCH):
                        p.on("tensor", lambda te, c=c, qs=qs, woff=woff, bank=bank:
                             te.matmul(bank[:, 0:512], wa.ap()[:, woff + c * 128:woff + c * 128 + 128],
                                       xnT[:, c * S + qs * 512:c * S + qs * 512 + 512],
                                       start=(c == 0), stop=(c == NCH - 1)))
                    tmm = p.ck("tensor")
                    p.wait("scalar", tmm)
                    p.on("scalar", lambda s, dst=dst, qs=qs, bank=bank, bcol=bcol:
                         s.activation(dst[:, ts(qs, 512)], bank[:, 0:512], AF.Identity,
                                      bias=bcol[:, 8:9], scale=1.0))
                    evtoks[qs] = p.ck("scalar")
                p.wait("tensor", evtoks[2])
                p.wait("tensor", evtoks[3])
            evtoks = {}
            for tt in range(NT):
                bank = ps[6]
                if tt >= 1:
                    p.wait("tensor", evtoks[tt - 1])
                for c in range(NCH):
                    p.on("tensor", lambda te, c=c, tt=tt, bank=bank:
                         te.matmul(bank[:, 0:128], xnT[:, c * S + tt * 128:c * S + tt * 128 + 128],
                                   wa.ap()[:, 2048 + c * 128:2048 + c * 128 + 128],
                                   start=(c == 0), stop=False))
                p.on("tensor", lambda te, tt=tt, bank=bank:
                     te.matmul(bank[:, 0:128], ones_row[0:1, 0:128], rrow[0:1, 0:128],
                               start=False, stop=True))
                tmm = p.ck("tensor")
                p.wait("vector", tmm)
                p.on("vector", lambda v, tt=tt, bank=bank: v.tensor_copy(V2[:, tt * 132:tt * 132 + 64], bank[:, 0:64]))
                p.on("vector", lambda v, tt=tt, bank=bank: v.tensor_copy(V2[:, tt * 132 + 66:tt * 132 + 130], bank[:, 64:128]))
                evtoks[tt] = p.ck("vector")
            V2r = V2.ap().rearrange("p (t o) -> p t o", t=NT)
            p.on("vector", lambda v: v.memset(V2r[:, :, 64:65], 1.0))
            p.on("vector", lambda v: v.memset(V2r[:, :, 130:131], 1.0))
            tones = p.ck("vector")
            p.wait("tensor", tones)

            prev_exp_last = None
            prev_av_last = None
            for hh in range(2):
                qoff = 64 * hh
                voff = 66 * hh
                for qs in range(4):
                    nkc = (4 * qs + 4) if causal else NT
                    av_bank = ps[2 + hh]
                    exp_ready = {}
                    av_done = {}
                    for kc in range(nkc):
                        sb_bank = ps[kc % 2]
                        if kc >= 2:
                            p.wait("tensor", exp_ready[kc - 2])
                        elif kc == 0 and prev_exp_last is not None:
                            p.wait("tensor", prev_exp_last)
                        p.on("tensor", lambda te, kc=kc, qs=qs, qoff=qoff, sb_bank=sb_bank:
                             te.matmul(sb_bank[:, 0:512],
                                       tC[qoff:qoff + 64, kc * 128:kc * 128 + 128],
                                       tD[qoff:qoff + 64, qs * 512:qs * 512 + 512],
                                       start=True, stop=True))
                        tsc = p.ck("tensor")
                        p.wait("scalar", tsc)
                        if kc >= 4:
                            p.wait("scalar", av_done[kc - 4])
                        elif kc == 0 and prev_av_last is not None:
                            p.wait("scalar", prev_av_last)
                        ebuf = expb[:, ts(kc % 4, 512)]
                        p.on("scalar", lambda s, ebuf=ebuf, sb_bank=sb_bank:
                             s.activation(ebuf, sb_bank[:, 0:512], AF.Exp, scale=scale))
                        texp = p.ck("scalar")
                        if causal and kc >= 4 * qs:
                            r = kc - 4 * qs
                            p.wait("vector", texp)
                            p.on("vector", lambda v, ebuf=ebuf, r=r:
                                 v.tensor_mul(ebuf, ebuf, masks[:, ts(r, 512)]))
                            texp = p.ck("vector")
                        if mask_fm is not None:
                            p.wait("vector", texp)
                            mask_fm(kc, qs, ebuf)
                            texp = p.ck("vector")
                        exp_ready[kc] = texp
                        p.wait("tensor", texp)
                        p.on("tensor", lambda te, kc=kc, voff=voff, av_bank=av_bank, ebuf=ebuf, nkc=nkc:
                             te.matmul(av_bank[0:65, 0:512], V2[:, kc * 132 + voff:kc * 132 + voff + 65],
                                       ebuf, start=(kc == 0), stop=(kc == nkc - 1)))
                        av_done[kc] = p.ck("tensor")
                    p.wait("vector", av_done[nkc - 1])
                    p.on("vector", lambda v, av_bank=av_bank: v.reciprocal(
                        rrow[0:1, 1024:1536], av_bank[64:65, 0:512]))
                    trc = p.ck("vector")
                    p.wait("tensor", trc)
                    p.on("tensor", lambda te: te.matmul(ps[4][0:64, 0:512], ones_row[0:1, 0:64],
                                                        rrow[0:1, 1024:1536], start=True, stop=True))
                    tbc = p.ck("tensor")
                    p.wait("vector", tbc)
                    p.on("vector", lambda v: v.tensor_copy(tE[0:64, 512:1024], ps[4][0:64, 0:512]))
                    p.on("vector", lambda v, av_bank=av_bank: v.tensor_mul(
                        tE[0:64, 0:512], av_bank[0:64, 0:512], tE[0:64, 512:1024]))
                    tn = p.ck("vector")
                    p.wait("gpsimd", tn)
                    tok_park = p.dma("gpsimd", d_av[64 * (h0 + hh):64 * (h0 + hh) + 64, qs * 512:qs * 512 + 512],
                                     tE[0:64, 0:512])
                    p.wait("vector", tok_park)
                    p.wait("tensor", tok_park)
                    prev_exp_last = exp_ready[nkc - 1]
                    prev_av_last = av_done[nkc - 1]
            p.barrier()

        if proj_fm_dst is None:
            proj_from_fm_dram(d_av, w_out_inmajor, b_out, dst_s)
        else:
            dstd, row0 = proj_fm_dst
            proj_fm(d_av, w_out_inmajor, b_out, dstd, post_evac=proj_post_evac, dst_row0=row0)

    def proj_from_fm_dram(d_src_fm, w_inmajor, b_dram, dst_dram, nch_in=8, dout=1024):
        tb = bias_row(b_dram, dout, off=0) if b_dram is not None else None
        if tb is not None:
            p.wait("tensor", tb)
        nos = dout // 512
        nhalf = (nch_in + 7) // 8
        prev_te = None
        prev_ev = None
        for tg in range(4):
            for half in range(nhalf):
                nchh = min(8, nch_in - half * 8)
                if prev_te is not None:
                    p.wait("sync", prev_te)
                toks = []
                for os_, buf in ((0, wa), (1, wb))[:nos]:
                    src = w_inmajor.rearrange("(c r) o -> r c o", r=128)[
                        :, half * 8:half * 8 + nchh, os_ * 512:os_ * 512 + 512]
                    dstap = buf.ap()[:, 0:nchh * 512].rearrange("p (c o) -> p c o", c=nchh)
                    toks.append(p.dma("sync", dstap, src))
                for c in range(nchh):
                    dstb = tC if c < 4 else tD
                    toks.append(p.dma("sync", dstb[:, ts(c % 4, 512)],
                                      d_src_fm[ts(half * 8 + c), tg * 512:tg * 512 + 512]))
                p.wait("tensor", toks)
                if prev_ev is not None and half == 0:
                    p.wait("tensor", prev_ev)
                for tt4 in range(4):
                    for os_ in range(nos):
                        bank = ps[tt4 * 2 + os_]
                        wbuf = wa if os_ == 0 else wb
                        for c in range(nchh):
                            dstb = tC if c < 4 else tD
                            p.on("tensor", lambda te, c=c, tt4=tt4, bank=bank, wbuf=wbuf, dstb=dstb, half=half:
                                 te.matmul(bank[:, 0:512],
                                           dstb[:, (c % 4) * 512 + tt4 * 128:(c % 4) * 512 + tt4 * 128 + 128],
                                           wbuf.ap()[:, c * 512:c * 512 + 512],
                                           start=(half == 0 and c == 0), stop=False))
                prev_te = p.ck("tensor")
            evs = []
            for tt4 in range(4):
                tt = tg * 4 + tt4
                for os_ in range(nos):
                    bank = ps[tt4 * 2 + os_]
                    p.on("tensor", lambda te, bank=bank, os_=os_:
                         te.matmul(bank[:, 0:512], ones_row[0:1, 0:128],
                                   brow[0:1, os_ * 512:os_ * 512 + 512], start=False, stop=True)
                         if b_dram is not None else
                         te.matmul(bank[:, 0:512], ones_row[0:1, 0:128],
                                   brow[0:1, os_ * 512:os_ * 512 + 512], start=False, stop=True))
                    tmm = p.ck("tensor")
                    last_tmm = tmm
                    p.wait("vector", tmm)
                    dstt = tE if os_ == 0 else tF
                    p.on("vector", lambda v, bank=bank, tt4=tt4, dstt=dstt:
                         v.tensor_copy(dstt[:, tt4 * 512:tt4 * 512 + 512], bank[:, 0:512]))
                    evs.append(p.ck("vector"))
                p.wait("gpsimd", evs[-1])
                tok_s1 = p.dma("gpsimd", dst_dram[ts(tt), 0:512], tE[:, tt4 * 512:tt4 * 512 + 512])
                if nos > 1:
                    tok_s1 = p.dma("gpsimd", dst_dram[ts(tt), 512:1024], tF[:, tt4 * 512:tt4 * 512 + 512])
                p.wait("vector", tok_s1)
            prev_te = last_tmm
            prev_ev = evs[-1]
        p.barrier()

    def proj_fm(d_src_fm, w_inmajor, b_dram, dst_fm_dram, nch_in=8, dout=1024,
                post_evac=None, dst_row0=0):
        tb = p.dma("sync", sc1[:, 0:dout // 128],
                   b_dram.rearrange("(c p) -> p c", p=128), allow_slow_non_contiguous=True)
        p.wait("scalar", tb)
        prev_te = None
        for oc in range(dout // 128):
            bufw = wa if oc % 2 == 0 else wb
            if prev_te is not None:
                p.wait("sync", prev_te)
            tokw = p.dma("sync", bufw.ap()[:, 0:nch_in * 128].rearrange("p (c o) -> p c o", c=nch_in),
                         w_inmajor.rearrange("(c r) o -> r c o", r=128)[:, :, ts(oc)])
            p.wait("tensor", tokw)
            for qs in range(4):
                toks = []
                if prev_te is not None:
                    p.wait("sync", prev_te)
                for c in range(nch_in):
                    dstb = (tC, tD)[c // 4]
                    toks.append(p.dma("sync", dstb[:, ts(c % 4, 512)],
                                      d_src_fm[ts(c), qs * 512:qs * 512 + 512]))
                p.wait("tensor", toks)
                bi = qs % 2
                p.wait("tensor", bank_guard.get(bi))
                for c in range(nch_in):
                    dstb = (tC, tD)[c // 4]
                    p.on("tensor", lambda te, c=c, qs=qs, bi=bi, bufw=bufw, dstb=dstb:
                         te.matmul(ps[bi][:, 0:512], bufw.ap()[:, c * 128:c * 128 + 128],
                                   dstb[:, ts(c % 4, 512)], start=(c == 0), stop=(c == nch_in - 1)))
                tmm = p.ck("tensor")
                prev_te = tmm
                p.wait("scalar", tmm)
                p.on("scalar", lambda sx, bi=bi, oc=oc: sx.activation(
                    tB[:, 0:512], ps[bi][:, 0:512], AF.Identity, bias=sc1[:, oc:oc + 1], scale=1.0))
                tev = p.ck("scalar")
                bank_guard[bi] = tev
                if post_evac is not None:
                    tev = post_evac(oc, qs, tB[:, 0:512], tev)
                p.wait("gpsimd", tev)
                tok_st = p.dma("gpsimd", dst_fm_dram[dst_row0 + oc * 128:dst_row0 + oc * 128 + 128,
                                                     qs * 512:qs * 512 + 512], tB[:, 0:512])
                p.wait("scalar", tok_st)
                p.wait("vector", tok_st)
        p.barrier()

    def l0_seg_and_router():
        tokp = p.dma("sync", wa[0:128, 0:8 * 8].rearrange("p (c o) -> p c o", c=8),
                     d_w["protos"].rearrange("(c r) o -> r c o", r=128))
        tokr = p.dma("sync", wb[0:128, 0:8 * 16].rearrange("p (c o) -> p c o", c=8),
                     din["L0_ffn_router_w"].rearrange("(c r) o -> r c o", r=128))
        tok8 = None
        p.on("gpsimd", lambda g: g.iota(tB[:, 0:16], [[1, 16]], base=0, channel_multiplier=0,
                                        allow_small_or_imprecise_dtypes=True))
        tio = p.ck("gpsimd")
        p.wait("tensor", tokp)
        p.wait("tensor", tokr)
        p.wait("vector", tio)
        for tt in range(NT):
            bi = 4 + (tt % 2)
            p.wait("tensor", bank_guard.get(bi))
            for c in range(NCH):
                p.on("tensor", lambda te, c=c, tt=tt, bi=bi: te.matmul(
                    ps[bi][:, 0:8], wa.ap()[:, c * 8:c * 8 + 8],
                    xnT[:, c * S + tt * 128:c * S + tt * 128 + 128].rearrange("p t -> p t") if False else xnT[:, c * S + tt * 128:c * S + tt * 128 + 128],
                    start=(c == 0), stop=False) if False else te.matmul(
                    ps[bi][0:128, 0:8], xnT[:, c * S + tt * 128:c * S + tt * 128 + 128],
                    wa.ap()[:, c * 8:c * 8 + 8], start=(c == 0), stop=(c == NCH - 1)))
            tmm = p.ck("tensor")
            p.wait("vector", tmm)
            p.on("vector", lambda v, bi=bi: v.tensor_copy(tA[:, 0:8], ps[bi][0:128, 0:8]))
            p.ckw("vector")
            p.on("vector", lambda v: v.reduce_max(sc2[:, 0:1], tA[:, 0:8], axis=AX.X))
            p.ckw("vector")
            p.on("vector", lambda v, tt=tt: v.tensor_scalar(tA[:, 8:16], tA[:, 0:8], sc2[:, 0:1], None, OP.is_equal))
            p.ckw("vector")
            p.on("vector", lambda v, tt=tt: v.tensor_mul(tA[:, 16:24], tA[:, 8:16], tB[:, 0:8]))
            p.ckw("vector")
            p.on("vector", lambda v, tt=tt: v.reduce_sum(seg_tm[:, tt:tt + 1], tA[:, 16:24], axis=AX.X))
            p.ckw("vector")
            p.on("vector", lambda v, tt=tt: v.tensor_copy(tF[:, tt * 8:tt * 8 + 8], tA[:, 8:16]))
            tv = p.ck("vector")
            bank_guard[bi] = tv
            last_tv = tv
            bi2 = 6 + (tt % 2)
            p.wait("tensor", bank_guard.get(bi2))
            for c in range(NCH):
                p.on("tensor", lambda te, c=c, tt=tt, bi2=bi2: te.matmul(
                    ps[bi2][0:128, 0:16], xnT[:, c * S + tt * 128:c * S + tt * 128 + 128],
                    wb.ap()[:, c * 16:c * 16 + 16], start=(c == 0), stop=(c == NCH - 1)))
            tmm2 = p.ck("tensor")
            p.wait("vector", tmm2)
            p.on("vector", lambda v, bi2=bi2: v.tensor_copy(tA[:, 32:48], ps[bi2][0:128, 0:16]))
            p.ckw("vector")
            p.on("vector", lambda v: v.reduce_max(sc3[:, 0:1], tA[:, 32:48], axis=AX.X))
            p.ckw("vector")
            p.on("vector", lambda v, tt=tt: v.tensor_scalar(ohR_tm[:, tt * 16:tt * 16 + 16], tA[:, 32:48], sc3[:, 0:1], None, OP.is_equal))
            tv2 = p.ck("vector")
            bank_guard[bi2] = tv2
            p.wait("scalar", tv2)
            p.on("scalar", lambda sx, tt=tt: sx.activation(wtok_tm[:, tt:tt + 1], sc3[:, 0:1], AF.Sigmoid))
            tsg = p.ck("scalar")
            p.wait("vector", tsg)
        bi = 4
        p.wait("tensor", bank_guard.get(bi))
        p.wait("tensor", last_tv)
        for tt in range(NT):
            p.on("tensor", lambda te, tt=tt, bi=bi: te.matmul(
                ps[bi][0:1, 0:8], ones_col[:, 0:1], tF[:, tt * 8:tt * 8 + 8],
                start=(tt == 0), stop=(tt == NT - 1)))
        tmm = p.ck("tensor")
        p.wait("vector", tmm)
        p.on("vector", lambda v, bi=bi: v.tensor_copy(rrow[0:1, 1600:1608], ps[bi][0:1, 0:8]))
        tcv = p.ck("vector")
        bank_guard[bi] = tcv
        p.wait("tensor", tcv)
        bi = 5
        p.wait("tensor", bank_guard.get(bi))
        p.on("tensor", lambda te, bi=bi: te.matmul(ps[bi][0:128, 0:8], ones_row[0:1, 0:128],
                                                   rrow[0:1, 1600:1608], start=True, stop=True))
        tmm = p.ck("tensor")
        p.wait("vector", tmm)
        p.on("vector", lambda v, bi=bi: v.tensor_copy(tA[:, 0:8], ps[bi][0:128, 0:8]))
        tbc = p.ck("vector")
        bank_guard[bi] = tbc
        for tt in range(NT):
            p.on("vector", lambda v, tt=tt: v.tensor_mul(tB[:, 32:40], tF[:, tt * 8:tt * 8 + 8], tA[:, 0:8]))
            p.ckw("vector")
            p.on("vector", lambda v, tt=tt: v.reduce_sum(sc2[:, tt:tt + 1], tB[:, 32:40], axis=AX.X))
            p.ckw("vector")
            p.on("vector", lambda v, tt=tt: v.tensor_scalar(sc2[:, tt:tt + 1], sc2[:, tt:tt + 1], 1.0, None, OP.is_equal))
            tv = p.ckw("vector")
        p.wait("tensor", tv)
        for (srccol, dst, dstoff) in ((seg_tm, rrow, 0), (sc2, brow, 0)):
            for tt in range(NT):
                bi = 6 + (tt % 2)
                p.wait("tensor", bank_guard.get(bi))
                p.on("tensor", lambda te, srccol=srccol, tt=tt, bi=bi: te.transpose(
                    ps[bi][0:1, 0:128], srccol[:, tt:tt + 1], ident[:, :]))
                tmm = p.ck("tensor")
                p.wait("vector", tmm)
                p.on("vector", lambda v, tt=tt, bi=bi, dst=dst, dstoff=dstoff: v.tensor_copy(
                    dst[0:1, dstoff + tt * 128:dstoff + tt * 128 + 128], ps[bi][0:1, 0:128]))
                bank_guard[bi] = p.ck("vector")
        for (srcrow, off, dstt) in ((rrow, 0, segq_bc), (brow, 0, sing_bc)):
            for qs in range(4):
                bi = 4 + (qs % 2)
                p.wait("tensor", bank_guard.get(bi))
                p.on("tensor", lambda te, srcrow=srcrow, off=off, qs=qs, bi=bi: te.matmul(
                    ps[bi][0:128, 0:512], ones_row[0:1, 0:128],
                    srcrow[0:1, off + qs * 512:off + qs * 512 + 512], start=True, stop=True))
                tmm = p.ck("tensor")
                p.wait("vector", tmm)
                p.on("vector", lambda v, qs=qs, bi=bi, dstt=dstt: v.tensor_copy(
                    dstt[:, qs * 512:qs * 512 + 512], ps[bi][0:128, 0:512]))
                bank_guard[bi] = p.ck("vector")
        p.barrier()

    def layer0():
        emit_ln_and_transpose(store_xn_dram=True)
        l0_seg_and_router()
        emit_mha(d_w["lt_w_in"], lambda h: (64 * h, 1024 + 64 * h, 2048 + 64 * h),
                 din["L0_attn_lt_b_in"], d_w["lt_w_out"], din["L0_attn_lt_b_out"],
                 causal=False, scale=0.125, proj_fm_dst=(d_comb, 0))
        if l0_stop == "long":
            return

        def st_mask(kc, qs, ebuf):
            p.on("vector", lambda v, kc=kc, qs=qs, ebuf=ebuf: v.scalar_tensor_tensor(
                ebuf, segq_bc[:, qs * 512:qs * 512 + 512], seg_tm[:, kc:kc + 1], ebuf,
                OP.is_equal, OP.mult))

        def st_where(oc, qs, tile_ap, tok):
            p.wait("vector", tok)
            p.on("vector", lambda v, oc=oc, qs=qs, tile_ap=tile_ap: v.tensor_sub(
                tD[:, 0:512], xnT[:, oc * S + qs * 512:oc * S + qs * 512 + 512], tile_ap))
            p.on("vector", lambda v, qs=qs: v.tensor_mul(
                tD[:, 0:512], tD[:, 0:512], sing_bc[:, qs * 512:qs * 512 + 512]))
            p.on("vector", lambda v, tile_ap=tile_ap: v.tensor_add(tile_ap, tile_ap, tD[:, 0:512]))
            return p.ck("vector")

        emit_mha(d_w["st_w_in"], lambda h: (64 * h, 1024 + 64 * h, 2048 + 64 * h),
                 din["L0_attn_st_b_in"], d_w["st_w_out"], din["L0_attn_st_b_out"],
                 causal=False, scale=0.125, mask_fm=st_mask,
                 proj_fm_dst=(d_comb, 1024), proj_post_evac=st_where)
        proj_from_fm_dram(d_comb, d_w["fus_w"], din["L0_attn_fus_b"], d_s, nch_in=16)
        l0_ffn_dense_and_combine()

    def l0_ffn_dense_and_combine():
        for e in range(16):
            tokb1 = p.dma("sync", b1col[:, 0:16],
                          din["L0_ffn_b1"][e].rearrange("(c p) -> p c", p=128),
                          allow_slow_non_contiguous=True)
            tokb2 = p.dma("sync", brow[0:1, 0:1024], din["L0_ffn_b2"][e].unsqueeze(0))
            p.wait("scalar", tokb1)
            p.wait("tensor", tokb2)
            for hc in range(16):
                bufw = wa if hc % 2 == 0 else wb
                tokw = p.dma("sync", bufw.ap()[:, 0:1024].rearrange("p (c o) -> p c o", c=8),
                             din["L0_ffn_w1"][e].rearrange("(c r) o -> r c o", r=128)[:, :, ts(hc)])
                p.wait("tensor", tokw)
                for qs in range(4):
                    bi = qs % 4
                    p.wait("tensor", bank_guard.get(bi))
                    for c in range(NCH):
                        p.on("tensor", lambda te, c=c, qs=qs, bi=bi, bufw=bufw: te.matmul(
                            ps[bi][:, 0:512], bufw.ap()[:, c * 128:c * 128 + 128],
                            xnT[:, c * S + qs * 512:c * S + qs * 512 + 512],
                            start=(c == 0), stop=(c == NCH - 1)))
                    tmm = p.ck("tensor")
                    p.wait("scalar", tmm)
                    p.on("scalar", lambda sx, bi=bi, hc=hc: sx.activation(
                        tE[:, 0:512], ps[bi][:, 0:512], AF.Gelu, bias=b1col[:, hc:hc + 1], scale=1.0))
                    tev = p.ck("scalar")
                    bank_guard[bi] = tev
                    p.wait("gpsimd", tev)
                    tok_st = p.dma("gpsimd", d_t1[ts(hc), qs * 512:qs * 512 + 512], tE[:, 0:512])
                    p.wait("scalar", tok_st)
                tte = p.ck("tensor") if p._dirty["tensor"] else tmm
                p.wait("sync", tte)
            for tg in range(4):
                for hc in range(16):
                    bufw = wa if hc % 2 == 0 else wb
                    tokw = p.dma("sync", bufw[0:128, 0:1024], din["L0_ffn_w2"][e][ts(hc), :])
                    tokh = p.dma("sync", tC[:, 0:512], d_t1[ts(hc), tg * 512:tg * 512 + 512])
                    p.wait("tensor", tokw)
                    p.wait("tensor", tokh)
                    for tt4 in range(4):
                        for os_ in range(2):
                            bi = tt4 * 2 + os_
                            if hc == 0:
                                p.wait("tensor", bank_guard.get(bi))
                            p.on("tensor", lambda te, tt4=tt4, os_=os_, bi=bi, hc=hc, bufw=bufw:
                                 te.matmul(ps[bi][:, 0:512], tC[:, tt4 * 128:tt4 * 128 + 128],
                                           bufw.ap()[:, os_ * 512:os_ * 512 + 512],
                                           start=(hc == 0), stop=False))
                    tmm = p.ck("tensor")
                    p.wait("sync", tmm)
                for tt4 in range(4):
                    tt = tg * 4 + tt4
                    for os_ in range(2):
                        bi = tt4 * 2 + os_
                        p.on("tensor", lambda te, bi=bi, os_=os_: te.matmul(
                            ps[bi][:, 0:512], ones_row[0:1, 0:128],
                            brow[0:1, os_ * 512:os_ * 512 + 512], start=False, stop=True))
                        tmm = p.ck("tensor")
                        p.wait("vector", tmm)
                        p.on("vector", lambda v, bi=bi, os_=os_, tt=tt, e=e: v.tensor_scalar(
                            tF[:, bi * 256:bi * 256 + 512] if False else tF[:, (bi % 4) * 512:(bi % 4) * 512 + 512],
                            ps[bi][:, 0:512], ohR_tm[:, tt * 16 + e:tt * 16 + e + 1], None, OP.mult))
                        tev = p.ck("vector")
                        bank_guard[bi] = tev
                        p.wait("gpsimd", tev)
                        tok_st = p.dma("gpsimd", d_t2[ts(tt), os_ * 512:os_ * 512 + 512],
                                       tF[:, (bi % 4) * 512:(bi % 4) * 512 + 512],
                                       accum_op=(OP.add if e > 0 else OP.bypass))
                        p.wait("vector", tok_st)
        p.barrier()
        for tt in range(NT):
            tokg = p.dma("sync", tB[:, 0:1024], d_t2[ts(tt), 0:1024])
            tok_xx = p.dma("sync", tC[:, 0:1024], d_x[ts(tt), :])
            tok_ss = p.dma("sync", tC[:, 1024:2048], d_s[ts(tt), :])
            p.wait("vector", tokg)
            p.on("vector", lambda v, tt=tt: v.tensor_scalar(tB[:, 0:1024], tB[:, 0:1024],
                                                            wtok_tm[:, tt:tt + 1], None, OP.mult))
            tv = p.ck("vector")
            p.wait("scalar", tv)
            p.on("scalar", lambda sx: sx.activation(tB[:, 1024:2048], tB[:, 0:1024], AF.Silu))
            tsl = p.ck("scalar")
            p.wait("vector", tsl)
            p.wait("vector", tok_xx)
            p.wait("vector", tok_ss)
            p.on("vector", lambda v: v.tensor_mul(tB[:, 1024:2048], tB[:, 1024:2048], tC[:, 1024:2048]))
            p.on("vector", lambda v: v.tensor_add(tC[:, 0:1024], tC[:, 0:1024], tB[:, 1024:2048]))
            tcb = p.ck("vector")
            p.wait("gpsimd", tcb)
            tok_st = p.dma("gpsimd", d_x[ts(tt), :], tC[:, 0:1024])
            p.wait("sync", tok_st)
            p.wait("scalar", tcb)
        p.barrier()

    L2_D4 = [0.4829629131445341, 0.8365163037378079, 0.2241438680420134, -0.1294095225512604]
    L2_OFFS = sorted({(1 << j) * tau for j in range(11) for tau in range(4) if (1 << j) * tau < S})
    L2_TERMS = {d_: [(j, tau) for j in range(11) for tau in range(1, 4) if (1 << j) * tau == d_]
                for d_ in L2_OFFS if d_ > 0}

    def proj_fm_xnT(w_inmajor, col0, bias_dram, dst_fm, dout=1024, evac_act=AF.Identity):
        if bias_dram is not None:
            tb = p.dma("sync", sc1[:, 0:dout // 128],
                       bias_dram.rearrange("(c p) -> p c", p=128), allow_slow_non_contiguous=True)
            p.wait("scalar", tb)
        prev_te = None
        for oc in range(dout // 128):
            bufw = wa if oc % 2 == 0 else wb
            if prev_te is not None:
                p.wait("sync", prev_te)
            tokw = p.dma("sync", bufw.ap()[:, 0:1024].rearrange("p (c o) -> p c o", c=8),
                         w_inmajor.rearrange("(c r) o -> r c o", r=128)[:, :, col0 + oc * 128:col0 + oc * 128 + 128])
            p.wait("tensor", tokw)
            for qs in range(4):
                bi = qs % 2
                p.wait("tensor", bank_guard.get(bi))
                for c in range(NCH):
                    p.on("tensor", lambda te, c=c, qs=qs, bi=bi, bufw=bufw:
                         te.matmul(ps[bi][:, 0:512], bufw.ap()[:, c * 128:c * 128 + 128],
                                   xnT[:, c * S + qs * 512:c * S + qs * 512 + 512],
                                   start=(c == 0), stop=(c == NCH - 1)))
                tmm = p.ck("tensor")
                prev_te = tmm
                p.wait("scalar", tmm)
                if bias_dram is not None:
                    p.on("scalar", lambda sx, bi=bi, oc=oc, evac_act=evac_act: sx.activation(
                        tB[:, 0:512], ps[bi][:, 0:512], evac_act, bias=sc1[:, oc:oc + 1], scale=1.0))
                else:
                    p.on("scalar", lambda sx, bi=bi, evac_act=evac_act: sx.activation(
                        tB[:, 0:512], ps[bi][:, 0:512], evac_act))
                tev = p.ck("scalar")
                bank_guard[bi] = tev
                p.wait("gpsimd", tev)
                tok_st = p.dma("gpsimd", dst_fm[oc * 128:oc * 128 + 128, qs * 512:qs * 512 + 512], tB[:, 0:512])
                p.wait("scalar", tok_st)
        p.barrier()

    def proj_tm_xnT(w_inmajor, col0, bias_dram, dst_tm, dout=1024, evac_act=None,
                    tile_hook=None):
        if bias_dram is not None:
            tb = p.dma("sync", brow[0:1, 0:dout], bias_dram.unsqueeze(0)[0:1, col0:col0 + dout]
                       if False else bias_dram.unsqueeze(0)[0:1, 0:dout])
            p.wait("tensor", tb)
        nos = dout // 512
        wtoks = []
        for os_ in range(nos):
            buf = wa if os_ == 0 else wb
            wtoks.append(p.dma("sync", buf.ap()[:, 0:4096].rearrange("p (c o) -> p c o", c=8),
                               w_inmajor.rearrange("(c r) o -> r c o", r=128)[
                                   :, :, col0 + os_ * 512:col0 + os_ * 512 + 512]))
        p.wait("tensor", wtoks)
        for tt in range(NT):
            evs = []
            for os_ in range(nos):
                bi = (tt % 2) * 2 + os_
                buf = wa if os_ == 0 else wb
                p.wait("tensor", bank_guard.get(bi))
                for c in range(NCH):
                    p.on("tensor", lambda te, c=c, tt=tt, bi=bi, buf=buf:
                         te.matmul(ps[bi][:, 0:512], xnT[:, c * S + tt * 128:c * S + tt * 128 + 128],
                                   buf.ap()[:, c * 512:c * 512 + 512], start=(c == 0), stop=False))
                if bias_dram is not None:
                    p.on("tensor", lambda te, bi=bi, os_=os_: te.matmul(
                        ps[bi][:, 0:512], ones_row[0:1, 0:128],
                        brow[0:1, os_ * 512:os_ * 512 + 512], start=False, stop=True))
                else:
                    p.on("tensor", lambda te, bi=bi, os_=os_: te.matmul(
                        ps[bi][:, 0:512], ones_row[0:1, 0:128],
                        brow[0:1, os_ * 512:os_ * 512 + 512], start=False, stop=True))
                tmm = p.ck("tensor")
                eng = "vector" if evac_act is None else "scalar"
                p.wait(eng, tmm)
                if evac_act is None:
                    p.on("vector", lambda v, bi=bi, os_=os_: v.tensor_copy(tB[:, os_ * 512:os_ * 512 + 512], ps[bi][:, 0:512]))
                else:
                    p.on("scalar", lambda sx, bi=bi, os_=os_, evac_act=evac_act: sx.activation(
                        tB[:, os_ * 512:os_ * 512 + 512], ps[bi][:, 0:512], evac_act))
                tev = p.ck(eng)
                bank_guard[bi] = tev
                evs.append(tev)
            if tile_hook is not None:
                tev = tile_hook(tt, tB[:, 0:dout], evs)
                evs = [tev]
            p.wait("gpsimd", evs)
            tok_st = p.dma("gpsimd", dst_tm[ts(tt), 0:dout], tB[:, 0:dout])
            p.wait("scalar", tok_st)
            p.wait("vector", tok_st)
        p.barrier()

    def l2_attn():
        proj_fm_xnT(d_w["l2_qkv"], 0, din["L2_attn_b_qkv"][0:1024], d_q2)
        proj_fm_xnT(d_w["l2_qkv"], 1024, din["L2_attn_b_qkv"][1024:2048], d_k2)
        proj_tm_xnT(d_w["l2_qkv"], 2048, din["L2_attn_b_qkv"][2048:3072], d_v2)
        p.on("gpsimd", lambda g: g.iota(tA[:, 0:16], [[1, 16]], base=0, channel_multiplier=0,
                                        allow_small_or_imprecise_dtypes=True))
        p.on("gpsimd", lambda g: g.iota(sc3[:, 0:1], [[0, 1]], base=0, channel_multiplier=1,
                                        allow_small_or_imprecise_dtypes=True))
        tgp = p.ck("gpsimd")
        p.wait("vector", tgp)
        p.on("vector", lambda v: v.tensor_scalar(sc3[:, 1:2], sc3[:, 0:1], 63.5, None, OP.is_gt))
        p.ckw("vector")
        for c in range(8):
            p.on("vector", lambda v, c=c: v.tensor_scalar(sc3[:, 2:3], sc3[:, 1:2], float(2 * c), None, OP.add))
            p.ckw("vector")
            p.on("vector", lambda v, c=c: v.tensor_scalar(tF[:, c * 16:c * 16 + 16], tA[:, 0:16],
                                                          sc3[:, 2:3], None, OP.is_equal))
            tE16 = p.ckw("vector")
        p.wait("tensor", tE16)
        tpr = p.dma("sync", sc2[0:16, 0:11], din["L2_attn_scale_gain"].rearrange("j h -> h j"),
                    allow_slow_non_contiguous=True)
        tqs = p.dma("sync", tA[0:64, 32:43], d_w["l2_qscale"][0:64, 0:11])
        tqs2 = p.dma("sync", tA[64:128, 32:43], d_w["l2_qscale"][0:64, 0:11])
        p.wait("vector", tqs2)
        p.wait("vector", tpr)
        tib = p.dma("sync", sc2[0:16, 11:12], din["L2_attn_identity_bypass"].unsqueeze(1))
        p.wait("scalar", tib)
        p.on("scalar", lambda sx: sx.activation(sc2[0:16, 12:13], sc2[0:16, 11:12], AF.Exp))
        tso = p.ck("scalar")
        p.wait("vector", tso)
        p.on("vector", lambda v: v.tensor_scalar(sc2[0:16, 12:13], sc2[0:16, 12:13], 1.0, None, OP.add))
        tso2 = p.ck("vector")
        p.wait("scalar", tso2)
        p.on("scalar", lambda sx: sx.activation(sc2[0:16, 12:13], sc2[0:16, 12:13], AF.Ln))
        tso = p.ck("scalar")
        p.wait("vector", tso)
        qkbuf = tE
        for qs in range(4):
            p.wait("sync", ("vector", p.cnt["vector"]))
            tokq = p.dma("sync", wa[0:128, 0:4096],
                         d_q2.rearrange("(c r) n -> r c n", r=128)[:, :, qs * 512:qs * 512 + 512])
            p.wait("vector", tokq)
            for j in range(11):
                bi = 4 + (j % 2)
                p.wait("tensor", bank_guard.get(bi))
                tvp = None
                for c in range(NCH):
                    p.on("vector", lambda v, c=c, j=j: v.tensor_scalar(
                        tB[:, (c % 2) * 512:(c % 2) * 512 + 512], wa[0:128, c * 512:c * 512 + 512],
                        tA[:, 32 + j:33 + j], None, OP.mult))
                    tvp = p.ck("vector")
                    p.wait("tensor", tvp)
                    p.on("tensor", lambda te, c=c, j=j, bi=bi: te.matmul(
                        ps[bi][0:16, 0:512], tF[:, c * 16:c * 16 + 16],
                        tB[:, (c % 2) * 512:(c % 2) * 512 + 512], start=(c == 0), stop=(c == NCH - 1)))
                    if c < NCH - 1:
                        tmm_g = p.ck("tensor")
                        p.wait("vector", tmm_g)
                tmm = p.ck("tensor")
                p.wait("scalar", tmm)
                p.on("scalar", lambda sx, j=j, bi=bi: sx.activation(
                    ti32.ap().bitcast(F32)[0:16, 0:2048][0:16, 0:512] if False else
                    qg[0:16, j * 512:j * 512 + 512], ps[bi][0:16, 0:512], AF.Identity,
                    bias=sc2[0:16, j:j + 1], scale=1.0))
                tev = p.ck("scalar")
                bank_guard[bi] = tev
            p.wait("vector", tev)
            p.wait("scalar", tev)
            p.on("scalar", lambda sx: sx.activation(qg[0:16, 0:5632], qg[0:16, 0:5632], AF.Exp))
            tse = p.ck("scalar")
            p.wait("vector", tse)
            p.on("vector", lambda v: v.tensor_copy(tB[0:16, 0:512], qg[0:16, 0:512]))
            for j in range(1, 11):
                p.on("vector", lambda v, j=j: v.tensor_add(tB[0:16, 0:512], tB[0:16, 0:512],
                                                           qg[0:16, j * 512:j * 512 + 512]))
            p.on("vector", lambda v: v.reciprocal(tB[0:16, 512:1024], tB[0:16, 0:512]))
            for j in range(11):
                p.on("vector", lambda v, j=j: v.tensor_mul(qg[0:16, j * 512:j * 512 + 512],
                                                           qg[0:16, j * 512:j * 512 + 512], tB[0:16, 512:1024]))
            tgn = p.ck("vector")
            p.wait("gpsimd", tgn)
            for j in range(11):
                p.dma("gpsimd", d_gains[0:16, j * S + qs * 512:j * S + qs * 512 + 512],
                      qg[0:16, j * 512:j * 512 + 512])
            for di, d_ in enumerate(L2_OFFS):
                lo = qs * 512 - d_
                tokk = None
                if lo + 512 > 0:
                    a0 = max(0, lo)
                    p.wait("sync", ("vector", p.cnt["vector"]))
                    tokk = p.dma("sync", wb.ap()[:, 0:4096].rearrange("p (c n) -> p c n", c=8)[
                        :, :, (a0 - lo):512],
                        d_k2.rearrange("(c r) n -> r c n", r=128)[:, :, a0:lo + 512])
                if lo < 0:
                    zer = min(-lo, 512)
                    p.on("vector", lambda v, zer=zer: v.memset(
                        wb.ap()[:, 0:4096].rearrange("p (c n) -> p c n", c=8)[:, :, 0:zer], 0.0))
                tz = p.ck("vector") if lo < 0 else None
                p.wait("vector", tokk)
                bi = 4 + (di % 2)
                p.wait("tensor", bank_guard.get(bi))
                p.wait("tensor", tz)
                tvp = None
                for c in range(NCH):
                    p.on("vector", lambda v, c=c: v.tensor_mul(
                        tB[:, (c % 2) * 512:(c % 2) * 512 + 512], wa[0:128, c * 512:c * 512 + 512],
                        wb[0:128, c * 512:c * 512 + 512]))
                    tvp = p.ck("vector")
                    p.wait("tensor", tvp)
                    p.on("tensor", lambda te, c=c, bi=bi: te.matmul(
                        ps[bi][0:16, 0:512], tF[:, c * 16:c * 16 + 16],
                        tB[:, (c % 2) * 512:(c % 2) * 512 + 512], start=(c == 0), stop=(c == NCH - 1)))
                    if c < NCH - 1:
                        tmm_g = p.ck("tensor")
                        p.wait("vector", tmm_g)
                tmm = p.ck("tensor")
                p.wait("vector", tmm)
                p.on("vector", lambda v, bi=bi: v.tensor_copy(tB[0:16, 1024:1536], ps[bi][0:16, 0:512]))
                tqk = p.ck("vector")
                bank_guard[bi] = tqk
                p.wait("vector", ("tensor", p.cnt["tensor"]))
                if d_ == 0:
                    p.on("vector", lambda v: v.tensor_scalar(sc3[0:16, 3:4], sc2[0:16, 12:13],
                                                             L2_D4[0], None, OP.add))
                    p.ckw("vector")
                    p.on("vector", lambda v: v.tensor_scalar(tB[0:16, 1536:2048], tB[0:16, 1024:1536],
                                                             sc3[0:16, 3:4], None, OP.mult))
                else:
                    terms = L2_TERMS[d_]
                    j0, tau0 = terms[0]
                    p.on("vector", lambda v, j0=j0, tau0=tau0: v.tensor_scalar(
                        tB[0:16, 512:1024], qg[0:16, j0 * 512:j0 * 512 + 512], L2_D4[tau0], None, OP.mult))
                    for (j1, tau1) in terms[1:]:
                        p.on("vector", lambda v, j1=j1, tau1=tau1: v.scalar_tensor_tensor(
                            tB[0:16, 512:1024], qg[0:16, j1 * 512:j1 * 512 + 512], L2_D4[tau1],
                            tB[0:16, 512:1024], OP.mult, OP.add))
                    p.on("vector", lambda v: v.tensor_mul(tB[0:16, 1536:2048], tB[0:16, 1024:1536],
                                                          tB[0:16, 512:1024]))
                twd = p.ck("vector")
                p.wait("tensor", twd)
                for t4 in range(4):
                    bi2 = 6 + (t4 % 2)
                    p.wait("tensor", bank_guard.get(bi2))
                    p.on("tensor", lambda te, t4=t4, bi2=bi2: te.transpose(
                        ps[bi2][0:128, 0:16], tB[0:16, 1536 + t4 * 128:1536 + t4 * 128 + 128],
                        ident[0:16, 0:16]))
                    tmm2 = p.ck("tensor")
                    p.wait("vector", tmm2)
                    p.on("vector", lambda v, t4=t4, bi2=bi2, di=di: v.tensor_copy(
                        tC[:, di * 64 + t4 * 16:di * 64 + t4 * 16 + 16], ps[bi2][0:128, 0:16]))
                    bank_guard[bi2] = p.ck("vector")
            p.wait("gpsimd", bank_guard.get(6))
            p.wait("gpsimd", bank_guard.get(7))
            tok_sds = []
            for t4 in range(4):
                tok_sd = p.dma("gpsimd",
                               d_sd[qs * 512 + t4 * 128:qs * 512 + t4 * 128 + 128, 0:352].rearrange(
                                   "p (dd c) -> p dd c", dd=22),
                               tC.ap()[:, 0:22 * 64].rearrange("p (dd tc) -> p dd tc", dd=22)[
                                   :, :, t4 * 16:t4 * 16 + 16])
                tok_sds.append(tok_sd)
            p.wait("vector", tok_sds)
            p.wait("sync", tok_sds)
        p.barrier()
        for tt in range(NT):
            tok_s = p.dma("sync", tC[:, 0:352], d_sd[ts(tt), 0:352])
            p.wait("vector", tok_s)
            first = True
            for di, d_ in enumerate(L2_OFFS):
                lo = tt * 128 - d_
                if lo + 128 <= 0:
                    continue
                a0 = max(0, lo)
                tok_v = p.dma("sync", tB[(a0 - lo):128, 0:1024], d_v2[a0:lo + 128, 0:1024])
                p.wait("vector", tok_v)
                if lo < 0:
                    p.on("vector", lambda v, lo=lo: v.memset(tB[0:-lo, 0:1024], 0.0))
                sbc = tC.ap()[:, di * 16:di * 16 + 16] if False else None
                if first:
                    p.on("vector", lambda v, di=di: v.tensor_tensor(
                        tD.ap()[:, 0:1024].rearrange("p (h e) -> p h e", h=16),
                        tB.ap()[:, 0:1024].rearrange("p (h e) -> p h e", h=16),
                        tC.ap()[:, di * 16:di * 16 + 16].unsqueeze(2).to_broadcast([128, 16, 64]),
                        OP.mult))
                else:
                    p.on("vector", lambda v, di=di: v.tensor_mul(
                        tE.ap()[:, 0:1024].rearrange("p (h e) -> p h e", h=16),
                        tB.ap()[:, 0:1024].rearrange("p (h e) -> p h e", h=16),
                        tC.ap()[:, di * 16:di * 16 + 16].unsqueeze(2).to_broadcast([128, 16, 64])))
                    p.on("vector", lambda v: v.tensor_add(tD[:, 0:1024], tD[:, 0:1024], tE[:, 0:1024]))
                first = False
                tacc = p.ck("vector")
                p.wait("sync", tacc)
            p.wait("gpsimd", tacc)
            tok_g = p.dma("gpsimd", d_t1[ts(tt), 0:1024], tD[:, 0:1024])
            p.wait("vector", tok_g)
        p.barrier()

        def l2_gate_hook(tt, tile_ap, evs):
            p.wait("sync", ("tensor", p.cnt["tensor"]))
            tokg = p.dma("sync", tD[:, 0:1024], d_t1[ts(tt), 0:1024])
            p.wait("vector", tokg)
            p.wait("vector", evs)
            p.on("vector", lambda v, tile_ap=tile_ap: v.tensor_mul(tD[:, 0:1024], tD[:, 0:1024], tile_ap))
            tu = p.ck("vector")
            p.wait("tensor", tu)
            for c in range(NCH):
                bi = 6 + (c % 2)
                p.wait("tensor", bank_guard.get(bi))
                p.on("tensor", lambda te, c=c, bi=bi: te.transpose(
                    ps[bi][0:128, 0:128], tD[:, ts(c)], ident[:, :]))
                tmm = p.ck("tensor")
                p.wait("vector", tmm)
                p.on("vector", lambda v, c=c, bi=bi: v.tensor_copy(tE[:, c * 128:c * 128 + 128], ps[bi][0:128, 0:128]))
                bank_guard[bi] = p.ck("vector")
            tev = bank_guard[6 + ((NCH - 1) % 2)]
            p.wait("gpsimd", tev)
            tok_u = p.dma("gpsimd", d_comb[0:1024, ts(tt)].rearrange("(c q) t -> q c t", q=128),
                          tE.ap()[:, 0:1024].rearrange("q (c t) -> q c t", c=8))
            p.wait("vector", tok_u)
            return tev

        proj_tm_xnT(d_w["l2_gate"], 0, din["L2_attn_b_gate"], d_t2, dout=1024,
                    evac_act=AF.Sigmoid, tile_hook=l2_gate_hook)
        if l0_stop == "l2gate":
            return True
        proj_from_fm_dram(d_comb, d_w["l2_out"], din["L2_attn_b_out"], d_s, nch_in=8)

    def l2_ffn_and_combine():
        _stop = l0_stop
        tokw = p.dma("sync", wa[0:128, 0:64].rearrange("p (c o) -> p c o", c=8),
                     d_w["l2f_gw"].rearrange("(c r) o -> r c o", r=128))
        p.wait("tensor", tokw)
        for tt in range(NT):
            bi = 4 + (tt % 2)
            p.wait("tensor", bank_guard.get(bi))
            for c in range(NCH):
                p.on("tensor", lambda te, c=c, tt=tt, bi=bi: te.matmul(
                    ps[bi][0:128, 0:8], xnT[:, c * S + tt * 128:c * S + tt * 128 + 128],
                    wa.ap()[:, c * 8:c * 8 + 8], start=(c == 0), stop=(c == NCH - 1)))
            tmm = p.ck("tensor")
            p.wait("vector", tmm)
            p.on("vector", lambda v, bi=bi: v.tensor_copy(tA[:, 0:8], ps[bi][0:128, 0:8]))
            p.ckw("vector")
            p.on("vector", lambda v: v.reduce_max(sc3[:, 0:1], tA[:, 0:8], axis=AX.X))
            p.ckw("vector")
            p.on("vector", lambda v: v.tensor_scalar(sc3[:, 1:2], sc3[:, 0:1], -1.0, None, OP.mult))
            tvx = p.ck("vector")
            bank_guard[bi] = tvx
            p.wait("scalar", tvx)
            p.wait("scalar", ("tensor", p.cnt["tensor"]))
            p.on("scalar", lambda sx: sx.activation(tA[:, 8:16], tA[:, 0:8], AF.Exp,
                                                    bias=sc3[:, 1:2], scale=1.0, accum_out=accA[:, 0:1]))
            tse = p.ck("scalar")
            p.wait("vector", tse)
            p.on("vector", lambda v: v.reciprocal(sc3[:, 3:4], accA[:, 0:1]))
            p.ckw("vector")
            p.on("vector", lambda v: v.tensor_scalar(tA[:, 8:16], tA[:, 8:16], sc3[:, 3:4], None, OP.mult))
            tg8 = p.ckw("vector")
            p.wait("tensor", tg8)
            bi2 = 6 + (tt % 2)
            p.wait("tensor", bank_guard.get(bi2))
            p.on("tensor", lambda te, bi2=bi2: te.transpose(ps[bi2][0:8, 0:128], tA[:, 8:16], ident[:, :]))
            tmm2 = p.ck("tensor")
            p.wait("vector", tmm2)
            p.on("vector", lambda v, tt=tt, bi2=bi2: v.tensor_copy(qg[0:8, tt * 128:tt * 128 + 128],
                                                                   ps[bi2][0:8, 0:128]))
            bank_guard[bi2] = p.ck("vector")
            if l0_stop == "l2g8" and tt == 0:
                return True
        tglast = bank_guard[6 + ((NT - 1) % 2)]
        p.on("gpsimd", lambda g: g.iota(tB[0:8, 0:128], [[1, 128]], base=0, channel_multiplier=0,
                                        allow_small_or_imprecise_dtypes=True))
        p.on("gpsimd", lambda g: g.iota(tB[0:8, 128:129], [[0, 1]], base=0, channel_multiplier=16,
                                        allow_small_or_imprecise_dtypes=True))
        tgp = p.ck("gpsimd")
        p.wait("vector", tgp)
        p.on("vector", lambda v: v.tensor_scalar(tB[0:8, 256:384], tB[0:8, 0:128], tB[0:8, 128:129], None, OP.is_ge))
        p.ckw("vector")
        p.on("vector", lambda v: v.tensor_scalar(tB[0:8, 128:129], tB[0:8, 128:129], 16.0, None, OP.add))
        p.ckw("vector")
        p.on("vector", lambda v: v.tensor_scalar(tB[0:8, 384:512], tB[0:8, 0:128], tB[0:8, 128:129], None, OP.is_lt))
        p.ckw("vector")
        p.on("vector", lambda v: v.tensor_mul(tB[0:8, 0:128], tB[0:8, 256:384], tB[0:8, 384:512]))
        p.ckw("vector")
        p.on("vector", lambda v: v.tensor_scalar(tB[0:8, 0:128], tB[0:8, 0:128], 1.0 / 16.0, None, OP.mult))
        tE8 = p.ckw("vector")
        p.wait("tensor", tE8)
        p.wait("tensor", tglast)
        for qs in range(4):
            bi = 4 + (qs % 2)
            p.wait("tensor", bank_guard.get(bi))
            p.on("tensor", lambda te, qs=qs, bi=bi: te.matmul(
                ps[bi][0:128, 0:512], tB[0:8, 0:128], qg[0:8, qs * 512:qs * 512 + 512],
                start=True, stop=True))
            tmm = p.ck("tensor")
            p.wait("vector", tmm)
            p.on("vector", lambda v, qs=qs, bi=bi: v.tensor_copy(tD[:, qs * 512:qs * 512 + 512],
                                                                 ps[bi][0:128, 0:512]))
            bank_guard[bi] = p.ck("vector")
        p.wait("sync", ("tensor", p.cnt["tensor"]))
        tokU = p.dma("sync", wa[0:128, 0:1024].rearrange("p (c o) -> p c o", c=8),
                     d_w["l2f_U"].rearrange("(c r) o -> r c o", r=128))
        p.wait("tensor", tokU)
        for qs in range(4):
            bi = qs % 2
            p.wait("tensor", bank_guard.get(bi))
            for c in range(NCH):
                p.on("tensor", lambda te, c=c, qs=qs, bi=bi: te.matmul(
                    ps[bi][:, 0:512], wa.ap()[:, c * 128:c * 128 + 128],
                    xnT[:, c * S + qs * 512:c * S + qs * 512 + 512], start=(c == 0), stop=(c == NCH - 1)))
            tmm = p.ck("tensor")
            p.wait("vector", tmm)
            p.on("vector", lambda v, qs=qs, bi=bi: v.tensor_mul(
                tE[:, qs * 512:qs * 512 + 512], ps[bi][:, 0:512], tD[:, qs * 512:qs * 512 + 512]))
            bank_guard[bi] = p.ck("vector")
        tlast = bank_guard[1]
        tokV = p.dma("sync", wb[0:128, 0:1024], d_w["l2f_V"][0:128, 0:1024])
        p.wait("tensor", tokV)
        p.wait("tensor", tlast)
        p.wait("sync", tlast)
        for os_ in range(2):
            tokw = p.dma("sync", wa.ap()[:, 0:4096].rearrange("p (c o) -> p c o", c=8),
                         d_w["l2f_w"].rearrange("(c r) o -> r c o", r=128)[:, :, os_ * 512:os_ * 512 + 512])
            p.wait("tensor", tokw)
            last_tmm = None
            for tt in range(NT):
                bi = 4 + (tt % 2) * 2 + os_ % 2
                p.wait("tensor", bank_guard.get(bi))
                for c in range(NCH):
                    p.on("tensor", lambda te, c=c, tt=tt, bi=bi: te.matmul(
                        ps[bi][:, 0:512], xnT[:, c * S + tt * 128:c * S + tt * 128 + 128],
                        wa.ap()[:, c * 512:c * 512 + 512], start=(c == 0), stop=False))
                p.on("tensor", lambda te, tt=tt, bi=bi, os_=os_: te.matmul(
                    ps[bi][:, 0:512], tE[:, tt * 128:tt * 128 + 128],
                    wb.ap()[:, os_ * 512:os_ * 512 + 512], start=False, stop=True))
                tmm = p.ck("tensor")
                last_tmm = tmm
                p.wait("scalar", tmm)
                p.on("scalar", lambda sx, bi=bi: sx.activation(tC[:, 0:512], ps[bi][:, 0:512], AF.Silu))
                tev = p.ck("scalar")
                bank_guard[bi] = tev
                p.wait("gpsimd", tev)
                tok_st = p.dma("gpsimd", d_t2[ts(tt), os_ * 512:os_ * 512 + 512], tC[:, 0:512])
                p.wait("scalar", tok_st)
            p.wait("sync", last_tmm)
        p.barrier()
        for tt in range(NT):
            tokg = p.dma("sync", tB[:, 0:1024], d_t2[ts(tt), 0:1024])
            tok_xx = p.dma("sync", tC[:, 0:1024], d_x[ts(tt), :])
            tok_ss = p.dma("sync", tC[:, 1024:2048], d_s[ts(tt), :])
            p.wait("vector", tokg)
            p.wait("vector", tok_xx)
            p.wait("vector", tok_ss)
            p.on("vector", lambda v: v.tensor_mul(tB[:, 1024:2048], tB[:, 0:1024], tC[:, 1024:2048]))
            p.on("vector", lambda v: v.tensor_add(tC[:, 0:1024], tC[:, 0:1024], tB[:, 1024:2048]))
            tcb = p.ck("vector")
            p.wait("gpsimd", tcb)
            tok_st = p.dma("gpsimd", d_x[ts(tt), :], tC[:, 0:1024])
            p.wait("sync", tok_st)
        p.barrier()

    def layer2():
        emit_ln_and_transpose(store_xn_dram=True)
        if l2_attn():
            return
        if l2_ffn_and_combine():
            return

    FQ_LN2_HI = LN2_HI
    FQ_LN2_LO = LN2_LO
    INV_LN2 = 1.4426950408889634

    def emit_float_quantize(dst, src, n, exp_bits, man, rows=128):
        bias = 2 ** (exp_bits - 1) - 1
        min_exp = 1 - bias
        maxv = (2.0 - 2.0 ** (-man)) * 2.0 ** bias
        r_ = slice(0, rows)
        p.on("scalar", lambda sx, src=src: sx.activation(tB[r_, 0:n], src, AF.Abs))
        ts1 = p.ck("scalar")
        p.wait("vector", ts1)
        p.on("vector", lambda v: v.tensor_scalar(tB[r_, 0:n], tB[r_, 0:n], 1e-30, None, OP.max))
        tv1 = p.ck("vector")
        p.wait("scalar", tv1)
        p.on("scalar", lambda sx: sx.activation(tB[r_, 0:n], tB[r_, 0:n], AF.Ln))
        ts2 = p.ck("scalar")
        p.wait("vector", ts2)
        p.on("vector", lambda v: v.tensor_scalar(tB[r_, 0:n], tB[r_, 0:n], INV_LN2, None, OP.mult))
        p.on("vector", lambda v: v.tensor_copy(ti32[r_, 0:n], tB[r_, 0:n]))
        p.on("vector", lambda v: v.tensor_copy(tD[r_, 0:n], ti32[r_, 0:n]))
        p.on("vector", lambda v: v.tensor_sub(tB[r_, 0:n], tB[r_, 0:n], tD[r_, 0:n]))
        p.on("vector", lambda v: v.tensor_scalar(ti32[r_, 0:n].bitcast(F32), tB[r_, 0:n], 0.0, None, OP.is_lt))
        p.on("vector", lambda v: v.tensor_scalar(tB[r_, 0:n], tB[r_, 0:n], 1.0, None, OP.is_ge))
        p.on("vector", lambda v: v.tensor_sub(tD[r_, 0:n], tD[r_, 0:n], ti32[r_, 0:n].bitcast(F32)))
        p.on("vector", lambda v: v.tensor_add(tD[r_, 0:n], tD[r_, 0:n], tB[r_, 0:n]))
        p.on("vector", lambda v: v.tensor_scalar(tD[r_, 0:n], tD[r_, 0:n], float(min_exp), None, OP.max))
        p.on("vector", lambda v: v.tensor_scalar(tD[r_, 0:n], tD[r_, 0:n], -1.0, float(man), OP.mult, OP.add))
        tu = p.ck("vector")
        p.wait("scalar", tu)
        p.on("scalar", lambda sx: sx.activation(tB[r_, 0:n], tD[r_, 0:n], AF.Exp, scale=FQ_LN2_HI))
        ts3 = p.ck("scalar")
        p.wait("vector", ts3)
        p.on("vector", lambda v, src=src: v.tensor_mul(tB[r_, 0:n], tB[r_, 0:n], src))
        tv2 = p.ck("vector")
        p.wait("scalar", tv2)
        p.on("scalar", lambda sx: sx.activation(tE[r_, 0:n], tD[r_, 0:n], AF.Exp, scale=FQ_LN2_LO))
        ts4 = p.ck("scalar")
        p.wait("vector", ts4)
        p.on("vector", lambda v: v.tensor_mul(tB[r_, 0:n], tB[r_, 0:n], tE[r_, 0:n]))
        p.on("vector", lambda v: v.tensor_copy(ti32[r_, 0:n], tB[r_, 0:n]))
        p.on("vector", lambda v: v.tensor_copy(tE[r_, 0:n], ti32[r_, 0:n]))
        p.on("vector", lambda v: v.tensor_sub(tB[r_, 0:n], tB[r_, 0:n], tE[r_, 0:n]))
        p.on("vector", lambda v: v.tensor_scalar(ti32[r_, 0:n].bitcast(F32), tB[r_, 0:n], 0.5, None, OP.is_gt))
        p.on("vector", lambda v: v.tensor_scalar(tB[r_, 0:n], tB[r_, 0:n], -0.5, None, OP.is_lt))
        p.on("vector", lambda v: v.tensor_sub(tE[r_, 0:n], tE[r_, 0:n], tB[r_, 0:n]))
        p.on("vector", lambda v: v.tensor_add(tB[r_, 0:n], tE[r_, 0:n], ti32[r_, 0:n].bitcast(F32)))
        tv3 = p.ck("vector")
        p.wait("scalar", tv3)
        p.on("scalar", lambda sx: sx.activation(tE[r_, 0:n], tD[r_, 0:n], AF.Exp, scale=-FQ_LN2_HI))
        ts5 = p.ck("scalar")
        p.wait("vector", ts5)
        p.on("vector", lambda v: v.tensor_mul(tB[r_, 0:n], tB[r_, 0:n], tE[r_, 0:n]))
        tv4 = p.ck("vector")
        p.wait("scalar", tv4)
        p.on("scalar", lambda sx: sx.activation(tE[r_, 0:n], tD[r_, 0:n], AF.Exp, scale=-FQ_LN2_LO))
        ts6 = p.ck("scalar")
        p.wait("vector", ts6)
        p.on("vector", lambda v: v.tensor_mul(tB[r_, 0:n], tB[r_, 0:n], tE[r_, 0:n]))
        tvq = p.ck("vector")
        p.wait("scalar", tvq)
        p.on("scalar", lambda sx: sx.activation(tE[r_, 0:n], tB[r_, 0:n], AF.Abs))
        tab = p.ck("scalar")
        p.wait("vector", tab)
        p.on("vector", lambda v, maxv=maxv, n=n: v.tensor_scalar(tE[r_, 0:n], tE[r_, 0:n], maxv, None, OP.is_gt))
        tov = p.ck("vector")
        p.wait("scalar", tov)
        p.on("scalar", lambda sx: sx.activation(tE[r_, 0:n], tE[r_, 0:n], AF.Exp, scale=1e5))
        tex = p.ck("scalar")
        p.wait("vector", tex)
        p.on("vector", lambda v, dst=dst, n=n: v.tensor_mul(dst, tB[r_, 0:n], tE[r_, 0:n]))
        return p.ck("vector")

    def l3_attn():
        proj_tm_xnT(din["L3_attn_Wq"], 0, din["L3_attn_bq"], d_q2_tm)
        proj_tm_xnT(din["L3_attn_Wk"], 0, din["L3_attn_bk"], d_k2_tm)
        proj_tm_xnT(din["L3_attn_Wv"], 0, din["L3_attn_bv"], d_v2)
        V2 = tF
        for gp in range(8):
            g0 = 2 * gp
            toks = []
            for (dstT, srcD) in ((tD, d_q2_tm), (tC, d_k2_tm)):
                for hh in range(2):
                    g = g0 + hh
                    src = srcD[128 * g:128 * g + 128, :].rearrange("(c two) d -> c two d", two=2)
                    toks.append(p.dma("sync", dstT[64 * hh:64 * hh + 64, 0:1024], src[:, 0, :]))
                    toks.append(p.dma("sync", dstT[64 * hh:64 * hh + 64, 1024:2048], src[:, 1, :]))
            p.wait("tensor", toks)
            for hh in range(2):
                g = g0 + hh
                srcv = d_v2[128 * g:128 * g + 128, :].rearrange("(c two) d -> c two d", two=2)
                tv1 = p.dma("sync", tE[0:64, 0:1024], srcv[:, 0, :])
                tv2 = p.dma("sync", tE[0:64, 1024:2048], srcv[:, 1, :])
                p.wait("tensor", [tv1, tv2])
                voff = 66 * hh
                for kc in range(NT):
                    bi = 6 + (kc % 2)
                    p.wait("tensor", bank_guard.get(bi))
                    p.on("tensor", lambda te, kc=kc, bi=bi: te.transpose(
                        ps[bi][0:128, 0:64], tE[0:64, kc * 128:kc * 128 + 128], ident[0:64, 0:64]))
                    tmm = p.ck("tensor")
                    p.wait("vector", tmm)
                    p.on("vector", lambda v, kc=kc, bi=bi, voff=voff: v.tensor_copy(
                        V2[:, kc * 132 + voff:kc * 132 + voff + 64], ps[bi][0:128, 0:64]))
                    bank_guard[bi] = p.ck("vector")
                p.wait("sync", bank_guard[6 + ((NT - 1) % 2)])
            V2r = V2.ap().rearrange("p (t o) -> p t o", t=NT)
            p.on("vector", lambda v: v.memset(V2r[:, :, 64:65], 1.0))
            p.on("vector", lambda v: v.memset(V2r[:, :, 130:131], 1.0))
            tones = p.ck("vector")
            p.wait("tensor", tones)
            prev_exp_last = None
            prev_av_last = None
            for hh in range(2):
                qoff = 64 * hh
                voff = 66 * hh
                for qs in range(4):
                    nkc = NT
                    av_bank = ps[2 + hh]
                    exp_ready = {}
                    av_done = {}
                    for kc in range(nkc):
                        sb_bank = ps[kc % 2]
                        if kc >= 2:
                            p.wait("tensor", exp_ready[kc - 2])
                        elif kc == 0 and prev_exp_last is not None:
                            p.wait("tensor", prev_exp_last)
                        p.on("tensor", lambda te, kc=kc, qs=qs, qoff=qoff, sb_bank=sb_bank:
                             te.matmul(sb_bank[:, 0:512],
                                       tC[qoff:qoff + 64, kc * 128:kc * 128 + 128],
                                       tD[qoff:qoff + 64, qs * 512:qs * 512 + 512],
                                       start=True, stop=True))
                        tsc = p.ck("tensor")
                        p.wait("scalar", tsc)
                        if kc >= 4:
                            p.wait("scalar", av_done[kc - 4])
                        elif kc == 0 and prev_av_last is not None:
                            p.wait("scalar", prev_av_last)
                        ebuf = expb[:, ts(kc % 4, 512)]
                        p.on("scalar", lambda sx, ebuf=ebuf, sb_bank=sb_bank:
                             sx.activation(ebuf, sb_bank[:, 0:512], AF.Exp, scale=1.0 / 1024.0))
                        texp = p.ck("scalar")
                        exp_ready[kc] = texp
                        p.wait("tensor", texp)
                        p.on("tensor", lambda te, kc=kc, voff=voff, av_bank=av_bank, ebuf=ebuf, nkc=nkc:
                             te.matmul(av_bank[0:65, 0:512], V2[:, kc * 132 + voff:kc * 132 + voff + 65],
                                       ebuf, start=(kc == 0), stop=(kc == nkc - 1)))
                        av_done[kc] = p.ck("tensor")
                    p.wait("vector", av_done[nkc - 1])
                    p.on("vector", lambda v, av_bank=av_bank: v.reciprocal(
                        rrow[0:1, 1024:1536], av_bank[64:65, 0:512]))
                    trc = p.ck("vector")
                    p.wait("tensor", trc)
                    p.on("tensor", lambda te: te.matmul(ps[4][0:64, 0:512], ones_row[0:1, 0:64],
                                                        rrow[0:1, 1024:1536], start=True, stop=True))
                    tbc = p.ck("tensor")
                    p.wait("vector", tbc)
                    p.on("vector", lambda v: v.tensor_copy(tE[0:64, 512:1024], ps[4][0:64, 0:512]))
                    p.on("vector", lambda v, av_bank=av_bank: v.tensor_mul(
                        tE[0:64, 0:512], av_bank[0:64, 0:512], tE[0:64, 512:1024]))
                    tn = p.ck("vector")
                    p.wait("gpsimd", tn)
                    tok_park = p.dma("gpsimd", d_av[64 * (g0 + hh):64 * (g0 + hh) + 64, qs * 512:qs * 512 + 512],
                                     tE[0:64, 0:512])
                    p.wait("vector", tok_park)
                    p.wait("tensor", tok_park)
                    prev_exp_last = exp_ready[nkc - 1]
                    prev_av_last = av_done[nkc - 1]
            p.barrier()
        for g in range(16):
            tok_a = p.dma("sync", tC[0:64, 0:2048], d_av[64 * g:64 * g + 64, :])
            p.wait("tensor", tok_a)
            for m in range(16):
                bi = 6 + (m % 2)
                p.wait("tensor", bank_guard.get(bi))
                src = tC.ap()[0:64, 0:2048].rearrange("c (j m) -> c m j", m=16)
                p.on("tensor", lambda te, m=m, bi=bi, src=src: te.transpose(
                    ps[bi][0:128, 0:64], src[:, m, :], ident[0:64, 0:64]))
                tmm = p.ck("tensor")
                p.wait("vector", tmm)
                p.on("vector", lambda v, m=m, bi=bi: v.tensor_copy(tD[:, 64 * m:64 * m + 64], ps[bi][0:128, 0:64]))
                bank_guard[bi] = p.ck("vector")
            p.wait("tensor", bank_guard[6])
            p.wait("tensor", bank_guard[7])
            p.wait("sync", bank_guard[7])
            for c in range(NCH):
                bi = 6 + (c % 2)
                p.wait("tensor", bank_guard.get(bi))
                p.on("tensor", lambda te, c=c, bi=bi: te.transpose(
                    ps[bi][0:128, 0:128], tD[:, ts(c)], ident[:, :]))
                tmm = p.ck("tensor")
                p.wait("vector", tmm)
                p.on("vector", lambda v, c=c, bi=bi: v.tensor_copy(tE[:, c * 128:c * 128 + 128], ps[bi][0:128, 0:128]))
                bank_guard[bi] = p.ck("vector")
            p.wait("gpsimd", bank_guard[6])
            p.wait("gpsimd", bank_guard[7])
            tok_u = p.dma("gpsimd", d_comb[0:1024, g * 128:g * 128 + 128].rearrange("(c q) t -> q c t", q=128),
                          tE.ap()[:, 0:1024].rearrange("q (c t) -> q c t", c=8))
            p.wait("vector", tok_u)
        p.barrier()
        proj_from_fm_dram(d_comb, din["L3_attn_Wo"], din["L3_attn_bo"], d_s, nch_in=8)

    def l3_ffn_and_combine():
        for ri in range(32):
            tok = p.dma("sync", wa[0:128, 0:1024], d_w["l3f_w2"][ts(ri), 0:1024])
            p.wait("vector", tok)
            p.wait("scalar", tok)
            tq = emit_float_quantize(tA[:, 0:1024], wa[0:128, 0:1024], 1024, 4, 3)
            p.wait("gpsimd", tq)
            tok_s = p.dma("gpsimd", d_wq2[ts(ri), 0:1024], tA[0:128, 0:1024])
            p.wait("vector", tok_s)
            p.wait("scalar", tok_s)
            p.wait("sync", tq)
        tokb = p.dma("sync", qg[0:1, 0:1024], din["L3_ffn_b2"].unsqueeze(0))
        p.wait("vector", tokb)
        p.wait("scalar", tokb)
        tq = emit_float_quantize(qg[0:1, 0:1024], qg[0:1, 0:1024], 1024, 5, 10, rows=1)
        p.wait("tensor", tq)
        tokb1 = p.dma("sync", b1col[:, 0:16], din["L3_ffn_b1"].rearrange("(c p) -> p c", p=128)[:, 0:16],
                      allow_slow_non_contiguous=True)
        tokb1b = p.dma("sync", convb[:, 0:16], din["L3_ffn_b1"].rearrange("(c p) -> p c", p=128)[:, 16:32],
                       allow_slow_non_contiguous=True)
        p.wait("scalar", [tokb1, tokb1b])
        for hc in range(32):
            bufw = wa if hc % 2 == 0 else wb
            p.wait("sync", ("tensor", p.cnt["tensor"]))
            tokw = p.dma("sync", bufw.ap()[:, 0:1024].rearrange("p (c o) -> p c o", c=8),
                         d_w["l3f_w1"].rearrange("(c r) o -> r c o", r=128)[:, :, ts(hc)])
            p.wait("tensor", tokw)
            tev = None
            for qs in range(4):
                bi = qs % 4
                p.wait("tensor", bank_guard.get(bi))
                for c in range(NCH):
                    p.on("tensor", lambda te, c=c, qs=qs, bi=bi, bufw=bufw: te.matmul(
                        ps[bi][:, 0:512], bufw.ap()[:, c * 128:c * 128 + 128],
                        xnT[:, c * S + qs * 512:c * S + qs * 512 + 512],
                        start=(c == 0), stop=(c == NCH - 1)))
                tmm = p.ck("tensor")
                p.wait("scalar", tmm)
                bcol = b1col if hc < 16 else convb
                p.on("scalar", lambda sx, bi=bi, hc=hc, qs=qs, bcol=bcol: sx.activation(
                    tC[:, qs * 512:qs * 512 + 512], ps[bi][:, 0:512], AF.Relu,
                    bias=bcol[:, (hc % 16):(hc % 16) + 1], scale=1.0))
                tev = p.ck("scalar")
                bank_guard[bi] = tev
            p.wait("vector", tev)
            tq = emit_float_quantize(tC[:, 0:2048], tC[:, 0:2048], 2048, 4, 3)
            p.wait("gpsimd", tq)
            tok_st = p.dma("gpsimd", d_t1[ts(hc), 0:2048], tC[0:128, 0:2048])
            p.wait("scalar", tok_st)
            p.wait("vector", tok_st)
        p.barrier()
        for tg in range(4):
            for hc in range(32):
                bufw = wa if hc % 2 == 0 else wb
                tokw = p.dma("sync", bufw[0:128, 0:1024], d_wq2[ts(hc), 0:1024])
                tokh = p.dma("sync", tC[:, 0:512], d_t1[ts(hc), tg * 512:tg * 512 + 512])
                p.wait("tensor", [tokw, tokh])
                for tt4 in range(4):
                    for os_ in range(2):
                        bi = tt4 * 2 + os_
                        if hc == 0:
                            p.wait("tensor", bank_guard.get(bi))
                        p.on("tensor", lambda te, tt4=tt4, os_=os_, bi=bi, hc=hc, bufw=bufw:
                             te.matmul(ps[bi][:, 0:512], tC[:, tt4 * 128:tt4 * 128 + 128],
                                       bufw.ap()[:, os_ * 512:os_ * 512 + 512],
                                       start=(hc == 0), stop=False))
                tmm = p.ck("tensor")
                p.wait("sync", tmm)
            for tt4 in range(4):
                tt = tg * 4 + tt4
                for os_ in range(2):
                    bi = tt4 * 2 + os_
                    p.on("tensor", lambda te, bi=bi, os_=os_: te.matmul(
                        ps[bi][:, 0:512], ones_row[0:1, 0:128],
                        qg[0:1, os_ * 512:os_ * 512 + 512], start=False, stop=True))
                    tmm = p.ck("tensor")
                    p.wait("vector", tmm)
                    p.on("vector", lambda v, bi=bi, os_=os_: v.tensor_copy(
                        tC[:, os_ * 512:os_ * 512 + 512], ps[bi][:, 0:512]))
                    tev = p.ck("vector")
                    bank_guard[bi] = tev
                p.wait("scalar", tev)
                tq = emit_float_quantize(tC[:, 0:1024], tC[:, 0:1024], 1024, 5, 10)
                p.wait("scalar", tq)
                p.on("scalar", lambda sx: sx.activation(tC[:, 1024:2048], tC[:, 0:1024], AF.Silu))
                tsl = p.ck("scalar")
                p.wait("sync", tq)
                tok_xx = p.dma("sync", tD[:, 0:1024], d_x[ts(tt), :])
                tok_ss = p.dma("sync", tD[:, 1024:2048], d_s[ts(tt), :])
                p.wait("vector", [tsl, tok_xx, tok_ss])
                p.on("vector", lambda v: v.tensor_mul(tC[:, 1024:2048], tC[:, 1024:2048], tD[:, 1024:2048]))
                p.on("vector", lambda v: v.tensor_add(tD[:, 0:1024], tD[:, 0:1024], tC[:, 1024:2048]))
                tcb = p.ck("vector")
                p.wait("gpsimd", tcb)
                dst = dout if final_store_combine else d_x
                tok_st = p.dma("gpsimd", dst[ts(tt), :], tD[:, 0:1024])
                p.wait("sync", tok_st)
                p.wait("scalar", tok_st)
        p.barrier()

    def layer3():
        emit_ln_and_transpose(store_xn_dram=False)
        l3_attn()
        l3_ffn_and_combine()

    tok_x = p.dma("sync", d_x[:, :], din["L0_attn_x"][:, :])
    p.wait("sync", tok_x)

    def emit_round(eng, dst, src, n, tmpf, tmpi):
        p.on(eng, lambda v: v.tensor_copy(tmpi[:, 0:n], src))
        p.on(eng, lambda v: v.tensor_copy(tmpf[:, 0:n], tmpi[:, 0:n]))
        p.on(eng, lambda v: v.tensor_sub(dst, src, tmpf[:, 0:n]))
        p.on(eng, lambda v: v.tensor_scalar(tmpi[:, 0:n].bitcast(F32), dst, 0.5, None, OP.is_gt))
        p.on(eng, lambda v: v.tensor_scalar(dst, dst, -0.5, None, OP.is_lt))
        p.on(eng, lambda v: v.tensor_sub(tmpf[:, 0:n], tmpf[:, 0:n], dst))
        p.on(eng, lambda v: v.tensor_add(dst, tmpf[:, 0:n], tmpi[:, 0:n].bitcast(F32)))

    def emit_weight_quant(dw_in, rows, cols, d_out):
        nslab = rows // 128
        p.on("vector", lambda v: v.memset(sc1[:, 0:1], 0.0))
        tv = None
        for ri in range(nslab):
            buf = wa if ri % 2 == 0 else wb
            tok = p.dma("sync", buf[0:128, 0:cols], dw_in[ts(ri), 0:cols])
            p.wait("vector", tok)
            p.on("vector", lambda v, buf=buf, cols=cols: v.reduce_sum(
                sc2[:, 0:1], buf[0:128, 0:cols], axis=AX.X, apply_absolute_value=True))
            p.ckw("vector")
            p.on("vector", lambda v: v.tensor_add(sc1[:, 0:1], sc1[:, 0:1], sc2[:, 0:1]))
            tv = p.ckw("vector")
            p.wait("sync", tv)
        p.wait("tensor", tv)
        p.wait("tensor", bank_guard.get(5))
        p.on("tensor", lambda te: te.transpose(ps[5][0:1, 0:128], sc1[:, 0:1], ident[:, :]))
        tmm = p.ck("tensor")
        p.wait("vector", tmm)
        p.on("vector", lambda v: v.reduce_sum(rrow[0:1, 1536:1537], ps[5][0:1, 0:128], axis=AX.X))
        p.ckw("vector")
        p.on("vector", lambda v, rows=rows, cols=cols: v.tensor_scalar(
            rrow[0:1, 1537:1538], rrow[0:1, 1536:1537], 1.0 / (rows * cols), None, OP.mult))
        p.ckw("vector")
        p.on("vector", lambda v: v.tensor_scalar(
            rrow[0:1, 1537:1538], rrow[0:1, 1537:1538], 1e-5, None, OP.max))
        p.ckw("vector")
        p.on("vector", lambda v: v.reciprocal(rrow[0:1, 1538:1539], rrow[0:1, 1537:1538]))
        trc = p.ckw("vector")
        bank_guard[5] = trc
        p.wait("tensor", trc)
        p.wait("tensor", bank_guard.get(4))
        p.on("tensor", lambda te: te.matmul(ps[4][0:128, 0:2], ones_row[0:1, 0:128],
                                            rrow[0:1, 1537:1539], start=True, stop=True))
        tmm = p.ck("tensor")
        p.wait("vector", tmm)
        p.on("vector", lambda v: v.tensor_copy(sc3[:, 0:2], ps[4][0:128, 0:2]))
        tq = p.ck("vector")
        bank_guard[4] = tq
        p.wait("tensor", tq)
        for ri in range(nslab):
            buf = wa if ri % 2 == 0 else wb
            tok = p.dma("sync", buf[0:128, 0:cols], dw_in[ts(ri), 0:cols])
            p.wait("vector", tok)
            for cg in range(0, cols, 2048):
                w_ = min(2048, cols - cg)
                p.on("vector", lambda v, buf=buf, cg=cg, w_=w_: v.tensor_scalar(
                    tB[:, 0:w_], buf[0:128, cg:cg + w_], sc3[:, 1:2], None, OP.mult))
                emit_round("vector", tA[:, 0:w_], tB[:, 0:w_], w_, tE, ti32)
                p.on("vector", lambda v, w_=w_: v.tensor_scalar(
                    tA[:, 0:w_], tA[:, 0:w_], -1.0, 1.0, OP.max, OP.min))
                p.on("vector", lambda v, w_=w_: v.tensor_scalar(
                    tA[:, 0:w_], tA[:, 0:w_], sc3[:, 0:1], None, OP.mult))
                tq = p.ck("vector")
                p.wait("gpsimd", tq)
                tok_s = p.dma("gpsimd", d_out[ts(ri), cg:cg + w_], tA[0:128, 0:w_])
                p.wait("vector", tok_s)
            p.wait("sync", tq)
        p.barrier()

    def l1_ffn_and_combine():
        emit_weight_quant(d_w["l1f_w1"], D, 4096, d_wq1)
        emit_weight_quant(d_w["l1f_w2"], 4096, D, d_wq2)
        tok1 = p.dma("sync", convw.ap().rearrange("p (c k) -> p c k", c=32)[:, :, 0:3],
                     din["L1_ffn_conv_w"].squeeze(1).rearrange("(c p) k -> p c k", p=128))
        tok2 = p.dma("sync", convb[:, 0:32],
                     din["L1_ffn_conv_b"].rearrange("(c p) -> p c", p=128),
                     allow_slow_non_contiguous=True)
        p.wait("vector", tok1)
        p.wait("scalar", tok2)
        for tt in range(NT):
            tok = p.dma("sync", tB[:, 0:1024], d_xn[ts(tt), :])
            p.wait("vector", tok)
            p.on("vector", lambda v: v.reduce_max(sc1[:, 0:1], tB[:, 0:1024], axis=AX.X,
                                                  apply_absolute_value=True))
            p.ckw("vector")
            p.on("vector", lambda v: v.tensor_scalar(sc1[:, 0:1], sc1[:, 0:1], 1e-5, None, OP.max))
            p.ckw("vector")
            p.on("vector", lambda v: v.reciprocal(sc2[:, 0:1], sc1[:, 0:1]))
            p.ckw("vector")
            p.on("vector", lambda v: v.tensor_scalar(sc2[:, 0:1], sc2[:, 0:1], 127.0, None, OP.mult))
            p.ckw("vector")
            p.on("vector", lambda v: v.reciprocal(sc3[:, 0:1], sc2[:, 0:1]))
            p.ckw("vector")
            p.on("vector", lambda v: v.tensor_scalar(tC[:, 0:1024], tB[:, 0:1024], sc2[:, 0:1], None, OP.mult))
            emit_round("vector", tD[:, 0:1024], tC[:, 0:1024], 1024, tE, ti32)
            p.on("vector", lambda v: v.tensor_scalar(tD[:, 0:1024], tD[:, 0:1024], -128.0, 127.0, OP.max, OP.min))
            p.on("vector", lambda v: v.tensor_scalar(tD[:, 0:1024], tD[:, 0:1024], sc3[:, 0:1], None, OP.mult))
            tq = p.ck("vector")
            p.wait("tensor", tq)
            for c in range(NCH):
                bi = 6 + (c % 2)
                p.wait("tensor", bank_guard.get(bi))
                p.on("tensor", lambda te, c=c, bi=bi: te.transpose(ps[bi][0:128, 0:128], tD[:, ts(c)], ident[:, :]))
                tmm = p.ck("tensor")
                p.wait("vector", tmm)
                p.on("vector", lambda v, c=c, tt=tt, bi=bi: v.tensor_copy(
                    xnT[:, c * S + tt * 128:c * S + tt * 128 + 128], ps[bi][0:128, 0:128]))
                bank_guard[bi] = p.ck("vector")
            p.wait("sync", bank_guard[7])
        p.barrier()
        p.on("vector", lambda v: v.memset(tA[:, 0:2048], 0.0))
        tamax = p.ck("vector")
        for hc in range(32):
            tok_w = p.dma("sync", (wa if hc % 2 == 0 else wb).ap()[:, 0:1024].rearrange(
                "p (c o) -> p c o", c=8), d_wq1.rearrange("(c r) o -> r c o", r=128)[:, :, ts(hc)])
            p.wait("tensor", tok_w)
            buf = wa if hc % 2 == 0 else wb
            for qs in range(4):
                bi = qs % 4
                p.wait("tensor", bank_guard.get(bi))
                for c in range(NCH):
                    p.on("tensor", lambda te, c=c, qs=qs, bi=bi, buf=buf:
                         te.matmul(ps[bi][:, 0:512], buf.ap()[:, c * 128:c * 128 + 128],
                                   xnT[:, c * S + qs * 512:c * S + qs * 512 + 512],
                                   start=(c == 0), stop=(c == NCH - 1)))
                tmm = p.ck("tensor")
                p.wait("vector", tmm)
                p.on("vector", lambda v, qs=qs, bi=bi: v.tensor_copy(tC[:, 1 + qs * 512:1 + qs * 512 + 512], ps[bi][:, 0:512]))
                bank_guard[bi] = p.ck("vector")
            p.on("vector", lambda v: v.memset(tC[:, 0:1], 0.0))
            p.on("vector", lambda v: v.memset(tC[:, 2049:2050], 0.0))
            p.on("vector", lambda v, hc=hc: v.tensor_scalar(tD[:, 0:2048], tC[:, 0:2048], convw[:, 4 * hc:4 * hc + 1], None, OP.mult))
            p.on("vector", lambda v, hc=hc: v.scalar_tensor_tensor(tD[:, 0:2048], tC[:, 1:2049], convw[:, 4 * hc + 1:4 * hc + 2], tD[:, 0:2048], OP.mult, OP.add))
            p.on("vector", lambda v, hc=hc: v.scalar_tensor_tensor(tD[:, 0:2048], tC[:, 2:2050], convw[:, 4 * hc + 2:4 * hc + 3], tD[:, 0:2048], OP.mult, OP.add))
            tcv = p.ck("vector")
            p.wait("scalar", tcv)
            p.on("scalar", lambda sx, hc=hc: sx.activation(tE[:, 0:2048], tD[:, 0:2048], AF.Silu,
                                                           bias=convb[:, hc:hc + 1], scale=1.0))
            tsl = p.ck("scalar")
            p.wait("vector", tsl)
            p.on("vector", lambda v: v.tensor_scalar(tB[:, 0:2048], tE[:, 0:2048], -1.0, None, OP.mult))
            p.on("vector", lambda v: v.tensor_max(tB[:, 0:2048], tB[:, 0:2048], tE[:, 0:2048]))
            p.on("vector", lambda v: v.tensor_max(tA[:, 0:2048], tA[:, 0:2048], tB[:, 0:2048]))
            tmx = p.ck("vector")
            p.wait("gpsimd", tsl)
            tok_st = p.dma("gpsimd", d_t1[ts(hc), 0:2048], tE[:, 0:2048])
            p.wait("scalar", tok_st)
            p.wait("sync", tmx)
        p.barrier()
        for tt in range(NT):
            bi = 6 + (tt % 2)
            p.wait("tensor", bank_guard.get(bi))
            p.on("tensor", lambda te, tt=tt, bi=bi: te.transpose(ps[bi][0:128, 0:128], tA[:, ts(tt)], ident[:, :]))
            tmm = p.ck("tensor")
            p.wait("vector", tmm)
            p.on("vector", lambda v, tt=tt, bi=bi: v.reduce_max(sc1[:, tt:tt + 1], ps[bi][0:128, 0:128], axis=AX.X))
            bank_guard[bi] = p.ckw("vector")
        for tt in range(NT):
            p.on("vector", lambda v, tt=tt: v.tensor_scalar(sc1[:, tt:tt + 1], sc1[:, tt:tt + 1], 1e-5, None, OP.max))
        p.ckw("vector")
        p.on("vector", lambda v: v.reciprocal(sc2[:, 0:NT], sc1[:, 0:NT]))
        p.ckw("vector")
        p.on("vector", lambda v: v.tensor_scalar(sc2[:, 0:NT], sc2[:, 0:NT], 127.0, None, OP.mult))
        p.ckw("vector")
        p.on("vector", lambda v: v.reciprocal(sc3[:, 0:NT], sc2[:, 0:NT]))
        tsc_ = p.ckw("vector")
        p.wait("tensor", tsc_)
        for (srccol, dstbase) in ((sc2, 0), (sc3, 0)):
            for tt in range(NT):
                bi = 6 + (tt % 2)
                p.wait("tensor", bank_guard.get(bi))
                p.on("tensor", lambda te, srccol=srccol, tt=tt, bi=bi: te.transpose(
                    ps[bi][0:1, 0:128], srccol[:, tt:tt + 1], ident[:, :]))
                tmm = p.ck("tensor")
                p.wait("vector", tmm)
                if srccol is sc2:
                    p.on("vector", lambda v, tt=tt, bi=bi: v.tensor_copy(
                        rrow[0:1, tt * 128:tt * 128 + 128], ps[bi][0:1, 0:128]))
                else:
                    p.on("vector", lambda v, tt=tt, bi=bi: v.tensor_copy(
                        brow[0:1, tt * 128:tt * 128 + 128], ps[bi][0:1, 0:128]))
                bank_guard[bi] = p.ck("vector")
        tsc2 = p.ck("vector") if p._dirty["vector"] else None
        for (srcrow, dstt) in ((rrow, tA), (brow, tB)):
            for qs in range(4):
                bi = 4 + (qs % 2)
                p.wait("tensor", bank_guard.get(bi))
                p.on("tensor", lambda te, srcrow=srcrow, qs=qs, bi=bi: te.matmul(
                    ps[bi][0:128, 0:512], ones_row[0:1, 0:128],
                    srcrow[0:1, qs * 512:qs * 512 + 512], start=True, stop=True))
                tmm = p.ck("tensor")
                p.wait("vector", tmm)
                p.on("vector", lambda v, qs=qs, bi=bi, dstt=dstt: v.tensor_copy(
                    dstt[:, qs * 512:qs * 512 + 512], ps[bi][0:128, 0:512]))
                bank_guard[bi] = p.ck("vector")
        p.barrier()
        for tg in range(4):
            for hc in range(32):
                buf = wa if hc % 2 == 0 else wb
                tokw = p.dma("sync", buf[0:128, 0:1024], d_wq2[ts(hc), 0:1024])
                tokh = p.dma("sync", tC[:, 0:512], d_t1[ts(hc), tg * 512:tg * 512 + 512])
                p.wait("vector", tokh)
                p.on("vector", lambda v, tg=tg: v.tensor_mul(tD[:, 0:512], tC[:, 0:512],
                                                             tA[:, tg * 512:tg * 512 + 512]))
                emit_round("vector", tD[:, 0:512], tD[:, 0:512], 512, tE, ti32)
                p.on("vector", lambda v: v.tensor_scalar(tD[:, 0:512], tD[:, 0:512], -128.0, 127.0, OP.max, OP.min))
                p.on("vector", lambda v, tg=tg: v.tensor_mul(tD[:, 0:512], tD[:, 0:512],
                                                             tB[:, tg * 512:tg * 512 + 512]))
                tq = p.ck("vector")
                p.wait("tensor", tq)
                p.wait("tensor", tokw)
                for tt4 in range(4):
                    for os_ in range(2):
                        bi = tt4 * 2 + os_
                        if hc == 0:
                            p.wait("tensor", bank_guard.get(bi))
                        p.on("tensor", lambda te, tt4=tt4, os_=os_, bi=bi, hc=hc, buf=buf:
                             te.matmul(ps[bi][:, 0:512], tD[:, tt4 * 128:tt4 * 128 + 128],
                                       buf.ap()[:, os_ * 512:os_ * 512 + 512],
                                       start=(hc == 0), stop=(hc == 31)))
                tmm = p.ck("tensor")
                p.wait("sync", tmm)
                p.wait("vector", tmm)
            p.wait("scalar", tmm)
            for tt4 in range(4):
                tt = tg * 4 + tt4
                for os_ in range(2):
                    bi = tt4 * 2 + os_
                    p.on("scalar", lambda sx, bi=bi, os_=os_: sx.activation(
                        tF[:, os_ * 512:os_ * 512 + 512], ps[bi][:, 0:512], AF.Silu))
                tsl = p.ck("scalar")
                for bi in (tt4 * 2, tt4 * 2 + 1):
                    bank_guard[bi] = tsl
                tok_xx = p.dma("sync", tC[:, 0:1024], d_x[ts(tt), :])
                tok_ss = p.dma("sync", tC[:, 1024:2048], d_s[ts(tt), :])
                p.wait("vector", tsl)
                p.wait("vector", tok_xx)
                p.wait("vector", tok_ss)
                p.on("vector", lambda v: v.tensor_mul(tF[:, 1024:2048], tC[:, 1024:2048], tF[:, 0:1024]))
                p.on("vector", lambda v: v.tensor_add(tC[:, 0:1024], tC[:, 0:1024], tF[:, 1024:2048]))
                tcb = p.ck("vector")
                p.wait("gpsimd", tcb)
                tok_st = p.dma("gpsimd", d_x[ts(tt), :], tC[:, 0:1024])
                p.wait("sync", tok_st)
                p.wait("scalar", tcb)
        p.barrier()

    def l1_qkv_cols(h):
        return (192 * h, 192 * h + 64, 192 * h + 128)

    def layer1():
        emit_ln_and_transpose(store_xn_dram=True)
        emit_mha(din["L1_attn_W_qkv"], l1_qkv_cols, din["L1_attn_b_qkv"],
                 din["L1_attn_W_out"], din["L1_attn_b_out"],
                 causal=True, dst_s=d_s, scale=0.125)
        l1_ffn_and_combine()

    if 0 in layers:
        layer0()
    if 1 in layers:
        layer1()
    if 2 in layers:
        layer2()
    if 3 in layers:
        layer3()

    p.barrier()
    if dbg == "s":
        tok2 = None
        for tt in range(NT):
            tok = p.dma("sync", tA[:, 0:1024], d_s[ts(tt), :])
            p.wait("gpsimd", tok)
            tok2 = p.dma("gpsimd", dout[ts(tt), :], tA[:, 0:1024])
            p.wait("sync", tok2)
        last = tok2
    elif dbg == "xn":
        tok2 = None
        for tt in range(NT):
            tok = p.dma("sync", tA[:, 0:1024], d_xn[ts(tt), :])
            p.wait("gpsimd", tok)
            tok2 = p.dma("gpsimd", dout[ts(tt), :], tA[:, 0:1024])
            p.wait("sync", tok2)
        last = tok2
    elif final_store_combine:
        last = None
    else:
        last = p.dma("gpsimd", dout[:, :], d_x[:, :])
    if last is not None:
        p.wait("gpsimd", last)

    with nc.Block() as block:
        @block.tensor
        def _(e):
            p.emit("tensor", e)

        @block.vector
        def _(e):
            p.emit("vector", e)

        @block.scalar
        def _(e):
            p.emit("scalar", e)

        @block.gpsimd
        def _(e):
            p.emit("gpsimd", e)

        @block.sync
        def _(e):
            p.emit("sync", e)

    es.close()
    return nc


def kernel(**inputs):
    x = np.ascontiguousarray(inputs["L0_attn_x"])
    nc = bass.Bass("TRN2", target_bir_lowering=False, detect_race_conditions=False)
    build_core_program(nc)
    in_maps = []
    for b in range(8):
        m = {k: np.ascontiguousarray(v) for k, v in inputs.items() if k != "L0_attn_x"}
        m["L0_attn_x"] = x[b]
        in_maps.append(m)
    res = run_bass_kernel_spmd(nc, in_maps, core_ids=list(range(8)))
    return np.stack([r["out"] for r in res.results], axis=0)
